# version 03
# baseline (speedup 1.0000x reference)
import contextlib
import numpy as np
import ml_dtypes

import concourse.bass as bass
import concourse.mybir as mybir
from concourse.alu_op_type import AluOpType
from concourse.bass_utils import run_bass_kernel_spmd
from concourse.libnrt import physical_to_logical_core_id as _p2l

F32 = mybir.dt.float32
BF16 = mybir.dt.bfloat16
I32 = mybir.dt.int32
AF = mybir.ActivationFunctionType
OP = AluOpType

N_CORES = 8
NPAIR = 4


def make_cfg(mini=False, **over):
    if mini:
        c = dict(F=2048, D=256, DI=512, NST=16, INTER=512, BLK=512, ST=256)
    else:
        c = dict(F=16384, D=2048, DI=4096, NST=16, INTER=5632, BLK=1024,
                 ST=512)
    c.update(over)
    c["DH"] = c["DI"] // 2
    c["NCC"] = c["DH"] // 128
    c["NKC"] = c["D"] // 128
    c["NIC"] = c["INTER"] // 128
    c["NHC"] = c["D"] // 128
    c["NBLK"] = c["F"] // c["BLK"]
    c["TL"] = c["F"] // NPAIR
    c["TAIL_BLK"] = c["TL"] // c["BLK"]
    c["TLH"] = c["TL"] // 2
    c["NSUB"] = c["BLK"] // c["ST"]
    c["EPS"] = 1e-5
    return c


class Sem:
    def __init__(self, stack, nc, name):
        self.h = stack.enter_context(nc.semaphore(name))
        self.v = 0

    def inc(self, inst, k=1):
        inst.then_inc(self.h, k)
        self.v += k
        return self.v


def _g_loads_until(c, b, cc):
    TB0 = c["NBLK"] - c["TAIL_BLK"]
    tot = 0
    for bb in range(b + 1):
        per = 48 if bb >= TB0 else 32
        n = c["NCC"] if bb < b else (cc + 1)
        tot += per * n
    return tot


def _g_evacs_until(c, b):
    TB0 = c["NBLK"] - c["TAIL_BLK"]
    tot = 0
    for bb in range(b + 1):
        tot += c["NST"] * c["NSUB"] * (2 if bb >= TB0 else 1)
    return tot


def _g_mms_until(c, b):
    return _g_evacs_until(c, b)


def build_program(c, debug=False):
    nc = bass.Bass("TRN2", target_bir_lowering=False)
    F, D, DI, DH = c["F"], c["D"], c["DI"], c["DH"]
    NCC, NKC, NIC, NHC = c["NCC"], c["NKC"], c["NIC"], c["NHC"]
    NST = c["NST"]
    BLK, ST, NSUB = c["BLK"], c["ST"], c["NSUB"]
    NBLK, TL, TAIL_BLK, TLH = c["NBLK"], c["TL"], c["TAIL_BLK"], c["TLH"]
    TB0 = NBLK - TAIL_BLK
    NROW = BLK // 128
    NT0 = NBLK * NROW

    din = lambda n, s, d=F32: nc.dram_tensor(n, s, d, kind="ExternalInput").ap()
    x_f = din("x_f", [F, D])
    nonce_i = din("nonce", [1, 1], I32)
    winu = din("winu", [NCC, 128, D])
    wing = din("wing", [NCC, 128, D])
    xpt = din("xpt", [128, NCC * 160])
    dtwt = din("dtwt", [128, DH])
    convd = din("convd", [128, NCC * 512])
    vecs = din("vecs", [128, 3 * NCC])
    alog = din("alog", [128, NCC * NST])
    maskc = din("maskc", [128, NBLK])
    selm = din("selm", [32, 2 * NST * 128], BF16)
    idnb_i = din("idnb", [128, 128], BF16)
    idnf_i = din("idnf", [128, 128])
    woutt = din("woutt", [NHC, 128, 2 * NCC * 128])
    gatet = din("gatet", [NIC, 128, NKC * 128])
    upt = din("upt", [NIC, 128, NKC * 128])
    downt = din("downt", [NHC, 128, NIC * 128])
    fscale = din("fscale", [128, 2 * NIC + NHC])
    out_r = nc.dram_tensor("out_r", [TLH, D], F32, kind="ExternalOutput").ap()

    xnt_d = nc.dram_tensor("xnt_d", [NBLK, NKC, 128, BLK], BF16)
    upre_d = nc.dram_tensor("upre_d", [NCC, 128, F], BF16)
    u_d = nc.dram_tensor("u_d", [NCC, 128, F], BF16)
    dt_d = nc.dram_tensor("dt_d", [NCC, 128, F], BF16)
    ts_d = nc.dram_tensor("ts_d", [128, F], BF16)
    bc_d = nc.dram_tensor("bc_d", [32, F], BF16)
    sg_d = nc.dram_tensor("sg_d", [NCC, 128, TL], BF16)
    s_d = nc.dram_tensor("s_d", [NHC, 128, TLH], F32)
    sgf_d = nc.dram_tensor("sgf_d", [NHC, 128, TLH], F32)

    sh_ts = nc.dram_tensor("sh_ts", [2, 128, F], F32, addr_space="Shared")
    sh_bc = nc.dram_tensor("sh_bc", [2, 32, F], F32, addr_space="Shared")
    sh_yg = nc.dram_tensor("sh_yg", [2, NCC, 128, TL], BF16, addr_space="Shared")
    sh_fl = nc.dram_tensor("sh_fl", [2, 8], I32, addr_space="Shared")

    st_ = contextlib.ExitStack()
    sbuf = lambda n, s, d: st_.enter_context(nc.sbuf_tensor(n, s, d))

    nonce_sb = sbuf("nonce_sb", [1, 1], I32)
    params_sb = sbuf("params_sb", [1, 2], I32)
    a_sb = sbuf("a_sb", [128, NCC * NST], F32)
    convb_sb = sbuf("convb_sb", [128, NCC], F32)
    dtb_sb = sbuf("dtb_sb", [128, NCC], F32)
    dpar_sb = sbuf("dpar_sb", [128, NCC], F32)
    mask_sb = sbuf("mask_sb", [128, NBLK], F32)
    fsc_sb = sbuf("fsc_sb", [128, 2 * NIC + NHC], F32)
    sel_sb = sbuf("sel_sb", [32, 2 * NST * 128], BF16)
    state_sb = sbuf("state_sb", [128, NCC * NST], F32)
    idnb = sbuf("idnb_s", [128, 128], BF16)
    idnf = sbuf("idnf_s", [128, 128], F32)
    lnsc = sbuf("lnsc", [128, 64], F32)
    lndbg = sbuf("lndbg", [128, 512], F32)

    SM = {k: Sem(st_, nc, k) for k in
          ["ld", "lw", "wc", "mmA", "mmB", "evA", "evB", "evC",
           "dvA", "dvB", "stq", "stu", "bar", "hs", "lg", "gq", "mmY"]}
    hs = SM["hs"]
    lg = SM["lg"]
    ld, lw, wc = SM["ld"], SM["lw"], SM["wc"]
    mmA, mmB = SM["mmA"], SM["mmB"]
    evA, evB, evC = SM["evA"], SM["evB"], SM["evC"]
    dvA, dvB = SM["dvA"], SM["dvB"]
    stq, stu, bar = SM["stq"], SM["stu"], SM["bar"]
    gq, mmY = SM["gq"], SM["mmY"]

    _ns_stack = []
    block = st_.enter_context(nc.Block())
    st_.enter_context(nc.allow_low_precision(
        reason="bf16 matmuls / bf16 intermediates sanctioned by tolerance"))

    r_slot = st_.enter_context(nc.sync.register("r_slot"))
    rg_slot = st_.enter_context(nc.gpsimd.register("rg_slot"))
    r_non = st_.enter_context(nc.sync.register("r_non"))
    r_flag = st_.enter_context(nc.sync.register("r_flag"))
    r_cont = st_.enter_context(nc.sync.register("r_cont"))
    r_poll = st_.enter_context(nc.sync.register("r_poll"))

    ALL_ENG = (block.sync, block.tensor, block.scalar, block.vector,
               block.gpsimd)

    def barrier():
        for emit in ALL_ENG:
            @emit
            def _(e):
                e.nop().then_inc(bar.h, 1)
        bar.v += len(ALL_ENG)
        tgt = bar.v
        for emit in ALL_ENG:
            @emit
            def _(e):
                e.wait_ge(bar.h, tgt)

    def flag_write(sync, idx, off):
        sync.reg_alu(r_flag, sync.snap(r_non), off, OP.add)
        s = slot_sv(sync, r_slot)
        with sync.If(s == 0):
            sync.store(sh_fl[0:1, idx:idx + 1], r_flag)
        with sync.Else():
            sync.store(sh_fl[1:2, idx:idx + 1], r_flag)

    def flag_poll(sync, idx, off):
        sync.reg_alu(r_flag, sync.snap(r_non), off, OP.add)
        s = slot_sv(sync, r_slot)
        with sync.If(s == 0):
            sync.reg_mov(r_cont, 1)
            with sync.While(r_cont):
                sync.load(r_poll, sh_fl[1:2, idx:idx + 1])
                sync.reg_alu(r_cont, sync.snap(r_poll), sync.snap(r_flag),
                             OP.not_equal)
        with sync.Else():
            sync.reg_mov(r_cont, 1)
            with sync.While(r_cont):
                sync.load(r_poll, sh_fl[0:1, idx:idx + 1])
                sync.reg_alu(r_cont, sync.snap(r_poll), sync.snap(r_flag),
                             OP.not_equal)

    SLOT_SV = {}

    def slot_sv(eng, reg):
        key = id(eng)
        if key not in SLOT_SV:
            SLOT_SV[key] = eng.snap(reg, min_val=0, max_val=1)
        return SLOT_SV[key]

    def slot_dma_out(eng, dst0, dst1, src, sem, reg=None):
        s = slot_sv(eng, reg if reg is not None else r_slot)
        with eng.If(s == 0):
            eng.dma_start(out=dst0, in_=src).then_inc(sem.h, 16)
        with eng.Else():
            eng.dma_start(out=dst1, in_=src).then_inc(sem.h, 16)
        sem.v += 16

    def slot_dma_in(eng, dst, src0, src1, sem, reg=None):
        s = slot_sv(eng, reg if reg is not None else r_slot)
        with eng.If(s == 0):
            eng.dma_start(out=dst, in_=src0).then_inc(sem.h, 16)
        with eng.Else():
            eng.dma_start(out=dst, in_=src1).then_inc(sem.h, 16)
        sem.v += 16

    def gstore(g, dst, src, sem):
        g.dma_start(out=dst, in_=src).then_inc(sem.h, 16)
        sem.v += 16

    @block.sync
    def _(sync):
        for arm in sync.switch_core_id(n=128):
            sync.reg_mov(r_slot,
                         _p2l(int(arm) % 8, 0, container_num_cores=8) % 2)
        for dst, src in [
                (nonce_sb[:, :], nonce_i), (a_sb[:, :], alog),
                (convb_sb[:, :], vecs[:, 0:NCC]),
                (dtb_sb[:, :], vecs[:, NCC:2 * NCC]),
                (dpar_sb[:, :], vecs[:, 2 * NCC:3 * NCC]),
                (mask_sb[:, :], maskc), (sel_sb[:, :], selm),
                (idnb[:, :], idnb_i), (idnf[:, :], idnf_i),
                (fsc_sb[:, :], fscale)]:
            sync.dma_start(out=dst, in_=src).then_inc(ld.h, 16)
        ld.v += 160
        sync.wait_ge(ld.h, ld.v)
        sync.load(r_non, nonce_sb[0:1, 0:1])
        sync.store(params_sb[0:1, 0:1], r_slot)
        sync.nop().then_inc(hs.h, 1)
    hs.v += 1

    @block.gpsimd
    def _(g):
        g.wait_ge(hs.h, hs.v)
        g.load(rg_slot, params_sb[0:1, 0:1])

    @block.scalar
    def _(sc):
        sc.wait_ge(hs.h, hs.v)
        i = sc.activation(a_sb[:, :], a_sb[:, :], AF.Exp, scale=1.0)
        evA.inc(i)

    @block.vector
    def _(v):
        v.wait_ge(evA.h, evA.v)
        v.tensor_scalar(a_sb[:, :], a_sb[:, :], -1.0, None, OP.mult)
        i = v.memset(state_sb[:, :], 0.0)
        dvA.inc(i)
    barrier()

    _ns_stack.append(("phA", nc.enter_named_scope("phA", False)[0]))
    with contextlib.ExitStack() as ph:
        sba = lambda n, s, d: ph.enter_context(nc.sbuf_tensor(n, s, d))
        XR = sba("XR", [128, 2 * D], F32)
        XNB = sba("XNB", [128, 2 * D], BF16)
        STG = [sba("STG0", [128, NKC * BLK], BF16),
               sba("STG1", [128, NKC * BLK], BF16)]
        pt_bf = [ph.enter_context(nc.psum_tensor("pt_bf0", [128, 512], BF16)),
                 ph.enter_context(nc.psum_tensor("pt_bf1", [128, 512], BF16))]
        xr = [XR[:, 0:D], XR[:, D:2 * D]]
        xnb = [XNB[:, 0:D], XNB[:, D:2 * D]]
        nstat = max(1, D // 512)
        kpg = min(4, NKC)
        ngrp = NKC // kpg

        ldA, dvA_A, evA_A = ld.v, dvA.v, evA.v
        mmA_A, evB_A, stqA = mmA.v, evB.v, stq.v

        @block.sync
        def _(sync):
            for it in range(NT0):
                sync.wait_ge(dvA.h, max(dvA_A, dvA_A + 2 * (it - 1)))
                if it >= 1:
                    sync.wait_ge(ld.h, ldA + it * 16)
                sync.dma_start(out=xr[it % 2],
                               in_=x_f[it * 128:(it + 1) * 128, :]
                               ).then_inc(ld.h, 16)
            ld.v += NT0 * 16

        @block.vector
        def _(v):
            for it in range(NT0):
                sl = lnsc[:, (it % 2) * 8:(it % 2) * 8 + 8]
                stt = lnsc[:, 16 + (it % 2) * 24:16 + (it % 2) * 24 + 6 * nstat]
                v.wait_ge(ld.h, ldA + min(NT0, it + 2) * 16)
                if it >= 2:
                    v.wait_ge(evA.h, evA_A + (it - 1))
                for s in range(nstat):
                    w = min(512, D)
                    i = v.bn_stats(stt[:, s * 6:(s + 1) * 6],
                                   xr[it % 2][:, s * w:(s + 1) * w])
                dvB.inc(i)
                v.wait_ge(dvB.h, dvB.v)
                i = v.bn_aggr(sl[:, 0:2], stt[:, 0:6 * nstat])
                dvB.inc(i)
                v.wait_ge(dvB.h, dvB.v)
                i = v.tensor_scalar(sl[:, 2:3], sl[:, 1:2], 1.0,
                                    float(c["EPS"]), OP.mult, OP.add)
                dvA.inc(i)
                v.wait_ge(evA.h, evA_A + it + 1)
                if it >= 2:
                    v.wait_ge(mmA.h, mmA_A + (it - 1) * ngrp)
                i = v.tensor_scalar(xnb[it % 2], xr[it % 2], sl[:, 0:1],
                                    sl[:, 3:4], OP.subtract, OP.mult)
                dvA.inc(i)

        def emit_evacs(sc, it):
            b, r = it // NROW, it % NROW
            if r == 0 and b >= 2:
                sc.wait_ge(stq.h, stqA + (b - 1) * NKC * 16)
            for g in range(ngrp):
                e = it * ngrp + g
                sc.wait_ge(mmA.h, mmA_A + e + 1)
                for j in range(kpg):
                    kc = g * kpg + j
                    i = sc.activation(
                        STG[b % 2][:, kc * BLK + r * 128:kc * BLK + (r + 1) * 128],
                        pt_bf[e % 2][:, j * 128:(j + 1) * 128], AF.Copy,
                        scale=1.0)
                evB.inc(i)

        @block.scalar
        def _(sc):
            for it in range(NT0):
                sl = lnsc[:, (it % 2) * 8:(it % 2) * 8 + 8]
                sc.wait_ge(dvA.h, dvA_A + 2 * it + 1)
                sc.activation(sl[:, 5:6], sl[:, 2:3], AF.Ln, scale=1.0)
                i = sc.activation(sl[:, 3:4], sl[:, 5:6], AF.Exp, scale=-0.5)
                evA.inc(i)
                if it >= 1:
                    emit_evacs(sc, it - 1)
            emit_evacs(sc, NT0 - 1)

        @block.tensor
        def _(t):
            for it in range(NT0):
                t.wait_ge(dvA.h, dvA_A + 2 * it + 2)
                for g in range(ngrp):
                    e = it * ngrp + g
                    if e >= 2:
                        t.wait_ge(evB.h, evB_A + e - 1)
                    for j in range(kpg):
                        kc = g * kpg + j
                        i = t.transpose(pt_bf[e % 2][:, j * 128:(j + 1) * 128],
                                        xnb[it % 2][:, kc * 128:(kc + 1) * 128],
                                        idnb[:, :])
                    mmA.inc(i)

        @block.gpsimd
        def _(sync):
            for b in range(NBLK):
                sync.wait_ge(evB.h, evB_A + (b + 1) * NROW * ngrp)
                if b >= 1:
                    sync.wait_ge(stq.h, stqA + b * NKC * 16)
                for kc in range(NKC):
                    sync.dma_start(out=xnt_d[b, kc],
                                   in_=STG[b % 2][:, kc * BLK:(kc + 1) * BLK]
                                   ).then_inc(stq.h, 16)
            stq.v += NBLK * NKC * 16
            sync.wait_ge(stq.h, stq.v)
        barrier()

    _ns_name, _ns_id = _ns_stack.pop(); nc.leave_named_scope(_ns_name, _ns_id, False)
    _ns_stack.append(("phB", nc.enter_named_scope("phB", False)[0]))
    with contextlib.ExitStack() as ph:
        sba = lambda n, s, d: ph.enter_context(nc.sbuf_tensor(n, s, d))
        WB = sba("WB", [128, NCC * D], BF16)
        WL = [sba("WL0", [128, D], F32), sba("WL1", [128, D], F32)]
        XNT = [sba("XNT0", [128, NKC * BLK], BF16),
               sba("XNT1", [128, NKC * BLK], BF16)]
        UPS = sba("UPS", [128, 4 * ST], BF16)
        p_a = [ph.enter_context(nc.psum_tensor("pb_a0", [128, 512], F32)),
               ph.enter_context(nc.psum_tensor("pb_a1", [128, 512], F32))]
        lwB, wcB = lw.v, wc.v

        @block.sync
        def _(sync):
            for pc in range(NCC):
                sync.wait_ge(wc.h, max(wcB, wcB + pc - 1))
                if pc >= 1:
                    sync.wait_ge(lw.h, lwB + pc * 16)
                sync.dma_start(out=WL[pc % 2][:, :], in_=winu[pc]
                               ).then_inc(lw.h, 16)
            lw.v += NCC * 16

        @block.scalar
        def _(sc):
            for pc in range(NCC):
                sc.wait_ge(lw.h, lwB + (pc + 1) * 16)
                i = sc.activation(WB[:, pc * D:(pc + 1) * D], WL[pc % 2][:, :],
                                  AF.Copy, scale=1.0)
                wc.inc(i)

        ldB, mmB_B, evC_B, stqB = ld.v, mmA.v, evC.v, stq.v

        @block.sync
        def _(sync):
            for b in range(NBLK):
                sync.wait_ge(mmA.h, max(mmB_B, mmB_B + (b - 1) * NCC * NSUB))
                if b >= 1:
                    sync.wait_ge(ld.h, ldB + b * NKC * 16)
                for kc in range(NKC):
                    sync.dma_start(out=XNT[b % 2][:, kc * BLK:(kc + 1) * BLK],
                                   in_=xnt_d[b, kc]).then_inc(ld.h, 16)
            ld.v += NBLK * NKC * 16

        @block.tensor
        def _(t):
            t.wait_ge(wc.h, wcB + NCC)
            for b in range(NBLK):
                t.wait_ge(ld.h, ldB + (b + 1) * NKC * 16)
                for pc in range(NCC):
                    for sub in range(NSUB):
                        nmm = (b * NCC + pc) * NSUB + sub
                        if nmm >= 2:
                            t.wait_ge(evC.h, evC_B + nmm - 1)
                        for kc in range(NKC):
                            i = t.matmul(
                                p_a[nmm % 2][:, 0:ST],
                                WB[:, pc * D + kc * 128:pc * D + (kc + 1) * 128],
                                XNT[b % 2][:, kc * BLK + sub * ST:kc * BLK + (sub + 1) * ST],
                                start=(kc == 0), stop=(kc == NKC - 1))
                        mmA.inc(i)

        @block.scalar
        def _(sc):
            for nmm in range(NBLK * NCC * NSUB):
                sc.wait_ge(mmA.h, mmB_B + nmm + 1)
                if nmm >= 4:
                    sc.wait_ge(stq.h, stqB + (nmm - 3) * 16)
                i = sc.activation(UPS[:, (nmm % 4) * ST:(nmm % 4 + 1) * ST],
                                  p_a[nmm % 2][:, 0:ST], AF.Copy, scale=1.0)
                evC.inc(i)

        @block.gpsimd
        def _(sync):
            for b in range(NBLK):
                for pc in range(NCC):
                    for sub in range(NSUB):
                        nmm = (b * NCC + pc) * NSUB + sub
                        sync.wait_ge(evC.h, evC_B + nmm + 1)
                        if nmm >= 1:
                            sync.wait_ge(stq.h, stqB + nmm * 16)
                        sync.dma_start(
                            out=upre_d[pc][:, b * BLK + sub * ST:b * BLK + (sub + 1) * ST],
                            in_=UPS[:, (nmm % 4) * ST:(nmm % 4 + 1) * ST]
                            ).then_inc(stq.h, 16)
            stq.v += NBLK * NCC * NSUB * 16
            sync.wait_ge(stq.h, stq.v)
        barrier()

    _ns_name, _ns_id = _ns_stack.pop(); nc.leave_named_scope(_ns_name, _ns_id, False)
    _ns_stack.append(("phC", nc.enter_named_scope("phC", False)[0]))
    with contextlib.ExitStack() as ph:
        sba = lambda n, s, d: ph.enter_context(nc.sbuf_tensor(n, s, d))
        HB = BLK + 3
        XPB = sba("XPB", [128, NCC * 160], BF16)
        CVB = sba("CVB", [128, NCC * 512], BF16)
        WLx = sba("WLx", [128, NCC * 512], F32)
        UPT = [sba("UPT0", [128, NCC * (HB + 1)], BF16),
               sba("UPT1", [128, NCC * (HB + 1)], BF16)]
        UT = sba("UT", [128, 4 * ST], BF16)
        TSE = sba("TSE", [128, 2 * ST], F32)
        BCE = sba("BCE", [32, 2 * ST], F32)
        p_a = [ph.enter_context(nc.psum_tensor("pc_a0", [128, 512], F32)),
               ph.enter_context(nc.psum_tensor("pc_a1", [128, 512], F32))]
        p_ts = [ph.enter_context(nc.psum_tensor("pc_t0", [128, 512], F32)),
                ph.enter_context(nc.psum_tensor("pc_t1", [128, 512], F32))]
        p_bc = [ph.enter_context(nc.psum_tensor("pc_b0", [32, 512], F32)),
                ph.enter_context(nc.psum_tensor("pc_b1", [32, 512], F32))]
        lwC, wcC = lw.v, wc.v

        @block.sync
        def _(sync):
            sync.dma_start(out=WLx[:, 0:NCC * 160], in_=xpt).then_inc(lw.h, 16)
            sync.wait_ge(wc.h, wcC + 1)
            sync.wait_ge(lw.h, lwC + 16)
            sync.dma_start(out=WLx[:, 0:NCC * 512], in_=convd).then_inc(lw.h, 16)
            lw.v += 32

        @block.scalar
        def _(sc):
            sc.wait_ge(lw.h, lwC + 16)
            i = sc.activation(XPB[:, :], WLx[:, 0:NCC * 160], AF.Copy, scale=1.0)
            wc.inc(i)
            sc.wait_ge(lw.h, lwC + 32)
            i = sc.activation(CVB[:, :], WLx[:, 0:NCC * 512], AF.Copy, scale=1.0)
            wc.inc(i)

        ldC, dvC = ld.v, dvA.v
        mmC_C, mmX_C, evU_C, evT_C = mmA.v, mmB.v, evA.v, evB.v
        stqC, stuC = stq.v, stu.v

        @block.vector
        def _(v):
            v.memset(UPT[0][:, :], 0.0)
            i = v.memset(UPT[1][:, :], 0.0)
            dvA.inc(i)

        @block.sync
        def _(sync):
            sync.wait_ge(dvA.h, dvC + 1)
            for b in range(NBLK):
                sync.wait_ge(mmA.h, max(mmC_C, mmC_C + (b - 1) * NSUB * NCC))
                if b >= 1:
                    sync.wait_ge(ld.h, ldC + b * NCC * 16)
                for cc in range(NCC):
                    base = cc * (HB + 1)
                    if b == 0:
                        sync.dma_start(out=UPT[0][:, base + 3:base + 3 + BLK],
                                       in_=upre_d[cc][:, 0:BLK]).then_inc(ld.h, 16)
                    else:
                        sync.dma_start(out=UPT[b % 2][:, base:base + HB],
                                       in_=upre_d[cc][:, b * BLK - 3:b * BLK + BLK]
                                       ).then_inc(ld.h, 16)
            ld.v += NBLK * NCC * 16

        @block.tensor
        def _(t):
            t.wait_ge(wc.h, wcC + 2)
            for b in range(NBLK):
                t.wait_ge(ld.h, ldC + (b + 1) * NCC * 16)
                for sub in range(NSUB):
                    nbs = b * NSUB + sub
                    for cc in range(NCC):
                        ncv = nbs * NCC + cc
                        if ncv >= 2:
                            t.wait_ge(evA.h, evU_C + ncv - 1)
                        base = cc * (HB + 1)
                        for j in range(4):
                            i = t.matmul(
                                p_a[ncv % 2][:, 0:ST],
                                CVB[:, cc * 512 + j * 128:cc * 512 + (j + 1) * 128],
                                UPT[b % 2][:, base + sub * ST + j:base + sub * ST + j + ST],
                                start=(j == 0), stop=(j == 3))
                        mmA.inc(i)
                        t.wait_ge(evA.h, evU_C + ncv + 1)
                        if cc == 0 and nbs >= 2:
                            t.wait_ge(evB.h, evT_C + nbs - 1)
                        i = t.matmul(p_ts[sub][:, 0:ST],
                                     XPB[:, cc * 160:cc * 160 + 128],
                                     UT[:, (ncv % 4) * ST:(ncv % 4 + 1) * ST],
                                     start=(cc == 0), stop=(cc == NCC - 1))
                        i = t.matmul(p_bc[sub][:, 0:ST],
                                     XPB[:, cc * 160 + 128:cc * 160 + 160],
                                     UT[:, (ncv % 4) * ST:(ncv % 4 + 1) * ST],
                                     start=(cc == 0), stop=(cc == NCC - 1))
                        mmB.inc(i)

        @block.scalar
        def _(sc):
            for b in range(NBLK):
                for sub in range(NSUB):
                    nbs = b * NSUB + sub
                    for cc in range(NCC):
                        ncv = nbs * NCC + cc
                        sc.wait_ge(mmA.h, mmC_C + ncv + 1)
                        if ncv >= 4:
                            sc.wait_ge(stu.h, stuC + (ncv - 3) * 16)
                        i = sc.activation(UT[:, (ncv % 4) * ST:(ncv % 4 + 1) * ST],
                                          p_a[ncv % 2][:, 0:ST], AF.Silu,
                                          bias=convb_sb[:, cc:cc + 1], scale=1.0)
                        evA.inc(i)
                    sc.wait_ge(mmB.h, mmX_C + (nbs + 1) * NCC)
                    if nbs >= 2:
                        sc.wait_ge(stq.h, stqC + (nbs - 1) * 32)
                    i = sc.activation(TSE[:, (nbs % 2) * ST:(nbs % 2 + 1) * ST],
                                      p_ts[sub][:, 0:ST], AF.Copy, scale=1.0)
                    i = sc.activation(BCE[:, (nbs % 2) * ST:(nbs % 2 + 1) * ST],
                                      p_bc[sub][:, 0:ST], AF.Copy, scale=1.0)
                    evB.inc(i)

        @block.gpsimd
        def _(sync):
            for b in range(NBLK):
                for sub in range(NSUB):
                    nbs = b * NSUB + sub
                    for cc in range(NCC):
                        ncv = nbs * NCC + cc
                        sync.wait_ge(evA.h, evU_C + ncv + 1)
                        if ncv >= 1:
                            sync.wait_ge(stu.h, stuC + ncv * 16)
                        sync.dma_start(
                            out=u_d[cc][:, b * BLK + sub * ST:b * BLK + (sub + 1) * ST],
                            in_=UT[:, (ncv % 4) * ST:(ncv % 4 + 1) * ST]
                            ).then_inc(stu.h, 16)
                    t0 = b * BLK + sub * ST
                    sync.wait_ge(evB.h, evT_C + nbs + 1)
                    if nbs >= 1:
                        sync.wait_ge(stq.h, stqC + nbs * 32)
                    slot_dma_out(sync, sh_ts[0][:, t0:t0 + ST],
                                 sh_ts[1][:, t0:t0 + ST],
                                 TSE[:, (nbs % 2) * ST:(nbs % 2 + 1) * ST], stq,
                                 reg=rg_slot)
                    slot_dma_out(sync, sh_bc[0][:, t0:t0 + ST],
                                 sh_bc[1][:, t0:t0 + ST],
                                 BCE[:, (nbs % 2) * ST:(nbs % 2 + 1) * ST], stq,
                                 reg=rg_slot)
            stu.v += NBLK * NSUB * NCC * 16
            sync.wait_ge(stu.h, stu.v)
            sync.wait_ge(stq.h, stq.v)

        @block.sync
        def _(sync):
            sync.wait_ge(stu.h, stu.v)
            sync.wait_ge(stq.h, stq.v)
            flag_write(sync, 0, 1)
        barrier()

    _ns_name, _ns_id = _ns_stack.pop(); nc.leave_named_scope(_ns_name, _ns_id, False)
    _ns_stack.append(("phD", nc.enter_named_scope("phD", False)[0]))
    with contextlib.ExitStack() as ph:
        sba = lambda n, s, d: ph.enter_context(nc.sbuf_tensor(n, s, d))
        WB = sba("WBd", [128, NCC * D], BF16)
        WL = [sba("WLd0", [128, D], F32), sba("WLd1", [128, D], F32)]
        XNT = [sba("XNTd0", [128, NKC * BLK], BF16),
               sba("XNTd1", [128, NKC * BLK], BF16)]
        SGT = sba("SGT", [128, 4 * ST], BF16)
        p_a = [ph.enter_context(nc.psum_tensor("pd_a0", [128, 512], F32)),
               ph.enter_context(nc.psum_tensor("pd_a1", [128, 512], F32))]
        lwD, wcD = lw.v, wc.v

        @block.sync
        def _(sync):
            for pc in range(NCC):
                sync.wait_ge(wc.h, max(wcD, wcD + pc - 1))
                if pc >= 1:
                    sync.wait_ge(lw.h, lwD + pc * 16)
                sync.dma_start(out=WL[pc % 2][:, :], in_=wing[pc]
                               ).then_inc(lw.h, 16)
            lw.v += NCC * 16

        @block.scalar
        def _(sc):
            for pc in range(NCC):
                sc.wait_ge(lw.h, lwD + (pc + 1) * 16)
                i = sc.activation(WB[:, pc * D:(pc + 1) * D], WL[pc % 2][:, :],
                                  AF.Copy, scale=1.0)
                wc.inc(i)

        ldD, mmD, evD, stqD = ld.v, mmA.v, evA.v, stq.v

        @block.sync
        def _(sync):
            for bi in range(TAIL_BLK):
                sync.wait_ge(mmA.h, max(mmD, mmD + (bi - 1) * NCC * NSUB))
                if bi >= 1:
                    sync.wait_ge(ld.h, ldD + bi * NKC * 16)
                for kc in range(NKC):
                    sync.dma_start(out=XNT[bi % 2][:, kc * BLK:(kc + 1) * BLK],
                                   in_=xnt_d[TB0 + bi, kc]).then_inc(ld.h, 16)
            ld.v += TAIL_BLK * NKC * 16

        @block.tensor
        def _(t):
            t.wait_ge(wc.h, wcD + NCC)
            for bi in range(TAIL_BLK):
                t.wait_ge(ld.h, ldD + (bi + 1) * NKC * 16)
                for pc in range(NCC):
                    for sub in range(NSUB):
                        nmm = (bi * NCC + pc) * NSUB + sub
                        if nmm >= 2:
                            t.wait_ge(evA.h, evD + nmm - 1)
                        for kc in range(NKC):
                            i = t.matmul(
                                p_a[nmm % 2][:, 0:ST],
                                WB[:, pc * D + kc * 128:pc * D + (kc + 1) * 128],
                                XNT[bi % 2][:, kc * BLK + sub * ST:kc * BLK + (sub + 1) * ST],
                                start=(kc == 0), stop=(kc == NKC - 1))
                        mmA.inc(i)

        @block.scalar
        def _(sc):
            for nmm in range(TAIL_BLK * NCC * NSUB):
                sc.wait_ge(mmA.h, mmD + nmm + 1)
                if nmm >= 4:
                    sc.wait_ge(stq.h, stqD + (nmm - 3) * 16)
                i = sc.activation(SGT[:, (nmm % 4) * ST:(nmm % 4 + 1) * ST],
                                  p_a[nmm % 2][:, 0:ST], AF.Silu, scale=1.0)
                evA.inc(i)

        @block.gpsimd
        def _(sync):
            for bi in range(TAIL_BLK):
                for pc in range(NCC):
                    for sub in range(NSUB):
                        nmm = (bi * NCC + pc) * NSUB + sub
                        sync.wait_ge(evA.h, evD + nmm + 1)
                        if nmm >= 1:
                            sync.wait_ge(stq.h, stqD + nmm * 16)
                        t0 = bi * BLK + sub * ST
                        sync.dma_start(out=sg_d[pc][:, t0:t0 + ST],
                                       in_=SGT[:, (nmm % 4) * ST:(nmm % 4 + 1) * ST]
                                       ).then_inc(stq.h, 16)
            stq.v += TAIL_BLK * NCC * NSUB * 16
            sync.wait_ge(stq.h, stq.v)
        barrier()

    _ns_name, _ns_id = _ns_stack.pop(); nc.leave_named_scope(_ns_name, _ns_id, False)
    _ns_stack.append(("phE", nc.enter_named_scope("phE", False)[0]))
    with contextlib.ExitStack() as ph:
        sba = lambda n, s, d: ph.enter_context(nc.sbuf_tensor(n, s, d))
        TSA = sba("TSA", [128, 2 * BLK], F32)
        TSS = sba("TSS", [128, BLK], F32)
        TSB = sba("TSBe", [128, BLK], BF16)
        BCA = sba("BCA", [32, 2 * BLK], F32)
        BCS = sba("BCS", [32, BLK], F32)
        BCB = sba("BCBe", [32, BLK], BF16)
        ldE, dvE, evE, stqE = ld.v, dvA.v, evA.v, stq.v

        @block.sync
        def _(sync):
            flag_poll(sync, 0, 1)
            for b in range(NBLK):
                sync.wait_ge(dvA.h, dvE + 2 * b)
                if b >= 1:
                    sync.wait_ge(ld.h, ldE + b * 64)
                t0 = b * BLK
                sync.dma_start(out=TSA[:, 0:BLK],
                               in_=sh_ts[0][:, t0:t0 + BLK]).then_inc(ld.h, 16)
                sync.dma_start(out=TSA[:, BLK:2 * BLK],
                               in_=sh_ts[1][:, t0:t0 + BLK]).then_inc(ld.h, 16)
                sync.dma_start(out=BCA[:, 0:BLK],
                               in_=sh_bc[0][:, t0:t0 + BLK]).then_inc(ld.h, 16)
                sync.dma_start(out=BCA[:, BLK:2 * BLK],
                               in_=sh_bc[1][:, t0:t0 + BLK]).then_inc(ld.h, 16)
            ld.v += NBLK * 64

        @block.vector
        def _(v):
            for b in range(NBLK):
                v.wait_ge(ld.h, ldE + (b + 1) * 64)
                if b >= 1:
                    v.wait_ge(evA.h, evE + 2 * b)
                v.tensor_tensor(TSS[:, :], TSA[:, 0:BLK], TSA[:, BLK:2 * BLK],
                                OP.add)
                i = v.tensor_tensor(BCS[:, :], BCA[:, 0:BLK],
                                    BCA[:, BLK:2 * BLK], OP.add)
                dvA.inc(i, 2)

        @block.scalar
        def _(sc):
            for b in range(NBLK):
                sc.wait_ge(dvA.h, dvE + 2 * (b + 1))
                if b >= 2:
                    sc.wait_ge(stq.h, stqE + (b - 1) * 32)
                i = sc.activation(TSB[:, :], TSS[:, :], AF.Copy, scale=1.0)
                i = sc.activation(BCB[:, :], BCS[:, :], AF.Copy, scale=1.0)
                evA.inc(i, 2)

        @block.gpsimd
        def _(sync):
            for b in range(NBLK):
                sync.wait_ge(evA.h, evE + 2 * (b + 1))
                if b >= 1:
                    sync.wait_ge(stq.h, stqE + b * 32)
                t0 = b * BLK
                sync.dma_start(out=ts_d[:, t0:t0 + BLK], in_=TSB[:, :]
                               ).then_inc(stq.h, 16)
                sync.dma_start(out=bc_d[:, t0:t0 + BLK], in_=BCB[:, :]
                               ).then_inc(stq.h, 16)
            stq.v += NBLK * 32
            sync.wait_ge(stq.h, stq.v)
        barrier()

    _ns_name, _ns_id = _ns_stack.pop(); nc.leave_named_scope(_ns_name, _ns_id, False)
    _ns_stack.append(("phF", nc.enter_named_scope("phF", False)[0]))
    with contextlib.ExitStack() as ph:
        sba = lambda n, s, d: ph.enter_context(nc.sbuf_tensor(n, s, d))
        DTW = sba("DTW", [128, DH], BF16)
        WLf = sba("WLf", [128, DH], F32)
        TSB = [sba("TSBf0", [128, BLK], BF16), sba("TSBf1", [128, BLK], BF16)]
        SPS = [sba("SPS0", [128, 4 * ST], F32), sba("SPS1", [128, 4 * ST], F32)]
        DTT = sba("DTT", [128, 4 * ST], BF16)
        p_a = [ph.enter_context(nc.psum_tensor("pf_a0", [128, 512], F32)),
               ph.enter_context(nc.psum_tensor("pf_a1", [128, 512], F32))]
        lwF, wcF = lw.v, wc.v

        @block.sync
        def _(sync):
            sync.dma_start(out=WLf[:, :], in_=dtwt).then_inc(lw.h, 16)
            lw.v += 16

        @block.scalar
        def _(sc):
            sc.wait_ge(lw.h, lwF + 16)
            i = sc.activation(DTW[:, :], WLf[:, :], AF.Copy, scale=1.0)
            wc.inc(i)

        ldF, mmF, evF, dvF, stqF = ld.v, mmA.v, evA.v, dvA.v, stq.v

        @block.sync
        def _(sync):
            for b in range(NBLK):
                sync.wait_ge(mmA.h, max(mmF, mmF + (b - 1) * NCC * NSUB))
                if b >= 1:
                    sync.wait_ge(ld.h, ldF + b * 16)
                sync.dma_start(out=TSB[b % 2][:, :],
                               in_=ts_d[:, b * BLK:(b + 1) * BLK]
                               ).then_inc(ld.h, 16)
            ld.v += NBLK * 16

        @block.tensor
        def _(t):
            t.wait_ge(wc.h, wcF + 1)
            for b in range(NBLK):
                t.wait_ge(ld.h, ldF + (b + 1) * 16)
                for cc in range(NCC):
                    for sub in range(NSUB):
                        nmm = (b * NCC + cc) * NSUB + sub
                        if nmm >= 2:
                            t.wait_ge(evA.h, evF + 4 * (nmm - 1))
                        i = t.matmul(p_a[nmm % 2][:, 0:ST],
                                     DTW[:, cc * 128:(cc + 1) * 128],
                                     TSB[b % 2][:, sub * ST:(sub + 1) * ST],
                                     start=True, stop=True)
                        mmA.inc(i)

        @block.scalar
        def _(sc):
            for b in range(NBLK):
                for cc in range(NCC):
                    for sub in range(NSUB):
                        nmm = (b * NCC + cc) * NSUB + sub
                        sc.wait_ge(mmA.h, mmF + nmm + 1)
                        if nmm >= 1:
                            sc.wait_ge(dvA.h, dvF + nmm)
                        sl = SPS[nmm % 2]
                        bias = dtb_sb[:, cc:cc + 1]
                        sc.activation(sl[:, 0:ST], p_a[nmm % 2][:, 0:ST],
                                      AF.Abs, bias=bias, scale=1.0)
                        sc.activation(sl[:, ST:2 * ST], sl[:, 0:ST], AF.Exp,
                                      scale=-1.0)
                        sc.activation(sl[:, 2 * ST:3 * ST], sl[:, ST:2 * ST],
                                      AF.Ln, bias=1.0, scale=1.0)
                        i = sc.activation(sl[:, 3 * ST:4 * ST],
                                          p_a[nmm % 2][:, 0:ST],
                                          AF.Relu, bias=bias, scale=1.0)
                        evA.inc(i, 4)

        @block.vector
        def _(v):
            for b in range(NBLK):
                for cc in range(NCC):
                    for sub in range(NSUB):
                        nmm = (b * NCC + cc) * NSUB + sub
                        v.wait_ge(evA.h, evF + 4 * (nmm + 1))
                        if nmm >= 4:
                            v.wait_ge(stq.h, stqF + (nmm - 3) * 16)
                        sl = SPS[nmm % 2]
                        dst = DTT[:, (nmm % 4) * ST:(nmm % 4 + 1) * ST]
                        v.tensor_tensor(dst, sl[:, 2 * ST:3 * ST],
                                        sl[:, 3 * ST:4 * ST], OP.add)
                        i = v.tensor_scalar(dst, dst, mask_sb[:, b:b + 1],
                                            None, OP.mult)
                        dvA.inc(i)

        @block.gpsimd
        def _(sync):
            for b in range(NBLK):
                for cc in range(NCC):
                    for sub in range(NSUB):
                        nmm = (b * NCC + cc) * NSUB + sub
                        sync.wait_ge(dvA.h, dvF + nmm + 1)
                        if nmm >= 1:
                            sync.wait_ge(stq.h, stqF + nmm * 16)
                        t0 = b * BLK + sub * ST
                        sync.dma_start(out=dt_d[cc][:, t0:t0 + ST],
                                       in_=DTT[:, (nmm % 4) * ST:(nmm % 4 + 1) * ST]
                                       ).then_inc(stq.h, 16)
            stq.v += NBLK * NCC * NSUB * 16
            sync.wait_ge(stq.h, stq.v)
        barrier()

    _ns_name, _ns_id = _ns_stack.pop(); nc.leave_named_scope(_ns_name, _ns_id, False)
    _ns_stack.append(("phG", nc.enter_named_scope("phG", False)[0]))
    with contextlib.ExitStack() as ph:
        sba = lambda n, s, d: ph.enter_context(nc.sbuf_tensor(n, s, d))
        BBC = sba("BBC", [128, NST * BLK], BF16)
        CBC = sba("CBC", [128, NST * BLK], BF16)
        BCT = [sba("BCT0", [32, BLK], BF16), sba("BCT1", [32, BLK], BF16)]
        DTU = [sba("DTU0", [128, BLK], BF16), sba("DTU1", [128, BLK], BF16)]
        UU = [sba("UU0", [128, BLK], BF16), sba("UU1", [128, BLK], BF16)]
        SGU = [sba("SGU0", [128, BLK], BF16), sba("SGU1", [128, BLK], BF16)]
        DU = sba("DU", [128, BLK], BF16)
        DBU = [sba("DBU0", [128, BLK], BF16), sba("DBU1", [128, BLK], BF16)]
        SS = [sba("SS0", [128, BLK], BF16), sba("SS1", [128, BLK], BF16)]
        SCT = [sba("SCT0", [128, BLK], BF16), sba("SCT1", [128, BLK], BF16)]
        YG = [sba("YG0", [128, BLK], BF16), sba("YG1", [128, BLK], BF16)]
        DA = [sba("DA0", [128, BLK], F32), sba("DA1", [128, BLK], F32)]
        YY = sba("YY", [128, BLK], F32)
        p_a = [ph.enter_context(nc.psum_tensor("pg_a0", [128, 512], F32)),
               ph.enter_context(nc.psum_tensor("pg_a1", [128, 512], F32))]
        BH = BLK // 2
        p_y = [[ph.enter_context(nc.psum_tensor("pg_y00", [128, BH], F32)),
                ph.enter_context(nc.psum_tensor("pg_y01", [128, BH], F32))],
               [ph.enter_context(nc.psum_tensor("pg_y10", [128, BH], F32)),
                ph.enter_context(nc.psum_tensor("pg_y11", [128, BH], F32))]]

        ldG, lwG, mmG, evG, acG, dvG, dvY, stqG = (
            ld.v, lw.v, mmA.v, evA.v, evB.v, dvA.v, dvB.v, stq.v)
        hsG, gqG, evCG, mmYG = hs.v, gq.v, evC.v, mmY.v
        U = NBLK * NCC
        NTAILU = TAIL_BLK * NCC
        lu = lambda j: _g_loads_until(c, j // NCC, j % NCC)
        W_DU = lambda j: dvG + 1 if j == 0 else dvG + 17 * j

        @block.sync
        def _(sync):
            for b in range(NBLK):
                tail = b >= TB0
                sync.wait_ge(mmA.h,
                             mmG + (_g_mms_until(c, b - 2) if b >= 2 else 0))
                if b >= 1:
                    sync.wait_ge(lw.h, lwG + b * 16)
                sync.dma_start(out=BCT[b % 2][:, :],
                               in_=bc_d[:, b * BLK:(b + 1) * BLK]
                               ).then_inc(lw.h, 16)
                for cc in range(NCC):
                    j = b * NCC + cc
                    jt = (b - TB0) * NCC + cc
                    if j >= 2:
                        sync.wait_ge(dvA.h, W_DU(j - 2))
                        sync.wait_ge(evB.h, acG + (j - 1) * NST)
                    if tail and jt >= 2:
                        sync.wait_ge(dvB.h, dvY + jt - 1)
                    if j >= 1:
                        sync.wait_ge(ld.h, ldG + lu(j - 1))
                    sync.dma_start(out=DTU[j % 2][:, :],
                                   in_=dt_d[cc][:, b * BLK:(b + 1) * BLK]
                                   ).then_inc(ld.h, 16)
                    sync.dma_start(out=UU[j % 2][:, :],
                                   in_=u_d[cc][:, b * BLK:(b + 1) * BLK]
                                   ).then_inc(ld.h, 16)
                    ld.v += 32
                    if tail:
                        sync.dma_start(
                            out=SGU[j % 2][:, :],
                            in_=sg_d[cc][:, (b - TB0) * BLK:(b - TB0 + 1) * BLK]
                            ).then_inc(ld.h, 16)
                        ld.v += 16
            lw.v += NBLK * 16

        nbc = [0]
        @block.tensor
        def _(t):
            for b in range(NBLK):
                tail = b >= TB0
                t.wait_ge(lw.h, lwG + (b + 1) * 16)
                if b >= 1:
                    t.wait_ge(gq.h, gqG + 16 * b * NCC)
                    if b - 1 >= TB0:
                        t.wait_ge(evC.h, evCG + (b - TB0) * NCC * NST)
                reps = 2 if tail else 1
                for rep in range(reps):
                    for n in range(NST):
                        for sub in range(NSUB):
                            idx = nbc[0]
                            if idx >= 2:
                                t.wait_ge(evA.h, evG + idx - 1)
                            selcol = (rep * NST + n) * 128
                            i = t.matmul(p_a[idx % 2][:, 0:ST],
                                         sel_sb[:, selcol:selcol + 128],
                                         BCT[b % 2][:, sub * ST:(sub + 1) * ST],
                                         start=True, stop=True)
                            mmA.inc(i)
                            nbc[0] += 1
                if tail:
                    for cc in range(NCC):
                        jt = (b - TB0) * NCC + cc
                        if jt >= 2:
                            t.wait_ge(dvB.h, dvY + jt - 1)
                        for n in range(NST):
                            t.wait_ge(evC.h, evCG + jt * NST + n + 1)
                            for h in range(2):
                                i = t.matmul(p_y[jt % 2][h][:, :], idnb[:, :],
                                             SCT[n % 2][:, h * BH:(h + 1) * BH],
                                             start=(n == 0), stop=(n == NST - 1))
                                mmY.inc(i)

        nev = [0]
        @block.scalar
        def _(sc):
            for b in range(NBLK):
                tail = b >= TB0
                reps = 2 if tail else 1
                for rep in range(reps):
                    dstA = BBC if rep == 0 else CBC
                    for n in range(NST):
                        for sub in range(NSUB):
                            idx = nev[0]
                            sc.wait_ge(mmA.h, mmG + idx + 1)
                            i = sc.activation(
                                dstA[:, n * BLK + sub * ST:n * BLK + (sub + 1) * ST],
                                p_a[idx % 2][:, 0:ST], AF.Copy, scale=1.0)
                            evA.inc(i)
                            nev[0] += 1
                for cc in range(NCC):
                    j = b * NCC + cc
                    sc.wait_ge(ld.h, ldG + _g_loads_until(c, b, cc))
                    for n in range(NST):
                        g = j * NST + n
                        if g >= 2:
                            sc.wait_ge(hs.h, hsG + g - 1)
                        i = sc.activation(
                            DA[g % 2][:, :], DTU[j % 2][:, :], AF.Exp,
                            scale=a_sb[:, cc * NST + n:cc * NST + n + 1])
                        evB.inc(i)

        @block.vector
        def _(v):
            v.wait_ge(ld.h, ldG + lu(0))
            i = v.tensor_tensor(DU[:, :], DTU[0][:, :], UU[0][:, :], OP.mult)
            dvA.inc(i)
            for b in range(NBLK):
                tail = b >= TB0
                for cc in range(NCC):
                    j = b * NCC + cc
                    g0 = j * NST
                    jt = (b - TB0) * NCC + cc
                    if tail:
                        v.wait_ge(evA.h, evG + _g_evacs_until(c, b))
                    for n in range(NST):
                        g = g0 + n
                        v.wait_ge(gq.h, gqG + 16 * j + n + 1)
                        v.wait_ge(evB.h, acG + g + 1)
                        i = v.tensor_tensor_scan(
                            SS[n % 2][:, :], DA[g % 2][:, :], DBU[n % 2][:, :],
                            state_sb[:, cc * NST + n:cc * NST + n + 1],
                            OP.mult, OP.add)
                        hs.inc(i)
                        if n == 0:
                            if j >= 1:
                                ccp = (j - 1) % NCC
                                i = v.tensor_copy(
                                    state_sb[:, ccp * NST + NST - 1:
                                             ccp * NST + NST],
                                    SS[1][:, BLK - 1:BLK])
                                dvA.inc(i)
                        else:
                            m = n - 1
                            i = v.tensor_copy(
                                state_sb[:, cc * NST + m:cc * NST + m + 1],
                                SS[m % 2][:, BLK - 1:BLK])
                            dvA.inc(i)
                            if tail:
                                if m >= 2:
                                    v.wait_ge(mmY.h,
                                              mmYG + jt * 2 * NST + 2 * (m - 1))
                                i = v.tensor_tensor(
                                    SCT[m % 2][:, :], SS[m % 2][:, :],
                                    CBC[:, m * BLK:(m + 1) * BLK], OP.mult)
                                evC.inc(i)
                    if j < U - 1:
                        v.wait_ge(ld.h, ldG + lu(j + 1))
                        i = v.tensor_tensor(DU[:, :], DTU[(j + 1) % 2][:, :],
                                            UU[(j + 1) % 2][:, :], OP.mult)
                        dvA.inc(i)
                    if tail:
                        m = NST - 1
                        v.wait_ge(hs.h, hs.v)
                        v.wait_ge(mmY.h, mmYG + jt * 2 * NST + 2 * (m - 1))
                        i = v.tensor_tensor(
                            SCT[m % 2][:, :], SS[m % 2][:, :],
                            CBC[:, m * BLK:(m + 1) * BLK], OP.mult)
                        evC.inc(i)
                        v.wait_ge(mmY.h, mmYG + (jt + 1) * 2 * NST)
                        if jt >= 2:
                            v.wait_ge(stq.h, stqG + (jt - 1) * 16)
                        v.scalar_tensor_tensor(YY[:, 0:BH], UU[j % 2][:, 0:BH],
                                               dpar_sb[:, cc:cc + 1],
                                               p_y[jt % 2][0][:, :],
                                               OP.mult, OP.add)
                        v.scalar_tensor_tensor(YY[:, BH:BLK],
                                               UU[j % 2][:, BH:BLK],
                                               dpar_sb[:, cc:cc + 1],
                                               p_y[jt % 2][1][:, :],
                                               OP.mult, OP.add)
                        v.wait_ge(ld.h, ldG + lu(j))
                        i = v.tensor_tensor(YG[jt % 2][:, :], YY[:, :],
                                            SGU[j % 2][:, :], OP.mult)
                        dvB.inc(i)
                    if j == U - 1:
                        i = v.tensor_copy(
                            state_sb[:, cc * NST + NST - 1:cc * NST + NST],
                            SS[1][:, BLK - 1:BLK])
                        dvA.inc(i)

        @block.gpsimd
        def _(sync):
            for b in range(NBLK):
                tail = b >= TB0
                for cc in range(NCC):
                    j = b * NCC + cc
                    g0 = j * NST
                    jt = (b - TB0) * NCC + cc
                    sync.wait_ge(dvA.h, W_DU(j))
                    sync.wait_ge(evA.h, evG + _g_evacs_until(c, b))
                    for n in range(NST):
                        g = g0 + n
                        if tail and n == 2 and jt >= 1:
                            m = jt - 1
                            sync.wait_ge(dvB.h, dvY + m + 1)
                            if m >= 1:
                                sync.wait_ge(stq.h, stqG + m * 16)
                            ccp = m % NCC
                            t0 = (m // NCC) * BLK
                            slot_dma_out(sync, sh_yg[0, ccp][:, t0:t0 + BLK],
                                         sh_yg[1, ccp][:, t0:t0 + BLK],
                                         YG[m % 2][:, :], stq, reg=rg_slot)
                        if g >= 2:
                            sync.wait_ge(hs.h, hsG + g - 1)
                        i = sync.tensor_tensor(DBU[n % 2][:, :], DU[:, :],
                                               BBC[:, n * BLK:(n + 1) * BLK],
                                               OP.mult)
                        gq.inc(i)
            m = NTAILU - 1
            sync.wait_ge(dvB.h, dvY + m + 1)
            sync.wait_ge(stq.h, stqG + m * 16)
            ccp = m % NCC
            t0 = (m // NCC) * BLK
            slot_dma_out(sync, sh_yg[0, ccp][:, t0:t0 + BLK],
                         sh_yg[1, ccp][:, t0:t0 + BLK],
                         YG[m % 2][:, :], stq, reg=rg_slot)
            sync.wait_ge(stq.h, stq.v)

        @block.sync
        def _(sync):
            sync.wait_ge(stq.h, stq.v)
            flag_write(sync, 1, 2)
        barrier()

    _ns_name, _ns_id = _ns_stack.pop(); nc.leave_named_scope(_ns_name, _ns_id, False)
    _ns_stack.append(("phH", nc.enter_named_scope("phH", False)[0]))
    with contextlib.ExitStack() as ph:
        sba = lambda n, s, d: ph.enter_context(nc.sbuf_tensor(n, s, d))
        NKG = 2 * NCC
        YGR = sba("YGR", [128, NKG * TLH], BF16)
        WOB = sba("WOB", [128, NKG * 128], BF16)
        WLh = [sba("WLh0", [128, NKG * 128], F32),
               sba("WLh1", [128, NKG * 128], F32)]
        SEV = sba("SEV", [128, 4 * ST], F32)
        p_a = [ph.enter_context(nc.psum_tensor("ph_a0", [128, 512], F32)),
               ph.enter_context(nc.psum_tensor("ph_a1", [128, 512], F32))]
        ldH, lwH, wcH, mmH, evH, stqH = ld.v, lw.v, wc.v, mmA.v, evA.v, stq.v
        NTT = TLH // ST

        @block.sync
        def _(sync):
            flag_poll(sync, 1, 2)
            for ss in range(2):
                for cc in range(NCC):
                    kc = ss * NCC + cc
                    slot_dma_in(sync, YGR[:, kc * TLH:(kc + 1) * TLH],
                                sh_yg[ss, cc][:, 0:TLH],
                                sh_yg[ss, cc][:, TLH:2 * TLH], ld)
            for hp in range(NHC):
                sync.wait_ge(wc.h, max(wcH, wcH + hp - 1))
                if hp >= 1:
                    sync.wait_ge(lw.h, lwH + hp * 16)
                sync.dma_start(out=WLh[hp % 2][:, :], in_=woutt[hp]
                               ).then_inc(lw.h, 16)
            lw.v += NHC * 16

        @block.scalar
        def _(sc):
            for hp in range(NHC):
                sc.wait_ge(lw.h, lwH + (hp + 1) * 16)
                if hp >= 1:
                    sc.wait_ge(mmA.h, mmH + hp * NTT)
                i = sc.activation(WOB[:, :], WLh[hp % 2][:, :], AF.Copy,
                                  scale=1.0)
                wc.inc(i)
                for tt in range(NTT):
                    ntile = hp * NTT + tt
                    sc.wait_ge(mmA.h, mmH + ntile + 1)
                    if ntile >= 4:
                        sc.wait_ge(stq.h, stqH + (ntile - 3) * 16)
                    i = sc.activation(
                        SEV[:, (ntile % 4) * ST:(ntile % 4 + 1) * ST],
                        p_a[ntile % 2][:, 0:ST], AF.Copy, scale=1.0)
                    evA.inc(i)

        @block.tensor
        def _(t):
            t.wait_ge(ld.h, ldH + NKG * 16)
            for hp in range(NHC):
                t.wait_ge(wc.h, wcH + hp + 1)
                for tt in range(NTT):
                    ntile = hp * NTT + tt
                    if ntile >= 2:
                        t.wait_ge(evA.h, evH + ntile - 1)
                    for kc in range(NKG):
                        i = t.matmul(p_a[ntile % 2][:, 0:ST],
                                     WOB[:, kc * 128:(kc + 1) * 128],
                                     YGR[:, kc * TLH + tt * ST:kc * TLH + (tt + 1) * ST],
                                     start=(kc == 0), stop=(kc == NKG - 1))
                    mmA.inc(i)

        @block.gpsimd
        def _(sync):
            for hp in range(NHC):
                for tt in range(NTT):
                    ntile = hp * NTT + tt
                    sync.wait_ge(evA.h, evH + ntile + 1)
                    if ntile >= 1:
                        sync.wait_ge(stq.h, stqH + ntile * 16)
                    sync.dma_start(out=s_d[hp][:, tt * ST:(tt + 1) * ST],
                                   in_=SEV[:, (ntile % 4) * ST:(ntile % 4 + 1) * ST]
                                   ).then_inc(stq.h, 16)
            stq.v += NHC * NTT * 16
            sync.wait_ge(stq.h, stq.v)
        barrier()

    _ns_name, _ns_id = _ns_stack.pop(); nc.leave_named_scope(_ns_name, _ns_id, False)
    _ns_stack.append(("phI", nc.enter_named_scope("phI", False)[0]))
    NSB = max(1, TLH // BLK)
    SBS = TLH // NSB
    NS2 = SBS // ST
    for sb_i in range(NSB):
        with contextlib.ExitStack() as ph:
            sba = (lambda sfx: (lambda n, s, d: ph.enter_context(
                nc.sbuf_tensor(n + sfx, s, d))))(f"_sb{sb_i}")
            XNS = sba("XNS", [128, NKC * SBS], BF16)
            HDN = sba("HDN", [128, NIC * SBS], BF16)
            GWB = sba("GWB", [128, NKC * 128], BF16)
            UWB = sba("UWB", [128, NKC * 128], BF16)
            DWB = sba("DWB", [128, NIC * 128], BF16)
            GUS = sba("GUS", [128, 4 * ST], F32)
            SGF = sba("SGF", [128, 2 * ST], F32)
            SRES = sba("SRES", [128, 2 * ST], F32)
            p_g = ph.enter_context(nc.psum_tensor(f"pi_g{sb_i}", [128, 512], F32))
            p_u = ph.enter_context(nc.psum_tensor(f"pi_u{sb_i}", [128, 512], F32))
            p_d = [ph.enter_context(nc.psum_tensor(f"pi_d0{sb_i}", [128, 512], F32)),
                   ph.enter_context(nc.psum_tensor(f"pi_d1{sb_i}", [128, 512], F32))]

            ldI, lwI, wcI, lgI = ld.v, lw.v, wc.v, lg.v
            mmI, evI, dvI = mmA.v, evA.v, dvA.v
            sA = F - TL + 0 * TLH + sb_i * SBS
            sB = F - TL + 1 * TLH + sb_i * SBS

            @block.sync
            def _(sync):
                for kc in range(NKC):
                    bA, cA = sA // BLK, sA % BLK
                    bB, cB = sB // BLK, sB % BLK
                    slot_dma_in(sync, XNS[:, kc * SBS:(kc + 1) * SBS],
                                xnt_d[bA, kc][:, cA:cA + SBS],
                                xnt_d[bB, kc][:, cB:cB + SBS], ld)

            @block.gpsimd
            def _(g):
                for ic in range(NIC):
                    if ic >= 1:
                        g.wait_ge(mmA.h, mmI + 2 * ic * NS2)
                        g.wait_ge(lg.h, lgI + 32 * ic)
                    g.dma_start(out=GWB[:, :], in_=gatet[ic]).then_inc(lg.h, 16)
                    g.wait_ge(lg.h, lgI + 32 * ic + 16)
                    g.dma_start(out=UWB[:, :], in_=upt[ic]).then_inc(lg.h, 16)
                lg.v += NIC * 32

            @block.scalar
            def _(sc):
                for ic in range(NIC):
                    for s2 in range(NS2):
                        nmm = ic * NS2 + s2
                        sc.wait_ge(mmA.h, mmI + 2 * nmm + 1)
                        if nmm >= 1:
                            sc.wait_ge(dvA.h, dvI + nmm)
                        sc.activation(GUS[:, 0:ST], p_g[:, 0:ST], AF.Copy,
                                      scale=fsc_sb[:, ic:ic + 1])
                        i = sc.activation(GUS[:, ST:2 * ST], GUS[:, 0:ST],
                                          AF.Silu, scale=1.0)
                        evA.inc(i)
                        sc.wait_ge(mmA.h, mmI + 2 * nmm + 2)
                        i = sc.activation(
                            GUS[:, 2 * ST:3 * ST], p_u[:, 0:ST], AF.Copy,
                            scale=fsc_sb[:, NIC + ic:NIC + ic + 1])
                        evA.inc(i)

            @block.tensor
            def _(t):
                t.wait_ge(ld.h, ldI + NKC * 16)
                for ic in range(NIC):
                    for s2 in range(NS2):
                        nmm = ic * NS2 + s2
                        t.wait_ge(lg.h, lgI + 32 * ic + 16)
                        if nmm >= 1:
                            t.wait_ge(evA.h, evI + 2 * nmm - 1)
                        for kc in range(NKC):
                            i = t.matmul(
                                p_g[:, 0:ST], GWB[:, kc * 128:(kc + 1) * 128],
                                XNS[:, kc * SBS + s2 * ST:kc * SBS + (s2 + 1) * ST],
                                start=(kc == 0), stop=(kc == NKC - 1))
                        mmA.inc(i)
                        t.wait_ge(lg.h, lgI + 32 * ic + 32)
                        if nmm >= 1:
                            t.wait_ge(evA.h, evI + 2 * nmm)
                        for kc in range(NKC):
                            i = t.matmul(
                                p_u[:, 0:ST], UWB[:, kc * 128:(kc + 1) * 128],
                                XNS[:, kc * SBS + s2 * ST:kc * SBS + (s2 + 1) * ST],
                                start=(kc == 0), stop=(kc == NKC - 1))
                        mmA.inc(i)


            @block.vector
            def _(v):
                for ic in range(NIC):
                    for s2 in range(NS2):
                        nmm = ic * NS2 + s2
                        v.wait_ge(evA.h, evI + 2 * nmm + 2)
                        i = v.tensor_tensor(
                            HDN[:, ic * SBS + s2 * ST:ic * SBS + (s2 + 1) * ST],
                            GUS[:, ST:2 * ST], GUS[:, 2 * ST:3 * ST], OP.mult)
                        dvA.inc(i)

            ldI2, lwI2, wcI2, lgI2 = ld.v, lw.v, wc.v, lg.v
            mmT, evT, evS, dvS, stqI = mmB.v, evB.v, evC.v, dvB.v, stq.v

            @block.gpsimd
            def _(g):
                def sgf_store(nmm):
                    hc2, s22 = divmod(nmm, NS2)
                    g.wait_ge(dvB.h, dvS + nmm + 1)
                    if nmm >= 1:
                        g.wait_ge(stq.h, stqI + nmm * 16)
                    g.dma_start(
                        out=sgf_d[hc2][:, sb_i * SBS + s22 * ST:sb_i * SBS + (s22 + 1) * ST],
                        in_=SGF[:, (nmm % 2) * ST:(nmm % 2 + 1) * ST]
                        ).then_inc(stq.h, 16)

                for hc in range(NHC):
                    if hc >= 1:
                        g.wait_ge(mmB.h, mmT + hc * NS2)
                        g.wait_ge(lg.h, lgI2 + hc * 16)
                    g.dma_start(out=DWB[:, :], in_=downt[hc]).then_inc(lg.h, 16)
                    if hc >= 1:
                        for s2 in range(NS2):
                            sgf_store((hc - 1) * NS2 + s2)
                for s2 in range(NS2):
                    sgf_store((NHC - 1) * NS2 + s2)
                stq.v += NHC * NS2 * 16
                g.wait_ge(stq.h, stq.v)
                lg.v += NHC * 16

            @block.sync
            def _(sync):
                for hc in range(NHC):
                    for s2 in range(NS2):
                        nmm = hc * NS2 + s2
                        sync.wait_ge(dvB.h, max(dvS, dvS + nmm - 1))
                        if nmm >= 1:
                            sync.wait_ge(ld.h, ldI2 + nmm * 16)
                        sync.dma_start(
                            out=SRES[:, (nmm % 2) * ST:(nmm % 2 + 1) * ST],
                            in_=s_d[hc][:, sb_i * SBS + s2 * ST:sb_i * SBS + (s2 + 1) * ST]
                            ).then_inc(ld.h, 16)
                ld.v += NHC * NS2 * 16

            @block.scalar
            def _(sc):
                for hc in range(NHC):
                    for s2 in range(NS2):
                        nmm = hc * NS2 + s2
                        sc.wait_ge(mmB.h, mmT + nmm + 1)
                        if nmm >= 1:
                            sc.wait_ge(dvB.h, dvS + nmm)
                        sc.activation(
                            GUS[:, 3 * ST:4 * ST], p_d[nmm % 2][:, 0:ST],
                            AF.Copy,
                            scale=fsc_sb[:, 2 * NIC + hc:2 * NIC + hc + 1])
                        i = sc.activation(GUS[:, 3 * ST:4 * ST],
                                          GUS[:, 3 * ST:4 * ST], AF.Silu,
                                          scale=1.0)
                        evB.inc(i)

            @block.tensor
            def _(t):
                t.wait_ge(dvA.h, dvI + NIC * NS2)
                for hc in range(NHC):
                    t.wait_ge(lg.h, lgI2 + (hc + 1) * 16)
                    for s2 in range(NS2):
                        nmm = hc * NS2 + s2
                        if nmm >= 2:
                            t.wait_ge(evB.h, evT + nmm - 1)
                        for ic in range(NIC):
                            i = t.matmul(
                                p_d[nmm % 2][:, 0:ST],
                                DWB[:, ic * 128:(ic + 1) * 128],
                                HDN[:, ic * SBS + s2 * ST:ic * SBS + (s2 + 1) * ST],
                                start=(ic == 0), stop=(ic == NIC - 1))
                        mmB.inc(i)

            @block.vector
            def _(v):
                for hc in range(NHC):
                    for s2 in range(NS2):
                        nmm = hc * NS2 + s2
                        v.wait_ge(evB.h, evT + nmm + 1)
                        v.wait_ge(ld.h, ldI2 + (nmm + 1) * 16)
                        if nmm >= 2:
                            v.wait_ge(stq.h, stqI + (nmm - 1) * 16)
                        i = v.tensor_tensor(
                            SGF[:, (nmm % 2) * ST:(nmm % 2 + 1) * ST],
                            SRES[:, (nmm % 2) * ST:(nmm % 2 + 1) * ST],
                            GUS[:, 3 * ST:4 * ST], OP.mult)
                        dvB.inc(i)

            barrier()

    _ns_name, _ns_id = _ns_stack.pop(); nc.leave_named_scope(_ns_name, _ns_id, False)
    _ns_stack.append(("phJ", nc.enter_named_scope("phJ", False)[0]))
    with contextlib.ExitStack() as ph:
        sba = lambda n, s, d: ph.enter_context(nc.sbuf_tensor(n, s, d))
        SGP = [sba("SGP0", [128, NHC * 128], F32),
               sba("SGP1", [128, NHC * 128], F32)]
        OUTS = [sba("OUTS0", [128, D], F32), sba("OUTS1", [128, D], F32)]
        XRES = [sba("XRES0", [128, D], F32), sba("XRES1", [128, D], F32)]
        p_t = [ph.enter_context(nc.psum_tensor("pj_t0", [128, 512], F32)),
               ph.enter_context(nc.psum_tensor("pj_t1", [128, 512], F32))]
        NTB = TLH // 128
        ldJ, mmJ, evJ, dvJ, stqJ = ld.v, mmA.v, evA.v, dvA.v, stq.v
        TPG = min(4, NHC)
        ngJ = NHC // TPG

        @block.sync
        def _(sync):
            for tb in range(NTB):
                sync.wait_ge(mmA.h, max(mmJ, mmJ + (tb - 1) * ngJ))
                if tb >= 1:
                    sync.wait_ge(ld.h, ldJ + tb * (NHC + 1) * 16)
                for hc in range(NHC):
                    sync.dma_start(out=SGP[tb % 2][:, hc * 128:(hc + 1) * 128],
                                   in_=sgf_d[hc][:, tb * 128:(tb + 1) * 128]
                                   ).then_inc(ld.h, 16)
                sync.wait_ge(dvA.h, max(dvJ, dvJ + tb - 1))
                rA = F - TL + 0 * TLH + tb * 128
                rB = F - TL + 1 * TLH + tb * 128
                slot_dma_in(sync, XRES[tb % 2][:, :],
                            x_f[rA:rA + 128, :], x_f[rB:rB + 128, :], ld)
                ld.v += NHC * 16

        @block.tensor
        def _(t):
            for tb in range(NTB):
                t.wait_ge(ld.h, ldJ + (tb + 1) * (NHC + 1) * 16)
                for g in range(ngJ):
                    ng = tb * ngJ + g
                    if ng >= 2:
                        t.wait_ge(evA.h, evJ + ng - 1)
                    for j in range(TPG):
                        hc = g * TPG + j
                        i = t.transpose(p_t[ng % 2][:, j * 128:(j + 1) * 128],
                                        SGP[tb % 2][:, hc * 128:(hc + 1) * 128],
                                        idnf[:, :])
                    mmA.inc(i)

        @block.scalar
        def _(sc):
            for tb in range(NTB):
                for g in range(ngJ):
                    ng = tb * ngJ + g
                    sc.wait_ge(mmA.h, mmJ + ng + 1)
                    if tb >= 2 and g == 0:
                        sc.wait_ge(dvA.h, dvJ + tb - 1)
                        sc.wait_ge(stq.h, stqJ + (tb - 1) * 16)
                    for j in range(TPG):
                        hc = g * TPG + j
                        i = sc.activation(
                            OUTS[tb % 2][:, hc * 128:(hc + 1) * 128],
                            p_t[ng % 2][:, j * 128:(j + 1) * 128],
                            AF.Copy, scale=1.0)
                    evA.inc(i)

        @block.vector
        def _(v):
            for tb in range(NTB):
                v.wait_ge(evA.h, evJ + (tb + 1) * ngJ)
                v.wait_ge(ld.h, ldJ + (tb + 1) * (NHC + 1) * 16)
                if tb >= 2:
                    v.wait_ge(stq.h, stqJ + (tb - 1) * 16)
                i = v.tensor_tensor(OUTS[tb % 2][:, :], OUTS[tb % 2][:, :],
                                    XRES[tb % 2][:, :], OP.add)
                dvA.inc(i)

        @block.gpsimd
        def _(sync):
            for tb in range(NTB):
                sync.wait_ge(dvA.h, dvJ + tb + 1)
                if tb >= 1:
                    sync.wait_ge(stq.h, stqJ + tb * 16)
                sync.dma_start(out=out_r[tb * 128:(tb + 1) * 128, :],
                               in_=OUTS[tb % 2][:, :]).then_inc(stq.h, 16)
            stq.v += NTB * 16
            sync.wait_ge(stq.h, stq.v)
        barrier()

    _ns_name, _ns_id = _ns_stack.pop(); nc.leave_named_scope(_ns_name, _ns_id, False)
    if debug:
        d_ln = nc.dram_tensor("d_ln", [128, 512], F32, kind="ExternalOutput").ap()
        dbg_specs = [
            ("d_xnt", [NBLK, NKC, 128, BLK], xnt_d),
            ("d_upre", [NCC, 128, F], upre_d),
            ("d_u", [NCC, 128, F], u_d),
            ("d_ts", [128, F], ts_d),
            ("d_bc", [32, F], bc_d),
            ("d_dt", [NCC, 128, F], dt_d),
            ("d_sg", [NCC, 128, TL], sg_d),
            ("d_s", [NHC, 128, TLH], s_d),
            ("d_sgf", [NHC, 128, TLH], sgf_d),
            ("d_ygA", [NCC, 128, TL], sh_yg[0]),
            ("d_ygB", [NCC, 128, TL], sh_yg[1]),
        ]
        dbg_sem = Sem(st_, nc, "dbg")

        @block.sync
        def _(sync):
            for name, shape, src_t in dbg_specs:
                dt_out = F32 if name in ("d_s", "d_sgf") else BF16
                od = nc.dram_tensor(name, shape, dt_out, kind="ExternalOutput").ap()
                ap = src_t.ap() if callable(getattr(src_t, "ap", None)) else src_t
                sync.dma_start(out=od, in_=ap).then_inc(dbg_sem.h, 16)
                dbg_sem.v += 16
            sync.dma_start(out=d_ln, in_=lndbg[:, :]).then_inc(dbg_sem.h, 16)
            dbg_sem.v += 16
            sync.wait_ge(dbg_sem.h, dbg_sem.v)
        barrier()

    st_.close()
    return nc



def stage_inputs(c, inp, nonce):
    F, D, DI, DH = c["F"], c["D"], c["DI"], c["DH"]
    NCC, NKC, NIC, NHC = c["NCC"], c["NKC"], c["NIC"], c["NHC"]
    NST, BLK, NBLK, TL, TLH = c["NST"], c["BLK"], c["NBLK"], c["TL"], c["TLH"]
    bf = ml_dtypes.bfloat16

    x = np.asarray(inp["L0_attn_hidden_states"], np.float32).reshape(-1, D)
    w_in = np.asarray(inp["L0_attn_in_proj_w"], np.float32)
    conv_w = np.asarray(inp["L0_attn_conv_w"], np.float32)
    conv_b = np.asarray(inp["L0_attn_conv_b"], np.float32)
    xpw = np.asarray(inp["L0_attn_x_proj_w"], np.float32)
    dtw = np.asarray(inp["L0_attn_dt_proj_w"], np.float32)
    dtb = np.asarray(inp["L0_attn_dt_proj_b"], np.float32)
    alog = np.asarray(inp["L0_attn_A_log"], np.float32)
    dpar = np.asarray(inp["L0_attn_D_param"], np.float32)
    wout = np.asarray(inp["L0_attn_out_proj_w"], np.float32)
    gw = np.asarray(inp["L0_ffn_gate_w"], np.float32)
    uw = np.asarray(inp["L0_ffn_up_w"], np.float32)
    dw = np.asarray(inp["L0_ffn_down_w"], np.float32)
    gs = np.asarray(inp["L0_ffn_gate_s"], np.float32)
    us = np.asarray(inp["L0_ffn_up_s"], np.float32)
    ds = np.asarray(inp["L0_ffn_down_s"], np.float32)

    idnb = np.eye(128, dtype=bf)
    idnf = np.eye(128, dtype=np.float32)
    sel = np.zeros((32, 2 * NST * 128), dtype=bf)
    for jj in range(2 * NST):
        sel[jj, jj * 128:(jj + 1) * 128] = 1.0

    in_maps = []
    for core in range(N_CORES):
        q, slot = core // 2, core % 2
        pend = (q + 1) * TL
        pad = F - pend
        xf = np.zeros((F, D), np.float32)
        xf[pad:] = x[:pend]
        mask = np.zeros((128, NBLK), np.float32)
        mask[:, pad // BLK:] = 1.0
        ch0 = slot * DH

        wu = np.empty((NCC, 128, D), np.float32)
        wg = np.empty((NCC, 128, D), np.float32)
        for pc in range(NCC):
            blk_u = w_in[ch0 + pc * 128:ch0 + (pc + 1) * 128, :]
            blk_g = w_in[DI + ch0 + pc * 128:DI + ch0 + (pc + 1) * 128, :]
            wu[pc] = (blk_u.T.reshape(NKC, 128, 128)
                      .transpose(1, 0, 2).reshape(128, D))
            wg[pc] = (blk_g.T.reshape(NKC, 128, 128)
                      .transpose(1, 0, 2).reshape(128, D))

        xp = np.empty((128, NCC * 160), np.float32)
        for pc in range(NCC):
            xp[:, pc * 160:(pc + 1) * 160] = \
                xpw[:, ch0 + pc * 128:ch0 + (pc + 1) * 128].T
        dtt = dtw[ch0:ch0 + DH, :].T.copy()

        cvd = np.zeros((128, NCC * 512), np.float32)
        for pc in range(NCC):
            for k in range(128):
                for j in range(4):
                    cvd[k, pc * 512 + j * 128 + k] = \
                        conv_w[ch0 + pc * 128 + k, 0, j]

        vec = np.zeros((128, 3 * NCC), np.float32)
        al = np.zeros((128, NCC * NST), np.float32)
        for pc in range(NCC):
            vec[:, pc] = conv_b[ch0 + pc * 128:ch0 + (pc + 1) * 128]
            vec[:, NCC + pc] = dtb[ch0 + pc * 128:ch0 + (pc + 1) * 128]
            vec[:, 2 * NCC + pc] = dpar[ch0 + pc * 128:ch0 + (pc + 1) * 128]
            al[:, pc * NST:(pc + 1) * NST] = \
                alog[ch0 + pc * 128:ch0 + (pc + 1) * 128, :]

        wo = np.empty((NHC, 128, 2 * NCC * 128), np.float32)
        for hp in range(NHC):
            blk = wout[hp * 128:(hp + 1) * 128, :]
            wo[hp] = (blk.T.reshape(2 * NCC, 128, 128)
                      .transpose(1, 0, 2).reshape(128, 2 * NCC * 128))

        gt = np.empty((NIC, 128, NKC * 128), np.float32)
        ut = np.empty((NIC, 128, NKC * 128), np.float32)
        for ic in range(NIC):
            blkg = gw[ic * 128:(ic + 1) * 128, :]
            blku = uw[ic * 128:(ic + 1) * 128, :]
            gt[ic] = (blkg.T.reshape(NKC, 128, 128)
                      .transpose(1, 0, 2).reshape(128, NKC * 128))
            ut[ic] = (blku.T.reshape(NKC, 128, 128)
                      .transpose(1, 0, 2).reshape(128, NKC * 128))
        dt_ = np.empty((NHC, 128, NIC * 128), np.float32)
        for hc in range(NHC):
            blk = dw[hc * 128:(hc + 1) * 128, :]
            dt_[hc] = (blk.T.reshape(NIC, 128, 128)
                       .transpose(1, 0, 2).reshape(128, NIC * 128))

        fs = np.zeros((128, 2 * NIC + NHC), np.float32)
        for ic in range(NIC):
            fs[:, ic] = gs[ic * 128:(ic + 1) * 128]
            fs[:, NIC + ic] = us[ic * 128:(ic + 1) * 128]
        for hc in range(NHC):
            fs[:, 2 * NIC + hc] = ds[hc * 128:(hc + 1) * 128]

        in_maps.append({
            "x_f": xf, "nonce": np.array([[nonce]], np.int32),
            "winu": wu, "wing": wg, "xpt": xp, "dtwt": dtt, "convd": cvd,
            "vecs": vec, "alog": al, "maskc": mask, "selm": sel,
            "idnb": idnb, "idnf": idnf, "woutt": wo, "gatet": gt,
            "upt": ut, "downt": dt_, "fscale": fs,
        })
    return in_maps


_NC_CACHE = {}


def kernel(L0_attn_hidden_states, L0_attn_in_proj_w, L0_attn_conv_w,
           L0_attn_conv_b, L0_attn_x_proj_w, L0_attn_dt_proj_w,
           L0_attn_dt_proj_b, L0_attn_A_log, L0_attn_D_param,
           L0_attn_out_proj_w, L0_ffn_gate_w, L0_ffn_up_w, L0_ffn_down_w,
           L0_ffn_gate_s, L0_ffn_up_s, L0_ffn_down_s):
    c = make_cfg(mini=False)
    inp = dict(
        L0_attn_hidden_states=L0_attn_hidden_states,
        L0_attn_in_proj_w=L0_attn_in_proj_w, L0_attn_conv_w=L0_attn_conv_w,
        L0_attn_conv_b=L0_attn_conv_b, L0_attn_x_proj_w=L0_attn_x_proj_w,
        L0_attn_dt_proj_w=L0_attn_dt_proj_w,
        L0_attn_dt_proj_b=L0_attn_dt_proj_b, L0_attn_A_log=L0_attn_A_log,
        L0_attn_D_param=L0_attn_D_param,
        L0_attn_out_proj_w=L0_attn_out_proj_w, L0_ffn_gate_w=L0_ffn_gate_w,
        L0_ffn_up_w=L0_ffn_up_w, L0_ffn_down_w=L0_ffn_down_w,
        L0_ffn_gate_s=L0_ffn_gate_s, L0_ffn_up_s=L0_ffn_up_s,
        L0_ffn_down_s=L0_ffn_down_s)
    nonce = int(np.random.randint(1, 1 << 30))
    in_maps = stage_inputs(c, inp, nonce)
    if "full" not in _NC_CACHE:
        _NC_CACHE["full"] = build_program(c)
    nc = _NC_CACHE["full"]
    res = run_bass_kernel_spmd(nc, in_maps, core_ids=list(range(N_CORES)))
    TLH, D = c["TLH"], c["D"]
    out = np.empty((N_CORES * TLH, D), np.float32)
    for core in range(N_CORES):
        out[core * TLH:(core + 1) * TLH] = res.results[core]["out_r"]
    return out.reshape(1, N_CORES * TLH, D)


# version 05
# speedup vs baseline: 1.0130x; 1.0130x over previous
import contextlib
import numpy as np
import ml_dtypes

import concourse.bass as bass
import concourse.mybir as mybir
from concourse.alu_op_type import AluOpType
from concourse.bass_utils import run_bass_kernel_spmd
from concourse.libnrt import physical_to_logical_core_id as _p2l

F32 = mybir.dt.float32
BF16 = mybir.dt.bfloat16
I32 = mybir.dt.int32
AF = mybir.ActivationFunctionType
OP = AluOpType

N_CORES = 8
NPAIR = 4


def make_cfg(mini=False, **over):
    if mini:
        c = dict(F=2048, D=256, DI=512, NST=16, INTER=512, BLK=512, ST=256)
    else:
        c = dict(F=16384, D=2048, DI=4096, NST=16, INTER=5632, BLK=1024,
                 ST=512)
    c.update(over)
    c["DH"] = c["DI"] // 2
    c["NCC"] = c["DH"] // 128
    c["NKC"] = c["D"] // 128
    c["NIC"] = c["INTER"] // 128
    c["NHC"] = c["D"] // 128
    c["NBLK"] = c["F"] // c["BLK"]
    c["TL"] = c["F"] // NPAIR
    c["TAIL_BLK"] = c["TL"] // c["BLK"]
    c["TLH"] = c["TL"] // 2
    c["NSUB"] = c["BLK"] // c["ST"]
    c["EPS"] = 1e-5
    return c


class Sem:
    def __init__(self, stack, nc, name):
        self.h = stack.enter_context(nc.semaphore(name))
        self.v = 0

    def inc(self, inst, k=1):
        inst.then_inc(self.h, k)
        self.v += k
        return self.v


def build_program(c, debug=False):
    nc = bass.Bass("TRN2", target_bir_lowering=False)
    F, D, DI, DH = c["F"], c["D"], c["DI"], c["DH"]
    NCC, NKC, NIC, NHC = c["NCC"], c["NKC"], c["NIC"], c["NHC"]
    NST = c["NST"]
    BLK, ST, NSUB = c["BLK"], c["ST"], c["NSUB"]
    NBLK, TL, TAIL_BLK, TLH = c["NBLK"], c["TL"], c["TAIL_BLK"], c["TLH"]
    TB0 = NBLK - TAIL_BLK
    NROW = BLK // 128
    NT0 = NBLK * NROW

    din = lambda n, s, d=F32: nc.dram_tensor(n, s, d, kind="ExternalInput").ap()
    x_f = din("x_f", [F, D])
    nonce_i = din("nonce", [1, 1], I32)
    winu = din("winu", [NCC, 128, D])
    wing = din("wing", [NCC, 128, D])
    xpt = din("xpt", [128, NCC * 160])
    dtwt = din("dtwt", [128, DH])
    convd = din("convd", [128, NCC * 512])
    vecs = din("vecs", [128, 3 * NCC])
    alog = din("alog", [128, NCC * NST])
    maskc = din("maskc", [128, NBLK])
    selm = din("selm", [32, 2 * NST * 128], BF16)
    idnb_i = din("idnb", [128, 128], BF16)
    idnf_i = din("idnf", [128, 128])
    woutt = din("woutt", [NHC, 128, 2 * NCC * 128])
    gatet = din("gatet", [NIC, 128, NKC * 128])
    upt = din("upt", [NIC, 128, NKC * 128])
    downt = din("downt", [NHC, 128, NIC * 128])
    fscale = din("fscale", [128, 2 * NIC + NHC])
    out_r = nc.dram_tensor("out_r", [TLH, D], F32, kind="ExternalOutput").ap()

    xnt_d = nc.dram_tensor("xnt_d", [NBLK, NKC, 128, BLK], BF16)
    upre_d = nc.dram_tensor("upre_d", [NCC, 128, F], BF16)
    u_d = nc.dram_tensor("u_d", [NCC, 128, F], BF16)
    dt_d = nc.dram_tensor("dt_d", [NCC, 128, F], BF16)
    ts_d = nc.dram_tensor("ts_d", [128, F], BF16)
    bc_d = nc.dram_tensor("bc_d", [32, F], BF16)
    sg_d = nc.dram_tensor("sg_d", [NCC, 128, TL], BF16)
    s_d = nc.dram_tensor("s_d", [NHC, 128, TLH], F32)
    sgf_d = nc.dram_tensor("sgf_d", [NHC, 128, TLH], F32)

    sh_ts = nc.dram_tensor("sh_ts", [2, 128, F], F32, addr_space="Shared")
    sh_bc = nc.dram_tensor("sh_bc", [2, 32, F], F32, addr_space="Shared")
    sh_yg = nc.dram_tensor("sh_yg", [2, NCC, 128, TL], BF16, addr_space="Shared")
    sh_fl = nc.dram_tensor("sh_fl", [2, 8], I32, addr_space="Shared")

    st_ = contextlib.ExitStack()
    sbuf = lambda n, s, d: st_.enter_context(nc.sbuf_tensor(n, s, d))

    nonce_sb = sbuf("nonce_sb", [1, 1], I32)
    params_sb = sbuf("params_sb", [1, 2], I32)
    a_sb = sbuf("a_sb", [128, NCC * NST], F32)
    convb_sb = sbuf("convb_sb", [128, NCC], F32)
    dtb_sb = sbuf("dtb_sb", [128, NCC], F32)
    dpar_sb = sbuf("dpar_sb", [128, NCC], F32)
    mask_sb = sbuf("mask_sb", [128, NBLK], F32)
    fsc_sb = sbuf("fsc_sb", [128, 2 * NIC + NHC], F32)
    sel_sb = sbuf("sel_sb", [32, 2 * NST * 128], BF16)
    state_sb = sbuf("state_sb", [128, NCC * NST], F32)
    idnb = sbuf("idnb_s", [128, 128], BF16)
    idnf = sbuf("idnf_s", [128, 128], F32)
    lnsc = sbuf("lnsc", [128, 64], F32)
    lndbg = sbuf("lndbg", [128, 512], F32)

    SM = {k: Sem(st_, nc, k) for k in
          ["ld", "lw", "wc", "mmA", "mmB", "evA", "evB", "evC",
           "dvA", "dvB", "stq", "stu", "bar", "hs", "lg", "gq", "mmY"]}
    hs = SM["hs"]
    lg = SM["lg"]
    ld, lw, wc = SM["ld"], SM["lw"], SM["wc"]
    mmA, mmB = SM["mmA"], SM["mmB"]
    evA, evB, evC = SM["evA"], SM["evB"], SM["evC"]
    dvA, dvB = SM["dvA"], SM["dvB"]
    stq, stu, bar = SM["stq"], SM["stu"], SM["bar"]
    gq, mmY = SM["gq"], SM["mmY"]

    _ns_stack = []
    block = st_.enter_context(nc.Block())
    st_.enter_context(nc.allow_low_precision(
        reason="bf16 matmuls / bf16 intermediates sanctioned by tolerance"))

    r_slot = st_.enter_context(nc.sync.register("r_slot"))
    rg_slot = st_.enter_context(nc.gpsimd.register("rg_slot"))
    r_non = st_.enter_context(nc.sync.register("r_non"))
    r_flag = st_.enter_context(nc.sync.register("r_flag"))
    r_cont = st_.enter_context(nc.sync.register("r_cont"))
    r_poll = st_.enter_context(nc.sync.register("r_poll"))

    ALL_ENG = (block.sync, block.tensor, block.scalar, block.vector,
               block.gpsimd)

    def barrier():
        for emit in ALL_ENG:
            @emit
            def _(e):
                e.nop().then_inc(bar.h, 1)
        bar.v += len(ALL_ENG)
        tgt = bar.v
        for emit in ALL_ENG:
            @emit
            def _(e):
                e.wait_ge(bar.h, tgt)

    def flag_write(sync, idx, off):
        sync.reg_alu(r_flag, sync.snap(r_non), off, OP.add)
        s = slot_sv(sync, r_slot)
        with sync.If(s == 0):
            sync.store(sh_fl[0:1, idx:idx + 1], r_flag)
        with sync.Else():
            sync.store(sh_fl[1:2, idx:idx + 1], r_flag)

    def flag_poll(sync, idx, off):
        sync.reg_alu(r_flag, sync.snap(r_non), off, OP.add)
        s = slot_sv(sync, r_slot)
        with sync.If(s == 0):
            sync.reg_mov(r_cont, 1)
            with sync.While(r_cont):
                sync.load(r_poll, sh_fl[1:2, idx:idx + 1])
                sync.reg_alu(r_cont, sync.snap(r_poll), sync.snap(r_flag),
                             OP.not_equal)
        with sync.Else():
            sync.reg_mov(r_cont, 1)
            with sync.While(r_cont):
                sync.load(r_poll, sh_fl[0:1, idx:idx + 1])
                sync.reg_alu(r_cont, sync.snap(r_poll), sync.snap(r_flag),
                             OP.not_equal)

    SLOT_SV = {}

    def slot_sv(eng, reg):
        key = id(eng)
        if key not in SLOT_SV:
            SLOT_SV[key] = eng.snap(reg, min_val=0, max_val=1)
        return SLOT_SV[key]

    def slot_dma_out(eng, dst0, dst1, src, sem, reg=None):
        s = slot_sv(eng, reg if reg is not None else r_slot)
        with eng.If(s == 0):
            eng.dma_start(out=dst0, in_=src).then_inc(sem.h, 16)
        with eng.Else():
            eng.dma_start(out=dst1, in_=src).then_inc(sem.h, 16)
        sem.v += 16

    def slot_dma_in(eng, dst, src0, src1, sem, reg=None):
        s = slot_sv(eng, reg if reg is not None else r_slot)
        with eng.If(s == 0):
            eng.dma_start(out=dst, in_=src0).then_inc(sem.h, 16)
        with eng.Else():
            eng.dma_start(out=dst, in_=src1).then_inc(sem.h, 16)
        sem.v += 16

    def gstore(g, dst, src, sem):
        g.dma_start(out=dst, in_=src).then_inc(sem.h, 16)
        sem.v += 16

    @block.sync
    def _(sync):
        for arm in sync.switch_core_id(n=128):
            sync.reg_mov(r_slot,
                         _p2l(int(arm) % 8, 0, container_num_cores=8) % 2)
        for dst, src in [
                (nonce_sb[:, :], nonce_i), (a_sb[:, :], alog),
                (convb_sb[:, :], vecs[:, 0:NCC]),
                (dtb_sb[:, :], vecs[:, NCC:2 * NCC]),
                (dpar_sb[:, :], vecs[:, 2 * NCC:3 * NCC]),
                (mask_sb[:, :], maskc), (sel_sb[:, :], selm),
                (idnb[:, :], idnb_i), (idnf[:, :], idnf_i),
                (fsc_sb[:, :], fscale)]:
            sync.dma_start(out=dst, in_=src).then_inc(ld.h, 16)
        ld.v += 160
        sync.wait_ge(ld.h, ld.v)
        sync.load(r_non, nonce_sb[0:1, 0:1])
        sync.store(params_sb[0:1, 0:1], r_slot)
        sync.nop().then_inc(hs.h, 1)
    hs.v += 1

    @block.gpsimd
    def _(g):
        g.wait_ge(hs.h, hs.v)
        g.load(rg_slot, params_sb[0:1, 0:1])

    @block.scalar
    def _(sc):
        sc.wait_ge(hs.h, hs.v)
        i = sc.activation(a_sb[:, :], a_sb[:, :], AF.Exp, scale=1.0)
        evA.inc(i)

    @block.vector
    def _(v):
        v.wait_ge(evA.h, evA.v)
        v.tensor_scalar(a_sb[:, :], a_sb[:, :], -1.0, None, OP.mult)
        i = v.memset(state_sb[:, :], 0.0)
        dvA.inc(i)
    barrier()

    _ns_stack.append(("phA", nc.enter_named_scope("phA", False)[0]))
    with contextlib.ExitStack() as ph:
        sba = lambda n, s, d: ph.enter_context(nc.sbuf_tensor(n, s, d))
        XR = sba("XR", [128, 2 * D], F32)
        XNB = sba("XNB", [128, 2 * D], BF16)
        STG = [sba("STG0", [128, NKC * BLK], BF16),
               sba("STG1", [128, NKC * BLK], BF16)]
        pt_bf = [ph.enter_context(nc.psum_tensor("pt_bf0", [128, 512], BF16)),
                 ph.enter_context(nc.psum_tensor("pt_bf1", [128, 512], BF16))]
        xr = [XR[:, 0:D], XR[:, D:2 * D]]
        xnb = [XNB[:, 0:D], XNB[:, D:2 * D]]
        nstat = max(1, D // 512)
        kpg = min(4, NKC)
        ngrp = NKC // kpg

        ldA, dvA_A, evA_A = ld.v, dvA.v, evA.v
        mmA_A, evB_A, stqA = mmA.v, evB.v, stq.v

        @block.sync
        def _(sync):
            for it in range(NT0):
                sync.wait_ge(dvA.h, max(dvA_A, dvA_A + 2 * (it - 1)))
                if it >= 1:
                    sync.wait_ge(ld.h, ldA + it * 16)
                sync.dma_start(out=xr[it % 2],
                               in_=x_f[it * 128:(it + 1) * 128, :]
                               ).then_inc(ld.h, 16)
            ld.v += NT0 * 16

        @block.vector
        def _(v):
            for it in range(NT0):
                sl = lnsc[:, (it % 2) * 8:(it % 2) * 8 + 8]
                stt = lnsc[:, 16 + (it % 2) * 24:16 + (it % 2) * 24 + 6 * nstat]
                v.wait_ge(ld.h, ldA + min(NT0, it + 2) * 16)
                if it >= 2:
                    v.wait_ge(evA.h, evA_A + (it - 1))
                for s in range(nstat):
                    w = min(512, D)
                    i = v.bn_stats(stt[:, s * 6:(s + 1) * 6],
                                   xr[it % 2][:, s * w:(s + 1) * w])
                dvB.inc(i)
                v.wait_ge(dvB.h, dvB.v)
                i = v.bn_aggr(sl[:, 0:2], stt[:, 0:6 * nstat])
                dvB.inc(i)
                v.wait_ge(dvB.h, dvB.v)
                i = v.tensor_scalar(sl[:, 2:3], sl[:, 1:2], 1.0,
                                    float(c["EPS"]), OP.mult, OP.add)
                dvA.inc(i)
                v.wait_ge(evA.h, evA_A + it + 1)
                if it >= 2:
                    v.wait_ge(mmA.h, mmA_A + (it - 1) * ngrp)
                i = v.tensor_scalar(xnb[it % 2], xr[it % 2], sl[:, 0:1],
                                    sl[:, 3:4], OP.subtract, OP.mult)
                dvA.inc(i)

        def emit_evacs(sc, it):
            b, r = it // NROW, it % NROW
            if r == 0 and b >= 2:
                sc.wait_ge(stq.h, stqA + (b - 1) * NKC * 16)
            for g in range(ngrp):
                e = it * ngrp + g
                sc.wait_ge(mmA.h, mmA_A + e + 1)
                for j in range(kpg):
                    kc = g * kpg + j
                    i = sc.activation(
                        STG[b % 2][:, kc * BLK + r * 128:kc * BLK + (r + 1) * 128],
                        pt_bf[e % 2][:, j * 128:(j + 1) * 128], AF.Copy,
                        scale=1.0)
                evB.inc(i)

        @block.scalar
        def _(sc):
            for it in range(NT0):
                sl = lnsc[:, (it % 2) * 8:(it % 2) * 8 + 8]
                sc.wait_ge(dvA.h, dvA_A + 2 * it + 1)
                sc.activation(sl[:, 5:6], sl[:, 2:3], AF.Ln, scale=1.0)
                i = sc.activation(sl[:, 3:4], sl[:, 5:6], AF.Exp, scale=-0.5)
                evA.inc(i)
                if it >= 1:
                    emit_evacs(sc, it - 1)
            emit_evacs(sc, NT0 - 1)

        @block.tensor
        def _(t):
            for it in range(NT0):
                t.wait_ge(dvA.h, dvA_A + 2 * it + 2)
                for g in range(ngrp):
                    e = it * ngrp + g
                    if e >= 2:
                        t.wait_ge(evB.h, evB_A + e - 1)
                    for j in range(kpg):
                        kc = g * kpg + j
                        i = t.transpose(pt_bf[e % 2][:, j * 128:(j + 1) * 128],
                                        xnb[it % 2][:, kc * 128:(kc + 1) * 128],
                                        idnb[:, :])
                    mmA.inc(i)

        @block.gpsimd
        def _(sync):
            for b in range(NBLK):
                sync.wait_ge(evB.h, evB_A + (b + 1) * NROW * ngrp)
                if b >= 1:
                    sync.wait_ge(stq.h, stqA + b * NKC * 16)
                for kc in range(NKC):
                    sync.dma_start(out=xnt_d[b, kc],
                                   in_=STG[b % 2][:, kc * BLK:(kc + 1) * BLK]
                                   ).then_inc(stq.h, 16)
            stq.v += NBLK * NKC * 16
            sync.wait_ge(stq.h, stq.v)
        barrier()

    _ns_name, _ns_id = _ns_stack.pop(); nc.leave_named_scope(_ns_name, _ns_id, False)
    _ns_stack.append(("phB", nc.enter_named_scope("phB", False)[0]))
    with contextlib.ExitStack() as ph:
        sba = lambda n, s, d: ph.enter_context(nc.sbuf_tensor(n, s, d))
        WB = sba("WB", [128, NCC * D], BF16)
        WL = [sba("WL0", [128, D], F32), sba("WL1", [128, D], F32)]
        XNT = [sba("XNT0", [128, NKC * BLK], BF16),
               sba("XNT1", [128, NKC * BLK], BF16)]
        UPS = sba("UPS", [128, 4 * ST], BF16)
        p_a = [ph.enter_context(nc.psum_tensor("pb_a0", [128, 512], F32)),
               ph.enter_context(nc.psum_tensor("pb_a1", [128, 512], F32))]
        lwB, wcB = lw.v, wc.v

        @block.sync
        def _(sync):
            for pc in range(NCC):
                sync.wait_ge(wc.h, max(wcB, wcB + pc - 1))
                if pc >= 1:
                    sync.wait_ge(lw.h, lwB + pc * 16)
                sync.dma_start(out=WL[pc % 2][:, :], in_=winu[pc]
                               ).then_inc(lw.h, 16)
            lw.v += NCC * 16

        @block.scalar
        def _(sc):
            for pc in range(NCC):
                sc.wait_ge(lw.h, lwB + (pc + 1) * 16)
                i = sc.activation(WB[:, pc * D:(pc + 1) * D], WL[pc % 2][:, :],
                                  AF.Copy, scale=1.0)
                wc.inc(i)

        ldB, mmB_B, evC_B, stqB = ld.v, mmA.v, evC.v, stq.v

        @block.sync
        def _(sync):
            for b in range(NBLK):
                sync.wait_ge(mmA.h, max(mmB_B, mmB_B + (b - 1) * NCC * NSUB))
                if b >= 1:
                    sync.wait_ge(ld.h, ldB + b * NKC * 16)
                for kc in range(NKC):
                    sync.dma_start(out=XNT[b % 2][:, kc * BLK:(kc + 1) * BLK],
                                   in_=xnt_d[b, kc]).then_inc(ld.h, 16)
            ld.v += NBLK * NKC * 16

        @block.tensor
        def _(t):
            t.wait_ge(wc.h, wcB + NCC)
            for b in range(NBLK):
                t.wait_ge(ld.h, ldB + (b + 1) * NKC * 16)
                for pc in range(NCC):
                    for sub in range(NSUB):
                        nmm = (b * NCC + pc) * NSUB + sub
                        if nmm >= 2:
                            t.wait_ge(evC.h, evC_B + nmm - 1)
                        for kc in range(NKC):
                            i = t.matmul(
                                p_a[nmm % 2][:, 0:ST],
                                WB[:, pc * D + kc * 128:pc * D + (kc + 1) * 128],
                                XNT[b % 2][:, kc * BLK + sub * ST:kc * BLK + (sub + 1) * ST],
                                start=(kc == 0), stop=(kc == NKC - 1))
                        mmA.inc(i)

        @block.scalar
        def _(sc):
            for nmm in range(NBLK * NCC * NSUB):
                sc.wait_ge(mmA.h, mmB_B + nmm + 1)
                if nmm >= 4:
                    sc.wait_ge(stq.h, stqB + (nmm - 3) * 16)
                i = sc.activation(UPS[:, (nmm % 4) * ST:(nmm % 4 + 1) * ST],
                                  p_a[nmm % 2][:, 0:ST], AF.Copy, scale=1.0)
                evC.inc(i)

        @block.gpsimd
        def _(sync):
            for b in range(NBLK):
                for pc in range(NCC):
                    for sub in range(NSUB):
                        nmm = (b * NCC + pc) * NSUB + sub
                        sync.wait_ge(evC.h, evC_B + nmm + 1)
                        if nmm >= 1:
                            sync.wait_ge(stq.h, stqB + nmm * 16)
                        sync.dma_start(
                            out=upre_d[pc][:, b * BLK + sub * ST:b * BLK + (sub + 1) * ST],
                            in_=UPS[:, (nmm % 4) * ST:(nmm % 4 + 1) * ST]
                            ).then_inc(stq.h, 16)
            stq.v += NBLK * NCC * NSUB * 16
            sync.wait_ge(stq.h, stq.v)
        barrier()

    _ns_name, _ns_id = _ns_stack.pop(); nc.leave_named_scope(_ns_name, _ns_id, False)
    _ns_stack.append(("phC", nc.enter_named_scope("phC", False)[0]))
    with contextlib.ExitStack() as ph:
        sba = lambda n, s, d: ph.enter_context(nc.sbuf_tensor(n, s, d))
        HB = BLK + 3
        XPB = sba("XPB", [128, NCC * 160], BF16)
        CVB = sba("CVB", [128, NCC * 512], BF16)
        WLx = sba("WLx", [128, NCC * 512], F32)
        UPT = [sba("UPT0", [128, NCC * (HB + 1)], BF16),
               sba("UPT1", [128, NCC * (HB + 1)], BF16)]
        UT = sba("UT", [128, 4 * ST], BF16)
        TSE = sba("TSE", [128, 2 * ST], F32)
        BCE = sba("BCE", [32, 2 * ST], F32)
        p_a = [ph.enter_context(nc.psum_tensor("pc_a0", [128, 512], F32)),
               ph.enter_context(nc.psum_tensor("pc_a1", [128, 512], F32))]
        p_ts = [ph.enter_context(nc.psum_tensor("pc_t0", [128, 512], F32)),
                ph.enter_context(nc.psum_tensor("pc_t1", [128, 512], F32))]
        p_bc = [ph.enter_context(nc.psum_tensor("pc_b0", [32, 512], F32)),
                ph.enter_context(nc.psum_tensor("pc_b1", [32, 512], F32))]
        lwC, wcC = lw.v, wc.v

        @block.sync
        def _(sync):
            sync.dma_start(out=WLx[:, 0:NCC * 160], in_=xpt).then_inc(lw.h, 16)
            sync.wait_ge(wc.h, wcC + 1)
            sync.wait_ge(lw.h, lwC + 16)
            sync.dma_start(out=WLx[:, 0:NCC * 512], in_=convd).then_inc(lw.h, 16)
            lw.v += 32

        @block.scalar
        def _(sc):
            sc.wait_ge(lw.h, lwC + 16)
            i = sc.activation(XPB[:, :], WLx[:, 0:NCC * 160], AF.Copy, scale=1.0)
            wc.inc(i)
            sc.wait_ge(lw.h, lwC + 32)
            i = sc.activation(CVB[:, :], WLx[:, 0:NCC * 512], AF.Copy, scale=1.0)
            wc.inc(i)

        ldC, dvC = ld.v, dvA.v
        mmC_C, mmX_C, evU_C, evT_C = mmA.v, mmB.v, evA.v, evB.v
        stqC, stuC = stq.v, stu.v

        @block.vector
        def _(v):
            v.memset(UPT[0][:, :], 0.0)
            i = v.memset(UPT[1][:, :], 0.0)
            dvA.inc(i)

        @block.sync
        def _(sync):
            sync.wait_ge(dvA.h, dvC + 1)
            for b in range(NBLK):
                sync.wait_ge(mmA.h, max(mmC_C, mmC_C + (b - 1) * NSUB * NCC))
                if b >= 1:
                    sync.wait_ge(ld.h, ldC + b * NCC * 16)
                for cc in range(NCC):
                    base = cc * (HB + 1)
                    if b == 0:
                        sync.dma_start(out=UPT[0][:, base + 3:base + 3 + BLK],
                                       in_=upre_d[cc][:, 0:BLK]).then_inc(ld.h, 16)
                    else:
                        sync.dma_start(out=UPT[b % 2][:, base:base + HB],
                                       in_=upre_d[cc][:, b * BLK - 3:b * BLK + BLK]
                                       ).then_inc(ld.h, 16)
            ld.v += NBLK * NCC * 16

        @block.tensor
        def _(t):
            t.wait_ge(wc.h, wcC + 2)
            for b in range(NBLK):
                t.wait_ge(ld.h, ldC + (b + 1) * NCC * 16)
                for sub in range(NSUB):
                    nbs = b * NSUB + sub
                    for cc in range(NCC):
                        ncv = nbs * NCC + cc
                        if ncv >= 2:
                            t.wait_ge(evA.h, evU_C + ncv - 1)
                        base = cc * (HB + 1)
                        for j in range(4):
                            i = t.matmul(
                                p_a[ncv % 2][:, 0:ST],
                                CVB[:, cc * 512 + j * 128:cc * 512 + (j + 1) * 128],
                                UPT[b % 2][:, base + sub * ST + j:base + sub * ST + j + ST],
                                start=(j == 0), stop=(j == 3))
                        mmA.inc(i)
                        t.wait_ge(evA.h, evU_C + ncv + 1)
                        if cc == 0 and nbs >= 2:
                            t.wait_ge(evB.h, evT_C + nbs - 1)
                        i = t.matmul(p_ts[sub][:, 0:ST],
                                     XPB[:, cc * 160:cc * 160 + 128],
                                     UT[:, (ncv % 4) * ST:(ncv % 4 + 1) * ST],
                                     start=(cc == 0), stop=(cc == NCC - 1))
                        i = t.matmul(p_bc[sub][:, 0:ST],
                                     XPB[:, cc * 160 + 128:cc * 160 + 160],
                                     UT[:, (ncv % 4) * ST:(ncv % 4 + 1) * ST],
                                     start=(cc == 0), stop=(cc == NCC - 1))
                        mmB.inc(i)

        @block.scalar
        def _(sc):
            for b in range(NBLK):
                for sub in range(NSUB):
                    nbs = b * NSUB + sub
                    for cc in range(NCC):
                        ncv = nbs * NCC + cc
                        sc.wait_ge(mmA.h, mmC_C + ncv + 1)
                        if ncv >= 4:
                            sc.wait_ge(stu.h, stuC + (ncv - 3) * 16)
                        i = sc.activation(UT[:, (ncv % 4) * ST:(ncv % 4 + 1) * ST],
                                          p_a[ncv % 2][:, 0:ST], AF.Silu,
                                          bias=convb_sb[:, cc:cc + 1], scale=1.0)
                        evA.inc(i)
                    sc.wait_ge(mmB.h, mmX_C + (nbs + 1) * NCC)
                    if nbs >= 2:
                        sc.wait_ge(stq.h, stqC + (nbs - 1) * 32)
                    i = sc.activation(TSE[:, (nbs % 2) * ST:(nbs % 2 + 1) * ST],
                                      p_ts[sub][:, 0:ST], AF.Copy, scale=1.0)
                    i = sc.activation(BCE[:, (nbs % 2) * ST:(nbs % 2 + 1) * ST],
                                      p_bc[sub][:, 0:ST], AF.Copy, scale=1.0)
                    evB.inc(i)

        @block.gpsimd
        def _(sync):
            for b in range(NBLK):
                for sub in range(NSUB):
                    nbs = b * NSUB + sub
                    for cc in range(NCC):
                        ncv = nbs * NCC + cc
                        sync.wait_ge(evA.h, evU_C + ncv + 1)
                        if ncv >= 1:
                            sync.wait_ge(stu.h, stuC + ncv * 16)
                        sync.dma_start(
                            out=u_d[cc][:, b * BLK + sub * ST:b * BLK + (sub + 1) * ST],
                            in_=UT[:, (ncv % 4) * ST:(ncv % 4 + 1) * ST]
                            ).then_inc(stu.h, 16)
                    t0 = b * BLK + sub * ST
                    sync.wait_ge(evB.h, evT_C + nbs + 1)
                    if nbs >= 1:
                        sync.wait_ge(stq.h, stqC + nbs * 32)
                    slot_dma_out(sync, sh_ts[0][:, t0:t0 + ST],
                                 sh_ts[1][:, t0:t0 + ST],
                                 TSE[:, (nbs % 2) * ST:(nbs % 2 + 1) * ST], stq,
                                 reg=rg_slot)
                    slot_dma_out(sync, sh_bc[0][:, t0:t0 + ST],
                                 sh_bc[1][:, t0:t0 + ST],
                                 BCE[:, (nbs % 2) * ST:(nbs % 2 + 1) * ST], stq,
                                 reg=rg_slot)
            stu.v += NBLK * NSUB * NCC * 16
            sync.wait_ge(stu.h, stu.v)
            sync.wait_ge(stq.h, stq.v)

        @block.sync
        def _(sync):
            sync.wait_ge(stu.h, stu.v)
            sync.wait_ge(stq.h, stq.v)
            flag_write(sync, 0, 1)
        barrier()

    _ns_name, _ns_id = _ns_stack.pop(); nc.leave_named_scope(_ns_name, _ns_id, False)
    _ns_stack.append(("phD", nc.enter_named_scope("phD", False)[0]))
    with contextlib.ExitStack() as ph:
        sba = lambda n, s, d: ph.enter_context(nc.sbuf_tensor(n, s, d))
        WB = sba("WBd", [128, NCC * D], BF16)
        WL = [sba("WLd0", [128, D], F32), sba("WLd1", [128, D], F32)]
        XNT = [sba("XNTd0", [128, NKC * BLK], BF16),
               sba("XNTd1", [128, NKC * BLK], BF16)]
        SGT = sba("SGT", [128, 4 * ST], BF16)
        p_a = [ph.enter_context(nc.psum_tensor("pd_a0", [128, 512], F32)),
               ph.enter_context(nc.psum_tensor("pd_a1", [128, 512], F32))]
        lwD, wcD = lw.v, wc.v

        @block.sync
        def _(sync):
            for pc in range(NCC):
                sync.wait_ge(wc.h, max(wcD, wcD + pc - 1))
                if pc >= 1:
                    sync.wait_ge(lw.h, lwD + pc * 16)
                sync.dma_start(out=WL[pc % 2][:, :], in_=wing[pc]
                               ).then_inc(lw.h, 16)
            lw.v += NCC * 16

        @block.scalar
        def _(sc):
            for pc in range(NCC):
                sc.wait_ge(lw.h, lwD + (pc + 1) * 16)
                i = sc.activation(WB[:, pc * D:(pc + 1) * D], WL[pc % 2][:, :],
                                  AF.Copy, scale=1.0)
                wc.inc(i)

        ldD, mmD, evD, stqD = ld.v, mmA.v, evA.v, stq.v

        @block.sync
        def _(sync):
            for bi in range(TAIL_BLK):
                sync.wait_ge(mmA.h, max(mmD, mmD + (bi - 1) * NCC * NSUB))
                if bi >= 1:
                    sync.wait_ge(ld.h, ldD + bi * NKC * 16)
                for kc in range(NKC):
                    sync.dma_start(out=XNT[bi % 2][:, kc * BLK:(kc + 1) * BLK],
                                   in_=xnt_d[TB0 + bi, kc]).then_inc(ld.h, 16)
            ld.v += TAIL_BLK * NKC * 16

        @block.tensor
        def _(t):
            t.wait_ge(wc.h, wcD + NCC)
            for bi in range(TAIL_BLK):
                t.wait_ge(ld.h, ldD + (bi + 1) * NKC * 16)
                for pc in range(NCC):
                    for sub in range(NSUB):
                        nmm = (bi * NCC + pc) * NSUB + sub
                        if nmm >= 2:
                            t.wait_ge(evA.h, evD + nmm - 1)
                        for kc in range(NKC):
                            i = t.matmul(
                                p_a[nmm % 2][:, 0:ST],
                                WB[:, pc * D + kc * 128:pc * D + (kc + 1) * 128],
                                XNT[bi % 2][:, kc * BLK + sub * ST:kc * BLK + (sub + 1) * ST],
                                start=(kc == 0), stop=(kc == NKC - 1))
                        mmA.inc(i)

        @block.scalar
        def _(sc):
            for nmm in range(TAIL_BLK * NCC * NSUB):
                sc.wait_ge(mmA.h, mmD + nmm + 1)
                if nmm >= 4:
                    sc.wait_ge(stq.h, stqD + (nmm - 3) * 16)
                i = sc.activation(SGT[:, (nmm % 4) * ST:(nmm % 4 + 1) * ST],
                                  p_a[nmm % 2][:, 0:ST], AF.Silu, scale=1.0)
                evA.inc(i)

        @block.gpsimd
        def _(sync):
            for bi in range(TAIL_BLK):
                for pc in range(NCC):
                    for sub in range(NSUB):
                        nmm = (bi * NCC + pc) * NSUB + sub
                        sync.wait_ge(evA.h, evD + nmm + 1)
                        if nmm >= 1:
                            sync.wait_ge(stq.h, stqD + nmm * 16)
                        t0 = bi * BLK + sub * ST
                        sync.dma_start(out=sg_d[pc][:, t0:t0 + ST],
                                       in_=SGT[:, (nmm % 4) * ST:(nmm % 4 + 1) * ST]
                                       ).then_inc(stq.h, 16)
            stq.v += TAIL_BLK * NCC * NSUB * 16
            sync.wait_ge(stq.h, stq.v)
        barrier()

    _ns_name, _ns_id = _ns_stack.pop(); nc.leave_named_scope(_ns_name, _ns_id, False)
    _ns_stack.append(("phE", nc.enter_named_scope("phE", False)[0]))
    with contextlib.ExitStack() as ph:
        sba = lambda n, s, d: ph.enter_context(nc.sbuf_tensor(n, s, d))
        TSA = sba("TSA", [128, 2 * BLK], F32)
        TSS = sba("TSS", [128, BLK], F32)
        TSB = sba("TSBe", [128, BLK], BF16)
        BCA = sba("BCA", [32, 2 * BLK], F32)
        BCS = sba("BCS", [32, BLK], F32)
        BCB = sba("BCBe", [32, BLK], BF16)
        ldE, dvE, evE, stqE = ld.v, dvA.v, evA.v, stq.v

        @block.sync
        def _(sync):
            flag_poll(sync, 0, 1)
            for b in range(NBLK):
                sync.wait_ge(dvA.h, dvE + 2 * b)
                if b >= 1:
                    sync.wait_ge(ld.h, ldE + b * 64)
                t0 = b * BLK
                sync.dma_start(out=TSA[:, 0:BLK],
                               in_=sh_ts[0][:, t0:t0 + BLK]).then_inc(ld.h, 16)
                sync.dma_start(out=TSA[:, BLK:2 * BLK],
                               in_=sh_ts[1][:, t0:t0 + BLK]).then_inc(ld.h, 16)
                sync.dma_start(out=BCA[:, 0:BLK],
                               in_=sh_bc[0][:, t0:t0 + BLK]).then_inc(ld.h, 16)
                sync.dma_start(out=BCA[:, BLK:2 * BLK],
                               in_=sh_bc[1][:, t0:t0 + BLK]).then_inc(ld.h, 16)
            ld.v += NBLK * 64

        @block.vector
        def _(v):
            for b in range(NBLK):
                v.wait_ge(ld.h, ldE + (b + 1) * 64)
                if b >= 1:
                    v.wait_ge(evA.h, evE + 2 * b)
                v.tensor_tensor(TSS[:, :], TSA[:, 0:BLK], TSA[:, BLK:2 * BLK],
                                OP.add)
                i = v.tensor_tensor(BCS[:, :], BCA[:, 0:BLK],
                                    BCA[:, BLK:2 * BLK], OP.add)
                dvA.inc(i, 2)

        @block.scalar
        def _(sc):
            for b in range(NBLK):
                sc.wait_ge(dvA.h, dvE + 2 * (b + 1))
                if b >= 2:
                    sc.wait_ge(stq.h, stqE + (b - 1) * 32)
                i = sc.activation(TSB[:, :], TSS[:, :], AF.Copy, scale=1.0)
                i = sc.activation(BCB[:, :], BCS[:, :], AF.Copy, scale=1.0)
                evA.inc(i, 2)

        @block.gpsimd
        def _(sync):
            for b in range(NBLK):
                sync.wait_ge(evA.h, evE + 2 * (b + 1))
                if b >= 1:
                    sync.wait_ge(stq.h, stqE + b * 32)
                t0 = b * BLK
                sync.dma_start(out=ts_d[:, t0:t0 + BLK], in_=TSB[:, :]
                               ).then_inc(stq.h, 16)
                sync.dma_start(out=bc_d[:, t0:t0 + BLK], in_=BCB[:, :]
                               ).then_inc(stq.h, 16)
            stq.v += NBLK * 32
            sync.wait_ge(stq.h, stq.v)
        barrier()

    _ns_name, _ns_id = _ns_stack.pop(); nc.leave_named_scope(_ns_name, _ns_id, False)
    _ns_stack.append(("phF", nc.enter_named_scope("phF", False)[0]))
    with contextlib.ExitStack() as ph:
        sba = lambda n, s, d: ph.enter_context(nc.sbuf_tensor(n, s, d))
        DTW = sba("DTW", [128, DH], BF16)
        WLf = sba("WLf", [128, DH], F32)
        TSB = [sba("TSBf0", [128, BLK], BF16), sba("TSBf1", [128, BLK], BF16)]
        SPS = [sba("SPS0", [128, 4 * ST], F32), sba("SPS1", [128, 4 * ST], F32)]
        DTT = sba("DTT", [128, 4 * ST], BF16)
        p_a = [ph.enter_context(nc.psum_tensor("pf_a0", [128, 512], F32)),
               ph.enter_context(nc.psum_tensor("pf_a1", [128, 512], F32))]
        lwF, wcF = lw.v, wc.v

        @block.sync
        def _(sync):
            sync.dma_start(out=WLf[:, :], in_=dtwt).then_inc(lw.h, 16)
            lw.v += 16

        @block.scalar
        def _(sc):
            sc.wait_ge(lw.h, lwF + 16)
            i = sc.activation(DTW[:, :], WLf[:, :], AF.Copy, scale=1.0)
            wc.inc(i)

        ldF, mmF, evF, dvF, stqF = ld.v, mmA.v, evA.v, dvA.v, stq.v

        @block.sync
        def _(sync):
            for b in range(NBLK):
                sync.wait_ge(mmA.h, max(mmF, mmF + (b - 1) * NCC * NSUB))
                if b >= 1:
                    sync.wait_ge(ld.h, ldF + b * 16)
                sync.dma_start(out=TSB[b % 2][:, :],
                               in_=ts_d[:, b * BLK:(b + 1) * BLK]
                               ).then_inc(ld.h, 16)
            ld.v += NBLK * 16

        @block.tensor
        def _(t):
            t.wait_ge(wc.h, wcF + 1)
            for b in range(NBLK):
                t.wait_ge(ld.h, ldF + (b + 1) * 16)
                for cc in range(NCC):
                    for sub in range(NSUB):
                        nmm = (b * NCC + cc) * NSUB + sub
                        if nmm >= 2:
                            t.wait_ge(evA.h, evF + 4 * (nmm - 1))
                        i = t.matmul(p_a[nmm % 2][:, 0:ST],
                                     DTW[:, cc * 128:(cc + 1) * 128],
                                     TSB[b % 2][:, sub * ST:(sub + 1) * ST],
                                     start=True, stop=True)
                        mmA.inc(i)

        @block.scalar
        def _(sc):
            for b in range(NBLK):
                for cc in range(NCC):
                    for sub in range(NSUB):
                        nmm = (b * NCC + cc) * NSUB + sub
                        sc.wait_ge(mmA.h, mmF + nmm + 1)
                        if nmm >= 1:
                            sc.wait_ge(dvA.h, dvF + nmm)
                        sl = SPS[nmm % 2]
                        bias = dtb_sb[:, cc:cc + 1]
                        sc.activation(sl[:, 0:ST], p_a[nmm % 2][:, 0:ST],
                                      AF.Abs, bias=bias, scale=1.0)
                        sc.activation(sl[:, ST:2 * ST], sl[:, 0:ST], AF.Exp,
                                      scale=-1.0)
                        sc.activation(sl[:, 2 * ST:3 * ST], sl[:, ST:2 * ST],
                                      AF.Ln, bias=1.0, scale=1.0)
                        i = sc.activation(sl[:, 3 * ST:4 * ST],
                                          p_a[nmm % 2][:, 0:ST],
                                          AF.Relu, bias=bias, scale=1.0)
                        evA.inc(i, 4)

        @block.vector
        def _(v):
            for b in range(NBLK):
                for cc in range(NCC):
                    for sub in range(NSUB):
                        nmm = (b * NCC + cc) * NSUB + sub
                        v.wait_ge(evA.h, evF + 4 * (nmm + 1))
                        if nmm >= 4:
                            v.wait_ge(stq.h, stqF + (nmm - 3) * 16)
                        sl = SPS[nmm % 2]
                        dst = DTT[:, (nmm % 4) * ST:(nmm % 4 + 1) * ST]
                        v.tensor_tensor(dst, sl[:, 2 * ST:3 * ST],
                                        sl[:, 3 * ST:4 * ST], OP.add)
                        i = v.tensor_scalar(dst, dst, mask_sb[:, b:b + 1],
                                            None, OP.mult)
                        dvA.inc(i)

        @block.gpsimd
        def _(sync):
            for b in range(NBLK):
                for cc in range(NCC):
                    for sub in range(NSUB):
                        nmm = (b * NCC + cc) * NSUB + sub
                        sync.wait_ge(dvA.h, dvF + nmm + 1)
                        if nmm >= 1:
                            sync.wait_ge(stq.h, stqF + nmm * 16)
                        t0 = b * BLK + sub * ST
                        sync.dma_start(out=dt_d[cc][:, t0:t0 + ST],
                                       in_=DTT[:, (nmm % 4) * ST:(nmm % 4 + 1) * ST]
                                       ).then_inc(stq.h, 16)
            stq.v += NBLK * NCC * NSUB * 16
            sync.wait_ge(stq.h, stq.v)
        barrier()

    _ns_name, _ns_id = _ns_stack.pop(); nc.leave_named_scope(_ns_name, _ns_id, False)
    _ns_stack.append(("phG1", nc.enter_named_scope("phG1", False)[0]))
    with contextlib.ExitStack() as ph:
        sba = lambda n, s, d: ph.enter_context(nc.sbuf_tensor(n, s, d))
        BBC = sba("BBCg1", [128, NST * BLK], BF16)
        BCT = [sba("BCTg10", [32, BLK], BF16), sba("BCTg11", [32, BLK], BF16)]
        DTU = [sba("DTUg10", [128, BLK], BF16), sba("DTUg11", [128, BLK], BF16)]
        UU = [sba("UUg10", [128, BLK], BF16), sba("UUg11", [128, BLK], BF16)]
        DU = [sba("DUg10", [128, BLK], BF16), sba("DUg11", [128, BLK], BF16)]
        DBU = [sba("DBUg10", [128, BLK], BF16), sba("DBUg11", [128, BLK], BF16)]
        EE = [sba("EEg10", [128, BLK], BF16), sba("EEg11", [128, BLK], BF16)]
        VV = sba("VVg1", [128, BLK], BF16)
        WW = [sba("WWg10", [128, BLK], BF16), sba("WWg11", [128, BLK], BF16)]
        ONES = sba("ONESg1", [128, BLK], BF16)
        CS = [sba("CSg10", [128, BLK], F32), sba("CSg11", [128, BLK], F32)]
        RR = [sba("RRg10", [128, BLK], F32), sba("RRg11", [128, BLK], F32)]
        TOT = sba("TOTg1", [128, 2], F32)
        PFP = [sba("PFPg10", [128, NST], F32), sba("PFPg11", [128, NST], F32)]
        PEX = [sba("PEXg10", [128, NST], F32), sba("PEXg11", [128, NST], F32)]
        QQ = [sba("QQg10", [128, NST], F32), sba("QQg11", [128, NST], F32)]
        SCR = sba("SCRg1", [128, NST], F32)
        p_a = [ph.enter_context(nc.psum_tensor("pg1_a0", [128, 512], F32)),
               ph.enter_context(nc.psum_tensor("pg1_a1", [128, 512], F32))]

        POOL_NS = [n for n in range(NST) if n % 2 == 1]
        PK = {n: k for k, n in enumerate(POOL_NS)}
        NPN = len(POOL_NS)
        U1 = TB0 * NCC
        ld1, lw1, mm1, ev1, evB1 = ld.v, lw.v, mmA.v, evA.v, evB.v
        dv1, dvB1, hs1, gq1 = dvA.v, dvB.v, hs.v, gq.v
        DVR = lambda j: dv1 + 1 + 2 * j + 2
        DVU = lambda j: dv1 + 1 + 2 * j + 1

        @block.sync
        def _(sync):
            for b in range(TB0):
                sync.wait_ge(mmA.h, mm1 + (32 * (b - 1) if b >= 2 else 0))
                if b >= 1:
                    sync.wait_ge(lw.h, lw1 + b * 16)
                sync.dma_start(out=BCT[b % 2][:, :],
                               in_=bc_d[:, b * BLK:(b + 1) * BLK]
                               ).then_inc(lw.h, 16)
                for cc in range(NCC):
                    j = b * NCC + cc
                    if j >= 2:
                        sync.wait_ge(dvA.h, DVR(j - 2))
                    if j >= 1:
                        sync.wait_ge(ld.h, ld1 + 32 * j)
                    sync.dma_start(out=DTU[j % 2][:, :],
                                   in_=dt_d[cc][:, b * BLK:(b + 1) * BLK]
                                   ).then_inc(ld.h, 16)
                    sync.dma_start(out=UU[j % 2][:, :],
                                   in_=u_d[cc][:, b * BLK:(b + 1) * BLK]
                                   ).then_inc(ld.h, 16)
            ld.v += U1 * 32
            lw.v += TB0 * 16

        @block.tensor
        def _(t):
            nb1 = 0
            for b in range(TB0):
                t.wait_ge(lw.h, lw1 + (b + 1) * 16)
                if b >= 1:
                    t.wait_ge(gq.h, gq1 + NPN * b * NCC)
                    t.wait_ge(hs.h, hs1 + NST * b * NCC)
                for n in range(NST):
                    for sub in range(NSUB):
                        if nb1 >= 2:
                            t.wait_ge(evA.h, ev1 + nb1 - 1)
                        i = t.matmul(p_a[nb1 % 2][:, 0:ST],
                                     sel_sb[:, n * 128:(n + 1) * 128],
                                     BCT[b % 2][:, sub * ST:(sub + 1) * ST],
                                     start=True, stop=True)
                        mmA.inc(i)
                        nb1 += 1

        @block.scalar
        def _(sc):
            ne1 = 0
            for b in range(TB0):
                for n in range(NST):
                    for sub in range(NSUB):
                        sc.wait_ge(mmA.h, mm1 + ne1 + 1)
                        i = sc.activation(
                            BBC[:, n * BLK + sub * ST:n * BLK + (sub + 1) * ST],
                            p_a[ne1 % 2][:, 0:ST], AF.Copy, scale=1.0)
                        evA.inc(i)
                        ne1 += 1
                for cc in range(NCC):
                    j = b * NCC + cc
                    sc.wait_ge(dvA.h, DVR(j))
                    if j >= 2:
                        sc.wait_ge(dvB.h, dvB1 + (j - 1))
                    i = sc.activation(PEX[j % 2][:, :], PFP[j % 2][:, :],
                                      AF.Exp, scale=1.0)
                    evB.inc(i)
                    for n in range(NST):
                        if n >= 2:
                            sc.wait_ge(hs.h, hs1 + NST * j + n - 1)
                        i = sc.activation(
                            EE[n % 2][:, :], RR[j % 2][:, :], AF.Exp,
                            scale=a_sb[:, cc * NST + n:cc * NST + n + 1])
                        evB.inc(i)

        @block.vector
        def _(v):
            i = v.memset(ONES[:, :], 1.0)
            dvA.inc(i)
            v.wait_ge(dvA.h, dv1 + 1)
            for b in range(TB0):
                for cc in range(NCC):
                    j = b * NCC + cc
                    v.wait_ge(ld.h, ld1 + 32 * (j + 1))
                    if j >= 2:
                        v.wait_ge(gq.h, gq1 + NPN * (j - 1))
                        v.wait_ge(evB.h, evB1 + 17 * (j - 1))
                    i = v.tensor_tensor(DU[j % 2][:, :], DTU[j % 2][:, :],
                                        UU[j % 2][:, :], OP.mult)
                    dvA.inc(i)
                    v.reduce_sum(TOT[:, j % 2:j % 2 + 1], DTU[j % 2][:, :],
                                 axis=mybir.AxisListType.X)
                    v.tensor_tensor_scan(CS[j % 2][:, :], ONES[:, :],
                                         DTU[j % 2][:, :], 0.0,
                                         OP.mult, OP.add)
                    v.tensor_scalar(RR[j % 2][:, :], CS[j % 2][:, :], -1.0,
                                    TOT[:, j % 2:j % 2 + 1], OP.mult, OP.add)
                    i = v.tensor_scalar(PFP[j % 2][:, :],
                                        a_sb[:, cc * NST:(cc + 1) * NST],
                                        TOT[:, j % 2:j % 2 + 1], None,
                                        OP.mult)
                    dvA.inc(i)
                    v.wait_ge(evA.h, ev1 + 32 * (b + 1))
                    for n in range(NST):
                        v.wait_ge(evB.h, evB1 + 17 * j + n + 2)
                        if n in PK:
                            k = PK[n]
                            v.wait_ge(gq.h, gq1 + NPN * j + k + 1)
                            i = v.scalar_tensor_tensor(
                                WW[n % 2][:, :], EE[n % 2][:, :], 1.0,
                                DBU[k % 2][:, :], OP.mult, OP.mult,
                                accum_out=QQ[j % 2][:, n:n + 1])
                        else:
                            v.tensor_tensor(VV[:, :], EE[n % 2][:, :],
                                            DU[j % 2][:, :], OP.mult)
                            i = v.scalar_tensor_tensor(
                                WW[n % 2][:, :], VV[:, :], 1.0,
                                BBC[:, n * BLK:(n + 1) * BLK], OP.mult,
                                OP.mult, accum_out=QQ[j % 2][:, n:n + 1])
                        hs.inc(i)
                    v.wait_ge(hs.h, hs.v)
                    if j >= NCC:
                        v.wait_ge(dvB.h, dvB1 + (j - NCC) + 1)
                    c16 = state_sb[:, cc * NST:(cc + 1) * NST]
                    v.tensor_tensor(SCR[:, :], c16, PEX[j % 2][:, :], OP.mult)
                    i = v.tensor_tensor(c16, SCR[:, :], QQ[j % 2][:, :],
                                        OP.add)
                    dvB.inc(i)

        @block.gpsimd
        def _(g):
            for b in range(TB0):
                for cc in range(NCC):
                    j = b * NCC + cc
                    g.wait_ge(dvA.h, DVU(j))
                    g.wait_ge(evA.h, ev1 + 32 * (b + 1))
                    for k, n in enumerate(POOL_NS):
                        if j * NPN + k >= 2:
                            pj, pk = divmod(j * NPN + k - 2, NPN)
                            g.wait_ge(hs.h,
                                      hs1 + NST * pj + POOL_NS[pk] + 1)
                        i = g.tensor_tensor(DBU[k % 2][:, :], DU[j % 2][:, :],
                                            BBC[:, n * BLK:(n + 1) * BLK],
                                            OP.mult)
                        gq.inc(i)
        barrier()

    _ns_name, _ns_id = _ns_stack.pop(); nc.leave_named_scope(_ns_name, _ns_id, False)
    _ns_stack.append(("phG2", nc.enter_named_scope("phG2", False)[0]))
    with contextlib.ExitStack() as ph:
        sba = lambda n, s, d: ph.enter_context(nc.sbuf_tensor(n, s, d))
        BBC = sba("BBC", [128, NST * BLK], BF16)
        CBC = sba("CBC", [128, NST * BLK], BF16)
        BCT = [sba("BCT0", [32, BLK], BF16), sba("BCT1", [32, BLK], BF16)]
        DTU = [sba("DTU0", [128, BLK], BF16), sba("DTU1", [128, BLK], BF16)]
        UU = [sba("UU0", [128, BLK], BF16), sba("UU1", [128, BLK], BF16)]
        SGU = [sba("SGU0", [128, BLK], BF16), sba("SGU1", [128, BLK], BF16)]
        DU = sba("DU", [128, BLK], BF16)
        DBU = [sba("DBU0", [128, BLK], BF16), sba("DBU1", [128, BLK], BF16)]
        SS = [sba("SS0", [128, BLK], BF16), sba("SS1", [128, BLK], BF16)]
        SCT = [sba("SCT0", [128, BLK], BF16), sba("SCT1", [128, BLK], BF16)]
        YG = [sba("YG0", [128, BLK], BF16), sba("YG1", [128, BLK], BF16)]
        DA = [sba("DA0", [128, BLK], F32), sba("DA1", [128, BLK], F32)]
        YY = sba("YY", [128, BLK], F32)
        p_a = [ph.enter_context(nc.psum_tensor("pg_a0", [128, 512], F32)),
               ph.enter_context(nc.psum_tensor("pg_a1", [128, 512], F32))]
        BH = BLK // 2
        p_y = [[ph.enter_context(nc.psum_tensor("pg_y00", [128, BH], F32)),
                ph.enter_context(nc.psum_tensor("pg_y01", [128, BH], F32))],
               [ph.enter_context(nc.psum_tensor("pg_y10", [128, BH], F32)),
                ph.enter_context(nc.psum_tensor("pg_y11", [128, BH], F32))]]

        U2 = TAIL_BLK * NCC
        ld2, lw2, mm2, ev2, ac2 = ld.v, lw.v, mmA.v, evA.v, evB.v
        dv2, dv2Y, hs2, gq2, stq2 = dvA.v, dvB.v, hs.v, gq.v, stq.v
        evC2, mmY2 = evC.v, mmY.v
        W_DU2 = lambda j: dv2 + 1 if j == 0 else dv2 + 17 * j

        @block.sync
        def _(sync):
            for bi in range(TAIL_BLK):
                b = TB0 + bi
                sync.wait_ge(mmA.h, mm2 + (64 * (bi - 1) if bi >= 2 else 0))
                if bi >= 1:
                    sync.wait_ge(lw.h, lw2 + bi * 16)
                sync.dma_start(out=BCT[bi % 2][:, :],
                               in_=bc_d[:, b * BLK:(b + 1) * BLK]
                               ).then_inc(lw.h, 16)
                for cc in range(NCC):
                    j = bi * NCC + cc
                    if j >= 2:
                        sync.wait_ge(dvA.h, W_DU2(j - 2))
                        sync.wait_ge(evB.h, ac2 + (j - 1) * NST)
                        sync.wait_ge(dvB.h, dv2Y + j - 1)
                    if j >= 1:
                        sync.wait_ge(ld.h, ld2 + 48 * j)
                    sync.dma_start(out=DTU[j % 2][:, :],
                                   in_=dt_d[cc][:, b * BLK:(b + 1) * BLK]
                                   ).then_inc(ld.h, 16)
                    sync.dma_start(out=UU[j % 2][:, :],
                                   in_=u_d[cc][:, b * BLK:(b + 1) * BLK]
                                   ).then_inc(ld.h, 16)
                    sync.dma_start(out=SGU[j % 2][:, :],
                                   in_=sg_d[cc][:, bi * BLK:(bi + 1) * BLK]
                                   ).then_inc(ld.h, 16)
            ld.v += U2 * 48
            lw.v += TAIL_BLK * 16

        @block.tensor
        def _(t):
            nbc = 0
            for bi in range(TAIL_BLK):
                t.wait_ge(lw.h, lw2 + (bi + 1) * 16)
                if bi >= 1:
                    t.wait_ge(gq.h, gq2 + 16 * bi * NCC)
                    t.wait_ge(evC.h, evC2 + bi * NCC * NST)
                for rep in range(2):
                    for n in range(NST):
                        for sub in range(NSUB):
                            if nbc >= 2:
                                t.wait_ge(evA.h, ev2 + nbc - 1)
                            selcol = (rep * NST + n) * 128
                            i = t.matmul(
                                p_a[nbc % 2][:, 0:ST],
                                sel_sb[:, selcol:selcol + 128],
                                BCT[bi % 2][:, sub * ST:(sub + 1) * ST],
                                start=True, stop=True)
                            mmA.inc(i)
                            nbc += 1
                for cc in range(NCC):
                    j = bi * NCC + cc
                    if j >= 2:
                        t.wait_ge(dvB.h, dv2Y + j - 1)
                    for n in range(NST):
                        t.wait_ge(evC.h, evC2 + j * NST + n + 1)
                        for h in range(2):
                            i = t.matmul(p_y[j % 2][h][:, :], idnb[:, :],
                                         SCT[n % 2][:, h * BH:(h + 1) * BH],
                                         start=(n == 0), stop=(n == NST - 1))
                            mmY.inc(i)

        @block.scalar
        def _(sc):
            nev = 0
            for bi in range(TAIL_BLK):
                for rep in range(2):
                    dstA = BBC if rep == 0 else CBC
                    for n in range(NST):
                        for sub in range(NSUB):
                            sc.wait_ge(mmA.h, mm2 + nev + 1)
                            i = sc.activation(
                                dstA[:, n * BLK + sub * ST:
                                     n * BLK + (sub + 1) * ST],
                                p_a[nev % 2][:, 0:ST], AF.Copy, scale=1.0)
                            evA.inc(i)
                            nev += 1
                for cc in range(NCC):
                    j = bi * NCC + cc
                    sc.wait_ge(ld.h, ld2 + 48 * (j + 1))
                    for n in range(NST):
                        g = j * NST + n
                        if g >= 2:
                            sc.wait_ge(hs.h, hs2 + g - 1)
                        i = sc.activation(
                            DA[g % 2][:, :], DTU[j % 2][:, :], AF.Exp,
                            scale=a_sb[:, cc * NST + n:cc * NST + n + 1])
                        evB.inc(i)

        @block.vector
        def _(v):
            v.wait_ge(dvB.h, dv2Y)
            v.wait_ge(ld.h, ld2 + 48)
            i = v.tensor_tensor(DU[:, :], DTU[0][:, :], UU[0][:, :], OP.mult)
            dvA.inc(i)
            for bi in range(TAIL_BLK):
                for cc in range(NCC):
                    j = bi * NCC + cc
                    g0 = j * NST
                    v.wait_ge(evA.h, ev2 + 64 * (bi + 1))
                    for n in range(NST):
                        g = g0 + n
                        v.wait_ge(gq.h, gq2 + 16 * j + n + 1)
                        v.wait_ge(evB.h, ac2 + g + 1)
                        i = v.tensor_tensor_scan(
                            SS[n % 2][:, :], DA[g % 2][:, :], DBU[n % 2][:, :],
                            state_sb[:, cc * NST + n:cc * NST + n + 1],
                            OP.mult, OP.add)
                        hs.inc(i)
                        if n == 0:
                            if j >= 1:
                                ccp = (j - 1) % NCC
                                i = v.tensor_copy(
                                    state_sb[:, ccp * NST + NST - 1:
                                             ccp * NST + NST],
                                    SS[1][:, BLK - 1:BLK])
                                dvA.inc(i)
                        else:
                            m = n - 1
                            i = v.tensor_copy(
                                state_sb[:, cc * NST + m:cc * NST + m + 1],
                                SS[m % 2][:, BLK - 1:BLK])
                            dvA.inc(i)
                            if m >= 2:
                                v.wait_ge(mmY.h,
                                          mmY2 + j * 2 * NST + 2 * (m - 1))
                            i = v.tensor_tensor(
                                SCT[m % 2][:, :], SS[m % 2][:, :],
                                CBC[:, m * BLK:(m + 1) * BLK], OP.mult)
                            evC.inc(i)
                    if j < U2 - 1:
                        v.wait_ge(ld.h, ld2 + 48 * (j + 2))
                        i = v.tensor_tensor(DU[:, :], DTU[(j + 1) % 2][:, :],
                                            UU[(j + 1) % 2][:, :], OP.mult)
                        dvA.inc(i)
                    m = NST - 1
                    v.wait_ge(hs.h, hs.v)
                    v.wait_ge(mmY.h, mmY2 + j * 2 * NST + 2 * (m - 1))
                    i = v.tensor_tensor(
                        SCT[m % 2][:, :], SS[m % 2][:, :],
                        CBC[:, m * BLK:(m + 1) * BLK], OP.mult)
                    evC.inc(i)
                    v.wait_ge(mmY.h, mmY2 + (j + 1) * 2 * NST)
                    if j >= 2:
                        v.wait_ge(stq.h, stq2 + (j - 1) * 16)
                    v.scalar_tensor_tensor(YY[:, 0:BH], UU[j % 2][:, 0:BH],
                                           dpar_sb[:, cc:cc + 1],
                                           p_y[j % 2][0][:, :],
                                           OP.mult, OP.add)
                    v.scalar_tensor_tensor(YY[:, BH:BLK],
                                           UU[j % 2][:, BH:BLK],
                                           dpar_sb[:, cc:cc + 1],
                                           p_y[j % 2][1][:, :],
                                           OP.mult, OP.add)
                    v.wait_ge(ld.h, ld2 + 48 * (j + 1))
                    i = v.tensor_tensor(YG[j % 2][:, :], YY[:, :],
                                        SGU[j % 2][:, :], OP.mult)
                    dvB.inc(i)
                    if j == U2 - 1:
                        i = v.tensor_copy(
                            state_sb[:, cc * NST + NST - 1:cc * NST + NST],
                            SS[1][:, BLK - 1:BLK])
                        dvA.inc(i)

        @block.gpsimd
        def _(sync):
            for bi in range(TAIL_BLK):
                for cc in range(NCC):
                    j = bi * NCC + cc
                    g0 = j * NST
                    sync.wait_ge(dvA.h, W_DU2(j))
                    sync.wait_ge(evA.h, ev2 + 64 * (bi + 1))
                    for n in range(NST):
                        g = g0 + n
                        if n == 2 and j >= 1:
                            m = j - 1
                            sync.wait_ge(dvB.h, dv2Y + m + 1)
                            if m >= 1:
                                sync.wait_ge(stq.h, stq2 + m * 16)
                            ccp = m % NCC
                            t0 = (m // NCC) * BLK
                            slot_dma_out(sync, sh_yg[0, ccp][:, t0:t0 + BLK],
                                         sh_yg[1, ccp][:, t0:t0 + BLK],
                                         YG[m % 2][:, :], stq, reg=rg_slot)
                        if g >= 2:
                            sync.wait_ge(hs.h, hs2 + g - 1)
                        i = sync.tensor_tensor(DBU[n % 2][:, :], DU[:, :],
                                               BBC[:, n * BLK:(n + 1) * BLK],
                                               OP.mult)
                        gq.inc(i)
            m = U2 - 1
            sync.wait_ge(dvB.h, dv2Y + m + 1)
            sync.wait_ge(stq.h, stq2 + m * 16)
            ccp = m % NCC
            t0 = (m // NCC) * BLK
            slot_dma_out(sync, sh_yg[0, ccp][:, t0:t0 + BLK],
                         sh_yg[1, ccp][:, t0:t0 + BLK],
                         YG[m % 2][:, :], stq, reg=rg_slot)
            sync.wait_ge(stq.h, stq.v)

        @block.sync
        def _(sync):
            sync.wait_ge(stq.h, stq.v)
            flag_write(sync, 1, 2)
        barrier()

    _ns_name, _ns_id = _ns_stack.pop(); nc.leave_named_scope(_ns_name, _ns_id, False)
    _ns_stack.append(("phH", nc.enter_named_scope("phH", False)[0]))
    with contextlib.ExitStack() as ph:
        sba = lambda n, s, d: ph.enter_context(nc.sbuf_tensor(n, s, d))
        NKG = 2 * NCC
        YGR = sba("YGR", [128, NKG * TLH], BF16)
        WOB = sba("WOB", [128, NKG * 128], BF16)
        WLh = [sba("WLh0", [128, NKG * 128], F32),
               sba("WLh1", [128, NKG * 128], F32)]
        SEV = sba("SEV", [128, 4 * ST], F32)
        p_a = [ph.enter_context(nc.psum_tensor("ph_a0", [128, 512], F32)),
               ph.enter_context(nc.psum_tensor("ph_a1", [128, 512], F32))]
        ldH, lwH, wcH, mmH, evH, stqH = ld.v, lw.v, wc.v, mmA.v, evA.v, stq.v
        NTT = TLH // ST

        @block.sync
        def _(sync):
            flag_poll(sync, 1, 2)
            for ss in range(2):
                for cc in range(NCC):
                    kc = ss * NCC + cc
                    slot_dma_in(sync, YGR[:, kc * TLH:(kc + 1) * TLH],
                                sh_yg[ss, cc][:, 0:TLH],
                                sh_yg[ss, cc][:, TLH:2 * TLH], ld)
            for hp in range(NHC):
                sync.wait_ge(wc.h, max(wcH, wcH + hp - 1))
                if hp >= 1:
                    sync.wait_ge(lw.h, lwH + hp * 16)
                sync.dma_start(out=WLh[hp % 2][:, :], in_=woutt[hp]
                               ).then_inc(lw.h, 16)
            lw.v += NHC * 16

        @block.scalar
        def _(sc):
            for hp in range(NHC):
                sc.wait_ge(lw.h, lwH + (hp + 1) * 16)
                if hp >= 1:
                    sc.wait_ge(mmA.h, mmH + hp * NTT)
                i = sc.activation(WOB[:, :], WLh[hp % 2][:, :], AF.Copy,
                                  scale=1.0)
                wc.inc(i)
                for tt in range(NTT):
                    ntile = hp * NTT + tt
                    sc.wait_ge(mmA.h, mmH + ntile + 1)
                    if ntile >= 4:
                        sc.wait_ge(stq.h, stqH + (ntile - 3) * 16)
                    i = sc.activation(
                        SEV[:, (ntile % 4) * ST:(ntile % 4 + 1) * ST],
                        p_a[ntile % 2][:, 0:ST], AF.Copy, scale=1.0)
                    evA.inc(i)

        @block.tensor
        def _(t):
            t.wait_ge(ld.h, ldH + NKG * 16)
            for hp in range(NHC):
                t.wait_ge(wc.h, wcH + hp + 1)
                for tt in range(NTT):
                    ntile = hp * NTT + tt
                    if ntile >= 2:
                        t.wait_ge(evA.h, evH + ntile - 1)
                    for kc in range(NKG):
                        i = t.matmul(p_a[ntile % 2][:, 0:ST],
                                     WOB[:, kc * 128:(kc + 1) * 128],
                                     YGR[:, kc * TLH + tt * ST:kc * TLH + (tt + 1) * ST],
                                     start=(kc == 0), stop=(kc == NKG - 1))
                    mmA.inc(i)

        @block.gpsimd
        def _(sync):
            for hp in range(NHC):
                for tt in range(NTT):
                    ntile = hp * NTT + tt
                    sync.wait_ge(evA.h, evH + ntile + 1)
                    if ntile >= 1:
                        sync.wait_ge(stq.h, stqH + ntile * 16)
                    sync.dma_start(out=s_d[hp][:, tt * ST:(tt + 1) * ST],
                                   in_=SEV[:, (ntile % 4) * ST:(ntile % 4 + 1) * ST]
                                   ).then_inc(stq.h, 16)
            stq.v += NHC * NTT * 16
            sync.wait_ge(stq.h, stq.v)
        barrier()

    _ns_name, _ns_id = _ns_stack.pop(); nc.leave_named_scope(_ns_name, _ns_id, False)
    _ns_stack.append(("phI", nc.enter_named_scope("phI", False)[0]))
    NSB = max(1, TLH // BLK)
    SBS = TLH // NSB
    NS2 = SBS // ST
    for sb_i in range(NSB):
        with contextlib.ExitStack() as ph:
            sba = (lambda sfx: (lambda n, s, d: ph.enter_context(
                nc.sbuf_tensor(n + sfx, s, d))))(f"_sb{sb_i}")
            XNS = sba("XNS", [128, NKC * SBS], BF16)
            HDN = sba("HDN", [128, NIC * SBS], BF16)
            GWB = sba("GWB", [128, NKC * 128], BF16)
            UWB = sba("UWB", [128, NKC * 128], BF16)
            DWB = sba("DWB", [128, NIC * 128], BF16)
            GUS = sba("GUS", [128, 4 * ST], F32)
            SGF = sba("SGF", [128, 2 * ST], F32)
            SRES = sba("SRES", [128, 2 * ST], F32)
            p_g = ph.enter_context(nc.psum_tensor(f"pi_g{sb_i}", [128, 512], F32))
            p_u = ph.enter_context(nc.psum_tensor(f"pi_u{sb_i}", [128, 512], F32))
            p_d = [ph.enter_context(nc.psum_tensor(f"pi_d0{sb_i}", [128, 512], F32)),
                   ph.enter_context(nc.psum_tensor(f"pi_d1{sb_i}", [128, 512], F32))]

            ldI, lwI, wcI, lgI = ld.v, lw.v, wc.v, lg.v
            mmI, evI, dvI = mmA.v, evA.v, dvA.v
            sA = F - TL + 0 * TLH + sb_i * SBS
            sB = F - TL + 1 * TLH + sb_i * SBS

            @block.sync
            def _(sync):
                for kc in range(NKC):
                    bA, cA = sA // BLK, sA % BLK
                    bB, cB = sB // BLK, sB % BLK
                    slot_dma_in(sync, XNS[:, kc * SBS:(kc + 1) * SBS],
                                xnt_d[bA, kc][:, cA:cA + SBS],
                                xnt_d[bB, kc][:, cB:cB + SBS], ld)

            @block.gpsimd
            def _(g):
                for ic in range(NIC):
                    if ic >= 1:
                        g.wait_ge(mmA.h, mmI + 2 * ic * NS2)
                        g.wait_ge(lg.h, lgI + 32 * ic)
                    g.dma_start(out=GWB[:, :], in_=gatet[ic]).then_inc(lg.h, 16)
                    g.wait_ge(lg.h, lgI + 32 * ic + 16)
                    g.dma_start(out=UWB[:, :], in_=upt[ic]).then_inc(lg.h, 16)
                lg.v += NIC * 32

            @block.scalar
            def _(sc):
                for ic in range(NIC):
                    for s2 in range(NS2):
                        nmm = ic * NS2 + s2
                        sc.wait_ge(mmA.h, mmI + 2 * nmm + 1)
                        if nmm >= 1:
                            sc.wait_ge(dvA.h, dvI + nmm)
                        sc.activation(GUS[:, 0:ST], p_g[:, 0:ST], AF.Copy,
                                      scale=fsc_sb[:, ic:ic + 1])
                        i = sc.activation(GUS[:, ST:2 * ST], GUS[:, 0:ST],
                                          AF.Silu, scale=1.0)
                        evA.inc(i)
                        sc.wait_ge(mmA.h, mmI + 2 * nmm + 2)
                        i = sc.activation(
                            GUS[:, 2 * ST:3 * ST], p_u[:, 0:ST], AF.Copy,
                            scale=fsc_sb[:, NIC + ic:NIC + ic + 1])
                        evA.inc(i)

            @block.tensor
            def _(t):
                t.wait_ge(ld.h, ldI + NKC * 16)
                for ic in range(NIC):
                    for s2 in range(NS2):
                        nmm = ic * NS2 + s2
                        t.wait_ge(lg.h, lgI + 32 * ic + 16)
                        if nmm >= 1:
                            t.wait_ge(evA.h, evI + 2 * nmm - 1)
                        for kc in range(NKC):
                            i = t.matmul(
                                p_g[:, 0:ST], GWB[:, kc * 128:(kc + 1) * 128],
                                XNS[:, kc * SBS + s2 * ST:kc * SBS + (s2 + 1) * ST],
                                start=(kc == 0), stop=(kc == NKC - 1))
                        mmA.inc(i)
                        t.wait_ge(lg.h, lgI + 32 * ic + 32)
                        if nmm >= 1:
                            t.wait_ge(evA.h, evI + 2 * nmm)
                        for kc in range(NKC):
                            i = t.matmul(
                                p_u[:, 0:ST], UWB[:, kc * 128:(kc + 1) * 128],
                                XNS[:, kc * SBS + s2 * ST:kc * SBS + (s2 + 1) * ST],
                                start=(kc == 0), stop=(kc == NKC - 1))
                        mmA.inc(i)


            @block.vector
            def _(v):
                for ic in range(NIC):
                    for s2 in range(NS2):
                        nmm = ic * NS2 + s2
                        v.wait_ge(evA.h, evI + 2 * nmm + 2)
                        i = v.tensor_tensor(
                            HDN[:, ic * SBS + s2 * ST:ic * SBS + (s2 + 1) * ST],
                            GUS[:, ST:2 * ST], GUS[:, 2 * ST:3 * ST], OP.mult)
                        dvA.inc(i)

            ldI2, lwI2, wcI2, lgI2 = ld.v, lw.v, wc.v, lg.v
            mmT, evT, evS, dvS, stqI = mmB.v, evB.v, evC.v, dvB.v, stq.v

            @block.gpsimd
            def _(g):
                def sgf_store(nmm):
                    hc2, s22 = divmod(nmm, NS2)
                    g.wait_ge(dvB.h, dvS + nmm + 1)
                    if nmm >= 1:
                        g.wait_ge(stq.h, stqI + nmm * 16)
                    g.dma_start(
                        out=sgf_d[hc2][:, sb_i * SBS + s22 * ST:sb_i * SBS + (s22 + 1) * ST],
                        in_=SGF[:, (nmm % 2) * ST:(nmm % 2 + 1) * ST]
                        ).then_inc(stq.h, 16)

                for hc in range(NHC):
                    if hc >= 1:
                        g.wait_ge(mmB.h, mmT + hc * NS2)
                        g.wait_ge(lg.h, lgI2 + hc * 16)
                    g.dma_start(out=DWB[:, :], in_=downt[hc]).then_inc(lg.h, 16)
                    if hc >= 1:
                        for s2 in range(NS2):
                            sgf_store((hc - 1) * NS2 + s2)
                for s2 in range(NS2):
                    sgf_store((NHC - 1) * NS2 + s2)
                stq.v += NHC * NS2 * 16
                g.wait_ge(stq.h, stq.v)
                lg.v += NHC * 16

            @block.sync
            def _(sync):
                for hc in range(NHC):
                    for s2 in range(NS2):
                        nmm = hc * NS2 + s2
                        sync.wait_ge(dvB.h, max(dvS, dvS + nmm - 1))
                        if nmm >= 1:
                            sync.wait_ge(ld.h, ldI2 + nmm * 16)
                        sync.dma_start(
                            out=SRES[:, (nmm % 2) * ST:(nmm % 2 + 1) * ST],
                            in_=s_d[hc][:, sb_i * SBS + s2 * ST:sb_i * SBS + (s2 + 1) * ST]
                            ).then_inc(ld.h, 16)
                ld.v += NHC * NS2 * 16

            @block.scalar
            def _(sc):
                for hc in range(NHC):
                    for s2 in range(NS2):
                        nmm = hc * NS2 + s2
                        sc.wait_ge(mmB.h, mmT + nmm + 1)
                        if nmm >= 1:
                            sc.wait_ge(dvB.h, dvS + nmm)
                        sc.activation(
                            GUS[:, 3 * ST:4 * ST], p_d[nmm % 2][:, 0:ST],
                            AF.Copy,
                            scale=fsc_sb[:, 2 * NIC + hc:2 * NIC + hc + 1])
                        i = sc.activation(GUS[:, 3 * ST:4 * ST],
                                          GUS[:, 3 * ST:4 * ST], AF.Silu,
                                          scale=1.0)
                        evB.inc(i)

            @block.tensor
            def _(t):
                t.wait_ge(dvA.h, dvI + NIC * NS2)
                for hc in range(NHC):
                    t.wait_ge(lg.h, lgI2 + (hc + 1) * 16)
                    for s2 in range(NS2):
                        nmm = hc * NS2 + s2
                        if nmm >= 2:
                            t.wait_ge(evB.h, evT + nmm - 1)
                        for ic in range(NIC):
                            i = t.matmul(
                                p_d[nmm % 2][:, 0:ST],
                                DWB[:, ic * 128:(ic + 1) * 128],
                                HDN[:, ic * SBS + s2 * ST:ic * SBS + (s2 + 1) * ST],
                                start=(ic == 0), stop=(ic == NIC - 1))
                        mmB.inc(i)

            @block.vector
            def _(v):
                for hc in range(NHC):
                    for s2 in range(NS2):
                        nmm = hc * NS2 + s2
                        v.wait_ge(evB.h, evT + nmm + 1)
                        v.wait_ge(ld.h, ldI2 + (nmm + 1) * 16)
                        if nmm >= 2:
                            v.wait_ge(stq.h, stqI + (nmm - 1) * 16)
                        i = v.tensor_tensor(
                            SGF[:, (nmm % 2) * ST:(nmm % 2 + 1) * ST],
                            SRES[:, (nmm % 2) * ST:(nmm % 2 + 1) * ST],
                            GUS[:, 3 * ST:4 * ST], OP.mult)
                        dvB.inc(i)

            barrier()

    _ns_name, _ns_id = _ns_stack.pop(); nc.leave_named_scope(_ns_name, _ns_id, False)
    _ns_stack.append(("phJ", nc.enter_named_scope("phJ", False)[0]))
    with contextlib.ExitStack() as ph:
        sba = lambda n, s, d: ph.enter_context(nc.sbuf_tensor(n, s, d))
        SGP = [sba("SGP0", [128, NHC * 128], F32),
               sba("SGP1", [128, NHC * 128], F32)]
        OUTS = [sba("OUTS0", [128, D], F32), sba("OUTS1", [128, D], F32)]
        XRES = [sba("XRES0", [128, D], F32), sba("XRES1", [128, D], F32)]
        p_t = [ph.enter_context(nc.psum_tensor("pj_t0", [128, 512], F32)),
               ph.enter_context(nc.psum_tensor("pj_t1", [128, 512], F32))]
        NTB = TLH // 128
        ldJ, mmJ, evJ, dvJ, stqJ = ld.v, mmA.v, evA.v, dvA.v, stq.v
        TPG = min(4, NHC)
        ngJ = NHC // TPG

        @block.sync
        def _(sync):
            for tb in range(NTB):
                sync.wait_ge(mmA.h, max(mmJ, mmJ + (tb - 1) * ngJ))
                if tb >= 1:
                    sync.wait_ge(ld.h, ldJ + tb * (NHC + 1) * 16)
                for hc in range(NHC):
                    sync.dma_start(out=SGP[tb % 2][:, hc * 128:(hc + 1) * 128],
                                   in_=sgf_d[hc][:, tb * 128:(tb + 1) * 128]
                                   ).then_inc(ld.h, 16)
                sync.wait_ge(dvA.h, max(dvJ, dvJ + tb - 1))
                rA = F - TL + 0 * TLH + tb * 128
                rB = F - TL + 1 * TLH + tb * 128
                slot_dma_in(sync, XRES[tb % 2][:, :],
                            x_f[rA:rA + 128, :], x_f[rB:rB + 128, :], ld)
                ld.v += NHC * 16

        @block.tensor
        def _(t):
            for tb in range(NTB):
                t.wait_ge(ld.h, ldJ + (tb + 1) * (NHC + 1) * 16)
                for g in range(ngJ):
                    ng = tb * ngJ + g
                    if ng >= 2:
                        t.wait_ge(evA.h, evJ + ng - 1)
                    for j in range(TPG):
                        hc = g * TPG + j
                        i = t.transpose(p_t[ng % 2][:, j * 128:(j + 1) * 128],
                                        SGP[tb % 2][:, hc * 128:(hc + 1) * 128],
                                        idnf[:, :])
                    mmA.inc(i)

        @block.scalar
        def _(sc):
            for tb in range(NTB):
                for g in range(ngJ):
                    ng = tb * ngJ + g
                    sc.wait_ge(mmA.h, mmJ + ng + 1)
                    if tb >= 2 and g == 0:
                        sc.wait_ge(dvA.h, dvJ + tb - 1)
                        sc.wait_ge(stq.h, stqJ + (tb - 1) * 16)
                    for j in range(TPG):
                        hc = g * TPG + j
                        i = sc.activation(
                            OUTS[tb % 2][:, hc * 128:(hc + 1) * 128],
                            p_t[ng % 2][:, j * 128:(j + 1) * 128],
                            AF.Copy, scale=1.0)
                    evA.inc(i)

        @block.vector
        def _(v):
            for tb in range(NTB):
                v.wait_ge(evA.h, evJ + (tb + 1) * ngJ)
                v.wait_ge(ld.h, ldJ + (tb + 1) * (NHC + 1) * 16)
                if tb >= 2:
                    v.wait_ge(stq.h, stqJ + (tb - 1) * 16)
                i = v.tensor_tensor(OUTS[tb % 2][:, :], OUTS[tb % 2][:, :],
                                    XRES[tb % 2][:, :], OP.add)
                dvA.inc(i)

        @block.gpsimd
        def _(sync):
            for tb in range(NTB):
                sync.wait_ge(dvA.h, dvJ + tb + 1)
                if tb >= 1:
                    sync.wait_ge(stq.h, stqJ + tb * 16)
                sync.dma_start(out=out_r[tb * 128:(tb + 1) * 128, :],
                               in_=OUTS[tb % 2][:, :]).then_inc(stq.h, 16)
            stq.v += NTB * 16
            sync.wait_ge(stq.h, stq.v)
        barrier()

    _ns_name, _ns_id = _ns_stack.pop(); nc.leave_named_scope(_ns_name, _ns_id, False)
    if debug:
        d_ln = nc.dram_tensor("d_ln", [128, 512], F32, kind="ExternalOutput").ap()
        dbg_specs = [
            ("d_xnt", [NBLK, NKC, 128, BLK], xnt_d),
            ("d_upre", [NCC, 128, F], upre_d),
            ("d_u", [NCC, 128, F], u_d),
            ("d_ts", [128, F], ts_d),
            ("d_bc", [32, F], bc_d),
            ("d_dt", [NCC, 128, F], dt_d),
            ("d_sg", [NCC, 128, TL], sg_d),
            ("d_s", [NHC, 128, TLH], s_d),
            ("d_sgf", [NHC, 128, TLH], sgf_d),
            ("d_ygA", [NCC, 128, TL], sh_yg[0]),
            ("d_ygB", [NCC, 128, TL], sh_yg[1]),
        ]
        dbg_sem = Sem(st_, nc, "dbg")

        @block.sync
        def _(sync):
            for name, shape, src_t in dbg_specs:
                dt_out = F32 if name in ("d_s", "d_sgf") else BF16
                od = nc.dram_tensor(name, shape, dt_out, kind="ExternalOutput").ap()
                ap = src_t.ap() if callable(getattr(src_t, "ap", None)) else src_t
                sync.dma_start(out=od, in_=ap).then_inc(dbg_sem.h, 16)
                dbg_sem.v += 16
            sync.dma_start(out=d_ln, in_=lndbg[:, :]).then_inc(dbg_sem.h, 16)
            dbg_sem.v += 16
            sync.wait_ge(dbg_sem.h, dbg_sem.v)
        barrier()

    st_.close()
    return nc



def stage_inputs(c, inp, nonce):
    F, D, DI, DH = c["F"], c["D"], c["DI"], c["DH"]
    NCC, NKC, NIC, NHC = c["NCC"], c["NKC"], c["NIC"], c["NHC"]
    NST, BLK, NBLK, TL, TLH = c["NST"], c["BLK"], c["NBLK"], c["TL"], c["TLH"]
    bf = ml_dtypes.bfloat16

    x = np.asarray(inp["L0_attn_hidden_states"], np.float32).reshape(-1, D)
    w_in = np.asarray(inp["L0_attn_in_proj_w"], np.float32)
    conv_w = np.asarray(inp["L0_attn_conv_w"], np.float32)
    conv_b = np.asarray(inp["L0_attn_conv_b"], np.float32)
    xpw = np.asarray(inp["L0_attn_x_proj_w"], np.float32)
    dtw = np.asarray(inp["L0_attn_dt_proj_w"], np.float32)
    dtb = np.asarray(inp["L0_attn_dt_proj_b"], np.float32)
    alog = np.asarray(inp["L0_attn_A_log"], np.float32)
    dpar = np.asarray(inp["L0_attn_D_param"], np.float32)
    wout = np.asarray(inp["L0_attn_out_proj_w"], np.float32)
    gw = np.asarray(inp["L0_ffn_gate_w"], np.float32)
    uw = np.asarray(inp["L0_ffn_up_w"], np.float32)
    dw = np.asarray(inp["L0_ffn_down_w"], np.float32)
    gs = np.asarray(inp["L0_ffn_gate_s"], np.float32)
    us = np.asarray(inp["L0_ffn_up_s"], np.float32)
    ds = np.asarray(inp["L0_ffn_down_s"], np.float32)

    idnb = np.eye(128, dtype=bf)
    idnf = np.eye(128, dtype=np.float32)
    sel = np.zeros((32, 2 * NST * 128), dtype=bf)
    for jj in range(2 * NST):
        sel[jj, jj * 128:(jj + 1) * 128] = 1.0

    in_maps = []
    for core in range(N_CORES):
        q, slot = core // 2, core % 2
        pend = (q + 1) * TL
        pad = F - pend
        xf = np.zeros((F, D), np.float32)
        xf[pad:] = x[:pend]
        mask = np.zeros((128, NBLK), np.float32)
        mask[:, pad // BLK:] = 1.0
        ch0 = slot * DH

        wu = np.empty((NCC, 128, D), np.float32)
        wg = np.empty((NCC, 128, D), np.float32)
        for pc in range(NCC):
            blk_u = w_in[ch0 + pc * 128:ch0 + (pc + 1) * 128, :]
            blk_g = w_in[DI + ch0 + pc * 128:DI + ch0 + (pc + 1) * 128, :]
            wu[pc] = (blk_u.T.reshape(NKC, 128, 128)
                      .transpose(1, 0, 2).reshape(128, D))
            wg[pc] = (blk_g.T.reshape(NKC, 128, 128)
                      .transpose(1, 0, 2).reshape(128, D))

        xp = np.empty((128, NCC * 160), np.float32)
        for pc in range(NCC):
            xp[:, pc * 160:(pc + 1) * 160] = \
                xpw[:, ch0 + pc * 128:ch0 + (pc + 1) * 128].T
        dtt = dtw[ch0:ch0 + DH, :].T.copy()

        cvd = np.zeros((128, NCC * 512), np.float32)
        for pc in range(NCC):
            for k in range(128):
                for j in range(4):
                    cvd[k, pc * 512 + j * 128 + k] = \
                        conv_w[ch0 + pc * 128 + k, 0, j]

        vec = np.zeros((128, 3 * NCC), np.float32)
        al = np.zeros((128, NCC * NST), np.float32)
        for pc in range(NCC):
            vec[:, pc] = conv_b[ch0 + pc * 128:ch0 + (pc + 1) * 128]
            vec[:, NCC + pc] = dtb[ch0 + pc * 128:ch0 + (pc + 1) * 128]
            vec[:, 2 * NCC + pc] = dpar[ch0 + pc * 128:ch0 + (pc + 1) * 128]
            al[:, pc * NST:(pc + 1) * NST] = \
                alog[ch0 + pc * 128:ch0 + (pc + 1) * 128, :]

        wo = np.empty((NHC, 128, 2 * NCC * 128), np.float32)
        for hp in range(NHC):
            blk = wout[hp * 128:(hp + 1) * 128, :]
            wo[hp] = (blk.T.reshape(2 * NCC, 128, 128)
                      .transpose(1, 0, 2).reshape(128, 2 * NCC * 128))

        gt = np.empty((NIC, 128, NKC * 128), np.float32)
        ut = np.empty((NIC, 128, NKC * 128), np.float32)
        for ic in range(NIC):
            blkg = gw[ic * 128:(ic + 1) * 128, :]
            blku = uw[ic * 128:(ic + 1) * 128, :]
            gt[ic] = (blkg.T.reshape(NKC, 128, 128)
                      .transpose(1, 0, 2).reshape(128, NKC * 128))
            ut[ic] = (blku.T.reshape(NKC, 128, 128)
                      .transpose(1, 0, 2).reshape(128, NKC * 128))
        dt_ = np.empty((NHC, 128, NIC * 128), np.float32)
        for hc in range(NHC):
            blk = dw[hc * 128:(hc + 1) * 128, :]
            dt_[hc] = (blk.T.reshape(NIC, 128, 128)
                       .transpose(1, 0, 2).reshape(128, NIC * 128))

        fs = np.zeros((128, 2 * NIC + NHC), np.float32)
        for ic in range(NIC):
            fs[:, ic] = gs[ic * 128:(ic + 1) * 128]
            fs[:, NIC + ic] = us[ic * 128:(ic + 1) * 128]
        for hc in range(NHC):
            fs[:, 2 * NIC + hc] = ds[hc * 128:(hc + 1) * 128]

        in_maps.append({
            "x_f": xf, "nonce": np.array([[nonce]], np.int32),
            "winu": wu, "wing": wg, "xpt": xp, "dtwt": dtt, "convd": cvd,
            "vecs": vec, "alog": al, "maskc": mask, "selm": sel,
            "idnb": idnb, "idnf": idnf, "woutt": wo, "gatet": gt,
            "upt": ut, "downt": dt_, "fscale": fs,
        })
    return in_maps


_NC_CACHE = {}


def kernel(L0_attn_hidden_states, L0_attn_in_proj_w, L0_attn_conv_w,
           L0_attn_conv_b, L0_attn_x_proj_w, L0_attn_dt_proj_w,
           L0_attn_dt_proj_b, L0_attn_A_log, L0_attn_D_param,
           L0_attn_out_proj_w, L0_ffn_gate_w, L0_ffn_up_w, L0_ffn_down_w,
           L0_ffn_gate_s, L0_ffn_up_s, L0_ffn_down_s):
    c = make_cfg(mini=False)
    inp = dict(
        L0_attn_hidden_states=L0_attn_hidden_states,
        L0_attn_in_proj_w=L0_attn_in_proj_w, L0_attn_conv_w=L0_attn_conv_w,
        L0_attn_conv_b=L0_attn_conv_b, L0_attn_x_proj_w=L0_attn_x_proj_w,
        L0_attn_dt_proj_w=L0_attn_dt_proj_w,
        L0_attn_dt_proj_b=L0_attn_dt_proj_b, L0_attn_A_log=L0_attn_A_log,
        L0_attn_D_param=L0_attn_D_param,
        L0_attn_out_proj_w=L0_attn_out_proj_w, L0_ffn_gate_w=L0_ffn_gate_w,
        L0_ffn_up_w=L0_ffn_up_w, L0_ffn_down_w=L0_ffn_down_w,
        L0_ffn_gate_s=L0_ffn_gate_s, L0_ffn_up_s=L0_ffn_up_s,
        L0_ffn_down_s=L0_ffn_down_s)
    nonce = int(np.random.randint(1, 1 << 30))
    in_maps = stage_inputs(c, inp, nonce)
    if "full" not in _NC_CACHE:
        _NC_CACHE["full"] = build_program(c)
    nc = _NC_CACHE["full"]
    res = run_bass_kernel_spmd(nc, in_maps, core_ids=list(range(N_CORES)))
    TLH, D = c["TLH"], c["D"]
    out = np.empty((N_CORES * TLH, D), np.float32)
    for core in range(N_CORES):
        out[core * TLH:(core + 1) * TLH] = res.results[core]["out_r"]
    return out.reshape(1, N_CORES * TLH, D)


# version 06
# speedup vs baseline: 1.1034x; 1.0893x over previous
import contextlib
import numpy as np
import ml_dtypes

import concourse.bass as bass
import concourse.mybir as mybir
from concourse.alu_op_type import AluOpType
from concourse.bass_utils import run_bass_kernel_spmd
from concourse.libnrt import physical_to_logical_core_id as _p2l

F32 = mybir.dt.float32
BF16 = mybir.dt.bfloat16
I32 = mybir.dt.int32
AF = mybir.ActivationFunctionType
OP = AluOpType

N_CORES = 8
NPAIR = 4


def make_cfg(mini=False, **over):
    if mini:
        c = dict(F=2048, D=256, DI=512, NST=16, INTER=512, BLK=512, ST=256)
    else:
        c = dict(F=16384, D=2048, DI=4096, NST=16, INTER=5632, BLK=1024,
                 ST=512)
    c.update(over)
    c["DH"] = c["DI"] // 2
    c["NCC"] = c["DH"] // 128
    c["NKC"] = c["D"] // 128
    c["NIC"] = c["INTER"] // 128
    c["NHC"] = c["D"] // 128
    c["NBLK"] = c["F"] // c["BLK"]
    c["TL"] = c["F"] // NPAIR
    c["TAIL_BLK"] = c["TL"] // c["BLK"]
    c["TLH"] = c["TL"] // 2
    c["NSUB"] = c["BLK"] // c["ST"]
    c["EPS"] = 1e-5
    return c


class Sem:
    def __init__(self, stack, nc, name):
        self.h = stack.enter_context(nc.semaphore(name))
        self.v = 0

    def inc(self, inst, k=1):
        inst.then_inc(self.h, k)
        self.v += k
        return self.v


def build_program(c, debug=False):
    nc = bass.Bass("TRN2", target_bir_lowering=False)
    F, D, DI, DH = c["F"], c["D"], c["DI"], c["DH"]
    NCC, NKC, NIC, NHC = c["NCC"], c["NKC"], c["NIC"], c["NHC"]
    NST = c["NST"]
    BLK, ST, NSUB = c["BLK"], c["ST"], c["NSUB"]
    NBLK, TL, TAIL_BLK, TLH = c["NBLK"], c["TL"], c["TAIL_BLK"], c["TLH"]
    TB0 = NBLK - TAIL_BLK
    NROW = BLK // 128
    NT0 = NBLK * NROW

    din = lambda n, s, d=F32: nc.dram_tensor(n, s, d, kind="ExternalInput").ap()
    x_f = din("x_f", [F, D])
    nonce_i = din("nonce", [1, 1], I32)
    winu = din("winu", [NCC, 128, D])
    wing = din("wing", [NCC, 128, D])
    xpt = din("xpt", [128, NCC * 160])
    dtwt = din("dtwt", [128, DH])
    convd = din("convd", [128, NCC * 512])
    vecs = din("vecs", [128, 3 * NCC])
    alog = din("alog", [128, NCC * NST])
    maskc = din("maskc", [128, NBLK])
    selm = din("selm", [32, 2 * NST * 128], BF16)
    idnb_i = din("idnb", [128, 128], BF16)
    idnf_i = din("idnf", [128, 128])
    woutt = din("woutt", [NHC, 128, 2 * NCC * 128])
    gatet = din("gatet", [NIC, 128, NKC * 128])
    upt = din("upt", [NIC, 128, NKC * 128])
    downt = din("downt", [NHC, 128, NIC * 128])
    fscale = din("fscale", [128, 2 * NIC + NHC])
    out_r = nc.dram_tensor("out_r", [TLH, D], F32, kind="ExternalOutput").ap()

    xnt_d = nc.dram_tensor("xnt_d", [NBLK, NKC, 128, BLK], BF16)
    upre_d = nc.dram_tensor("upre_d", [NCC, 128, F], BF16)
    u_d = nc.dram_tensor("u_d", [NCC, 128, F], BF16)
    dt_d = nc.dram_tensor("dt_d", [NCC, 128, F], BF16)
    ts_d = nc.dram_tensor("ts_d", [128, F], BF16)
    bc_d = nc.dram_tensor("bc_d", [32, F], BF16)
    sg_d = nc.dram_tensor("sg_d", [NCC, 128, TL], BF16)
    s_d = nc.dram_tensor("s_d", [NHC, 128, TLH], F32)
    sgf_d = nc.dram_tensor("sgf_d", [NHC, 128, TLH], F32)

    sh_ts = nc.dram_tensor("sh_ts", [2, 128, F], F32, addr_space="Shared")
    sh_bc = nc.dram_tensor("sh_bc", [2, 32, F], F32, addr_space="Shared")
    sh_yg = nc.dram_tensor("sh_yg", [2, NCC, 128, TL], BF16, addr_space="Shared")
    sh_fl = nc.dram_tensor("sh_fl", [2, 8], I32, addr_space="Shared")

    st_ = contextlib.ExitStack()
    sbuf = lambda n, s, d: st_.enter_context(nc.sbuf_tensor(n, s, d))

    nonce_sb = sbuf("nonce_sb", [1, 1], I32)
    params_sb = sbuf("params_sb", [1, 2], I32)
    a_sb = sbuf("a_sb", [128, NCC * NST], F32)
    convb_sb = sbuf("convb_sb", [128, NCC], F32)
    dtb_sb = sbuf("dtb_sb", [128, NCC], F32)
    dpar_sb = sbuf("dpar_sb", [128, NCC], F32)
    mask_sb = sbuf("mask_sb", [128, NBLK], F32)
    fsc_sb = sbuf("fsc_sb", [128, 2 * NIC + NHC], F32)
    sel_sb = sbuf("sel_sb", [32, 2 * NST * 128], BF16)
    state_sb = sbuf("state_sb", [128, NCC * NST], F32)
    idnb = sbuf("idnb_s", [128, 128], BF16)
    idnf = sbuf("idnf_s", [128, 128], F32)
    lnsc = sbuf("lnsc", [128, 64], F32)
    lndbg = sbuf("lndbg", [128, 512], F32)

    SM = {k: Sem(st_, nc, k) for k in
          ["ld", "lw", "wc", "mmA", "mmB", "evA", "evB", "evC",
           "dvA", "dvB", "stq", "stu", "bar", "hs", "lg", "gq", "mmY"]}
    hs = SM["hs"]
    lg = SM["lg"]
    ld, lw, wc = SM["ld"], SM["lw"], SM["wc"]
    mmA, mmB = SM["mmA"], SM["mmB"]
    evA, evB, evC = SM["evA"], SM["evB"], SM["evC"]
    dvA, dvB = SM["dvA"], SM["dvB"]
    stq, stu, bar = SM["stq"], SM["stu"], SM["bar"]
    gq, mmY = SM["gq"], SM["mmY"]

    _ns_stack = []
    block = st_.enter_context(nc.Block())
    st_.enter_context(nc.allow_low_precision(
        reason="bf16 matmuls / bf16 intermediates sanctioned by tolerance"))

    r_slot = st_.enter_context(nc.sync.register("r_slot"))
    rg_slot = st_.enter_context(nc.gpsimd.register("rg_slot"))
    r_non = st_.enter_context(nc.sync.register("r_non"))
    r_flag = st_.enter_context(nc.sync.register("r_flag"))
    r_cont = st_.enter_context(nc.sync.register("r_cont"))
    r_poll = st_.enter_context(nc.sync.register("r_poll"))

    ALL_ENG = (block.sync, block.tensor, block.scalar, block.vector,
               block.gpsimd)

    def barrier():
        for emit in ALL_ENG:
            @emit
            def _(e):
                e.nop().then_inc(bar.h, 1)
        bar.v += len(ALL_ENG)
        tgt = bar.v
        for emit in ALL_ENG:
            @emit
            def _(e):
                e.wait_ge(bar.h, tgt)

    def flag_write(sync, idx, off):
        sync.reg_alu(r_flag, sync.snap(r_non), off, OP.add)
        s = slot_sv(sync, r_slot)
        with sync.If(s == 0):
            sync.store(sh_fl[0:1, idx:idx + 1], r_flag)
        with sync.Else():
            sync.store(sh_fl[1:2, idx:idx + 1], r_flag)

    def flag_poll(sync, idx, off):
        sync.reg_alu(r_flag, sync.snap(r_non), off, OP.add)
        s = slot_sv(sync, r_slot)
        with sync.If(s == 0):
            sync.reg_mov(r_cont, 1)
            with sync.While(r_cont):
                sync.load(r_poll, sh_fl[1:2, idx:idx + 1])
                sync.reg_alu(r_cont, sync.snap(r_poll), sync.snap(r_flag),
                             OP.not_equal)
        with sync.Else():
            sync.reg_mov(r_cont, 1)
            with sync.While(r_cont):
                sync.load(r_poll, sh_fl[0:1, idx:idx + 1])
                sync.reg_alu(r_cont, sync.snap(r_poll), sync.snap(r_flag),
                             OP.not_equal)

    SLOT_SV = {}

    def slot_sv(eng, reg):
        key = id(eng)
        if key not in SLOT_SV:
            SLOT_SV[key] = eng.snap(reg, min_val=0, max_val=1)
        return SLOT_SV[key]

    def slot_dma_out(eng, dst0, dst1, src, sem, reg=None):
        s = slot_sv(eng, reg if reg is not None else r_slot)
        with eng.If(s == 0):
            eng.dma_start(out=dst0, in_=src).then_inc(sem.h, 16)
        with eng.Else():
            eng.dma_start(out=dst1, in_=src).then_inc(sem.h, 16)
        sem.v += 16

    def slot_dma_in(eng, dst, src0, src1, sem, reg=None):
        s = slot_sv(eng, reg if reg is not None else r_slot)
        with eng.If(s == 0):
            eng.dma_start(out=dst, in_=src0).then_inc(sem.h, 16)
        with eng.Else():
            eng.dma_start(out=dst, in_=src1).then_inc(sem.h, 16)
        sem.v += 16

    def gstore(g, dst, src, sem):
        g.dma_start(out=dst, in_=src).then_inc(sem.h, 16)
        sem.v += 16

    @block.sync
    def _(sync):
        for arm in sync.switch_core_id(n=128):
            sync.reg_mov(r_slot,
                         _p2l(int(arm) % 8, 0, container_num_cores=8) % 2)
        for dst, src in [
                (nonce_sb[:, :], nonce_i), (a_sb[:, :], alog),
                (convb_sb[:, :], vecs[:, 0:NCC]),
                (dtb_sb[:, :], vecs[:, NCC:2 * NCC]),
                (dpar_sb[:, :], vecs[:, 2 * NCC:3 * NCC]),
                (mask_sb[:, :], maskc), (sel_sb[:, :], selm),
                (idnb[:, :], idnb_i), (idnf[:, :], idnf_i),
                (fsc_sb[:, :], fscale)]:
            sync.dma_start(out=dst, in_=src).then_inc(ld.h, 16)
        ld.v += 160
        sync.wait_ge(ld.h, ld.v)
        sync.load(r_non, nonce_sb[0:1, 0:1])
        sync.store(params_sb[0:1, 0:1], r_slot)
        sync.nop().then_inc(hs.h, 1)
    hs.v += 1

    @block.gpsimd
    def _(g):
        g.wait_ge(hs.h, hs.v)
        g.load(rg_slot, params_sb[0:1, 0:1])

    @block.scalar
    def _(sc):
        sc.wait_ge(hs.h, hs.v)
        i = sc.activation(a_sb[:, :], a_sb[:, :], AF.Exp, scale=1.0)
        evA.inc(i)

    @block.vector
    def _(v):
        v.wait_ge(evA.h, evA.v)
        v.tensor_scalar(a_sb[:, :], a_sb[:, :], -1.0, None, OP.mult)
        i = v.memset(state_sb[:, :], 0.0)
        dvA.inc(i)
    barrier()

    _ns_stack.append(("phA", nc.enter_named_scope("phA", False)[0]))
    with contextlib.ExitStack() as ph:
        sba = lambda n, s, d: ph.enter_context(nc.sbuf_tensor(n, s, d))
        XR = sba("XR", [128, 2 * D], F32)
        XNB = sba("XNB", [128, 2 * D], BF16)
        STG = [sba("STG0", [128, NKC * BLK], BF16),
               sba("STG1", [128, NKC * BLK], BF16)]
        pt_bf = [ph.enter_context(nc.psum_tensor("pt_bf0", [128, 512], BF16)),
                 ph.enter_context(nc.psum_tensor("pt_bf1", [128, 512], BF16))]
        xr = [XR[:, 0:D], XR[:, D:2 * D]]
        xnb = [XNB[:, 0:D], XNB[:, D:2 * D]]
        nstat = max(1, D // 512)
        kpg = min(4, NKC)
        ngrp = NKC // kpg

        ldA, dvA_A, evA_A = ld.v, dvA.v, evA.v
        mmA_A, evB_A, stqA = mmA.v, evB.v, stq.v

        @block.sync
        def _(sync):
            for it in range(NT0):
                sync.wait_ge(dvA.h, max(dvA_A, dvA_A + 2 * (it - 1)))
                if it >= 1:
                    sync.wait_ge(ld.h, ldA + it * 16)
                sync.dma_start(out=xr[it % 2],
                               in_=x_f[it * 128:(it + 1) * 128, :]
                               ).then_inc(ld.h, 16)
            ld.v += NT0 * 16

        @block.vector
        def _(v):
            for it in range(NT0):
                sl = lnsc[:, (it % 2) * 8:(it % 2) * 8 + 8]
                stt = lnsc[:, 16 + (it % 2) * 24:16 + (it % 2) * 24 + 6 * nstat]
                v.wait_ge(ld.h, ldA + min(NT0, it + 2) * 16)
                if it >= 2:
                    v.wait_ge(evA.h, evA_A + (it - 1))
                for s in range(nstat):
                    w = min(512, D)
                    i = v.bn_stats(stt[:, s * 6:(s + 1) * 6],
                                   xr[it % 2][:, s * w:(s + 1) * w])
                dvB.inc(i)
                v.wait_ge(dvB.h, dvB.v)
                i = v.bn_aggr(sl[:, 0:2], stt[:, 0:6 * nstat])
                dvB.inc(i)
                v.wait_ge(dvB.h, dvB.v)
                i = v.tensor_scalar(sl[:, 2:3], sl[:, 1:2], 1.0,
                                    float(c["EPS"]), OP.mult, OP.add)
                dvA.inc(i)
                v.wait_ge(evA.h, evA_A + it + 1)
                if it >= 2:
                    v.wait_ge(mmA.h, mmA_A + (it - 1) * ngrp)
                i = v.tensor_scalar(xnb[it % 2], xr[it % 2], sl[:, 0:1],
                                    sl[:, 3:4], OP.subtract, OP.mult)
                dvA.inc(i)

        def emit_evacs(sc, it):
            b, r = it // NROW, it % NROW
            if r == 0 and b >= 2:
                sc.wait_ge(stq.h, stqA + (b - 1) * NKC * 16)
            for g in range(ngrp):
                e = it * ngrp + g
                sc.wait_ge(mmA.h, mmA_A + e + 1)
                for j in range(kpg):
                    kc = g * kpg + j
                    i = sc.activation(
                        STG[b % 2][:, kc * BLK + r * 128:kc * BLK + (r + 1) * 128],
                        pt_bf[e % 2][:, j * 128:(j + 1) * 128], AF.Copy,
                        scale=1.0)
                evB.inc(i)

        @block.scalar
        def _(sc):
            for it in range(NT0):
                sl = lnsc[:, (it % 2) * 8:(it % 2) * 8 + 8]
                sc.wait_ge(dvA.h, dvA_A + 2 * it + 1)
                sc.activation(sl[:, 5:6], sl[:, 2:3], AF.Ln, scale=1.0)
                i = sc.activation(sl[:, 3:4], sl[:, 5:6], AF.Exp, scale=-0.5)
                evA.inc(i)
                if it >= 1:
                    emit_evacs(sc, it - 1)
            emit_evacs(sc, NT0 - 1)

        @block.tensor
        def _(t):
            for it in range(NT0):
                t.wait_ge(dvA.h, dvA_A + 2 * it + 2)
                for g in range(ngrp):
                    e = it * ngrp + g
                    if e >= 2:
                        t.wait_ge(evB.h, evB_A + e - 1)
                    for j in range(kpg):
                        kc = g * kpg + j
                        i = t.transpose(pt_bf[e % 2][:, j * 128:(j + 1) * 128],
                                        xnb[it % 2][:, kc * 128:(kc + 1) * 128],
                                        idnb[:, :])
                    mmA.inc(i)

        @block.gpsimd
        def _(sync):
            for b in range(NBLK):
                sync.wait_ge(evB.h, evB_A + (b + 1) * NROW * ngrp)
                if b >= 1:
                    sync.wait_ge(stq.h, stqA + b * NKC * 16)
                for kc in range(NKC):
                    sync.dma_start(out=xnt_d[b, kc],
                                   in_=STG[b % 2][:, kc * BLK:(kc + 1) * BLK]
                                   ).then_inc(stq.h, 16)
            stq.v += NBLK * NKC * 16
            sync.wait_ge(stq.h, stq.v)
        barrier()

    _ns_name, _ns_id = _ns_stack.pop(); nc.leave_named_scope(_ns_name, _ns_id, False)
    _ns_stack.append(("phB", nc.enter_named_scope("phB", False)[0]))
    with contextlib.ExitStack() as ph:
        sba = lambda n, s, d: ph.enter_context(nc.sbuf_tensor(n, s, d))
        WB = sba("WB", [128, NCC * D], BF16)
        WL = [sba("WL0", [128, D], F32), sba("WL1", [128, D], F32)]
        XNT = [sba("XNT0", [128, NKC * BLK], BF16),
               sba("XNT1", [128, NKC * BLK], BF16)]
        UPS = sba("UPS", [128, 4 * ST], BF16)
        p_a = [ph.enter_context(nc.psum_tensor("pb_a0", [128, 512], F32)),
               ph.enter_context(nc.psum_tensor("pb_a1", [128, 512], F32))]
        lwB, wcB = lw.v, wc.v

        @block.sync
        def _(sync):
            for pc in range(NCC):
                sync.wait_ge(wc.h, max(wcB, wcB + pc - 1))
                if pc >= 1:
                    sync.wait_ge(lw.h, lwB + pc * 16)
                sync.dma_start(out=WL[pc % 2][:, :], in_=winu[pc]
                               ).then_inc(lw.h, 16)
            lw.v += NCC * 16

        @block.scalar
        def _(sc):
            for pc in range(NCC):
                sc.wait_ge(lw.h, lwB + (pc + 1) * 16)
                i = sc.activation(WB[:, pc * D:(pc + 1) * D], WL[pc % 2][:, :],
                                  AF.Copy, scale=1.0)
                wc.inc(i)

        ldB, mmB_B, evC_B, stqB = ld.v, mmA.v, evC.v, stq.v

        @block.sync
        def _(sync):
            for b in range(NBLK):
                sync.wait_ge(mmA.h, max(mmB_B, mmB_B + (b - 1) * NCC * NSUB))
                if b >= 1:
                    sync.wait_ge(ld.h, ldB + b * NKC * 16)
                for kc in range(NKC):
                    sync.dma_start(out=XNT[b % 2][:, kc * BLK:(kc + 1) * BLK],
                                   in_=xnt_d[b, kc]).then_inc(ld.h, 16)
            ld.v += NBLK * NKC * 16

        @block.tensor
        def _(t):
            t.wait_ge(wc.h, wcB + NCC)
            for b in range(NBLK):
                t.wait_ge(ld.h, ldB + (b + 1) * NKC * 16)
                for pc in range(NCC):
                    for sub in range(NSUB):
                        nmm = (b * NCC + pc) * NSUB + sub
                        if nmm >= 2:
                            t.wait_ge(evC.h, evC_B + nmm - 1)
                        for kc in range(NKC):
                            i = t.matmul(
                                p_a[nmm % 2][:, 0:ST],
                                WB[:, pc * D + kc * 128:pc * D + (kc + 1) * 128],
                                XNT[b % 2][:, kc * BLK + sub * ST:kc * BLK + (sub + 1) * ST],
                                start=(kc == 0), stop=(kc == NKC - 1))
                        mmA.inc(i)

        @block.scalar
        def _(sc):
            for nmm in range(NBLK * NCC * NSUB):
                sc.wait_ge(mmA.h, mmB_B + nmm + 1)
                if nmm >= 4:
                    sc.wait_ge(stq.h, stqB + (nmm - 3) * 16)
                i = sc.activation(UPS[:, (nmm % 4) * ST:(nmm % 4 + 1) * ST],
                                  p_a[nmm % 2][:, 0:ST], AF.Copy, scale=1.0)
                evC.inc(i)

        @block.gpsimd
        def _(sync):
            for b in range(NBLK):
                for pc in range(NCC):
                    for sub in range(NSUB):
                        nmm = (b * NCC + pc) * NSUB + sub
                        sync.wait_ge(evC.h, evC_B + nmm + 1)
                        if nmm >= 1:
                            sync.wait_ge(stq.h, stqB + nmm * 16)
                        sync.dma_start(
                            out=upre_d[pc][:, b * BLK + sub * ST:b * BLK + (sub + 1) * ST],
                            in_=UPS[:, (nmm % 4) * ST:(nmm % 4 + 1) * ST]
                            ).then_inc(stq.h, 16)
            stq.v += NBLK * NCC * NSUB * 16
            sync.wait_ge(stq.h, stq.v)
        barrier()

    _ns_name, _ns_id = _ns_stack.pop(); nc.leave_named_scope(_ns_name, _ns_id, False)
    _ns_stack.append(("phC", nc.enter_named_scope("phC", False)[0]))
    with contextlib.ExitStack() as ph:
        sba = lambda n, s, d: ph.enter_context(nc.sbuf_tensor(n, s, d))
        HB = BLK + 3
        XPB = sba("XPB", [128, NCC * 160], BF16)
        CVB = sba("CVB", [128, NCC * 512], BF16)
        WLx = sba("WLx", [128, NCC * 512], F32)
        UPT = [sba("UPT0", [128, NCC * (HB + 1)], BF16),
               sba("UPT1", [128, NCC * (HB + 1)], BF16)]
        UT = sba("UT", [128, 4 * ST], BF16)
        TSE = sba("TSE", [128, 2 * ST], F32)
        BCE = sba("BCE", [32, 2 * ST], F32)
        p_a = [ph.enter_context(nc.psum_tensor("pc_a0", [128, 512], F32)),
               ph.enter_context(nc.psum_tensor("pc_a1", [128, 512], F32))]
        p_ts = [ph.enter_context(nc.psum_tensor("pc_t0", [128, 512], F32)),
                ph.enter_context(nc.psum_tensor("pc_t1", [128, 512], F32))]
        p_bc = [ph.enter_context(nc.psum_tensor("pc_b0", [32, 512], F32)),
                ph.enter_context(nc.psum_tensor("pc_b1", [32, 512], F32))]
        lwC, wcC = lw.v, wc.v

        @block.sync
        def _(sync):
            sync.dma_start(out=WLx[:, 0:NCC * 160], in_=xpt).then_inc(lw.h, 16)
            sync.wait_ge(wc.h, wcC + 1)
            sync.wait_ge(lw.h, lwC + 16)
            sync.dma_start(out=WLx[:, 0:NCC * 512], in_=convd).then_inc(lw.h, 16)
            lw.v += 32

        @block.scalar
        def _(sc):
            sc.wait_ge(lw.h, lwC + 16)
            i = sc.activation(XPB[:, :], WLx[:, 0:NCC * 160], AF.Copy, scale=1.0)
            wc.inc(i)
            sc.wait_ge(lw.h, lwC + 32)
            i = sc.activation(CVB[:, :], WLx[:, 0:NCC * 512], AF.Copy, scale=1.0)
            wc.inc(i)

        ldC, dvC = ld.v, dvA.v
        mmC_C, mmX_C, evU_C, evT_C = mmA.v, mmB.v, evA.v, evB.v
        stqC, stuC = stq.v, stu.v

        @block.vector
        def _(v):
            v.memset(UPT[0][:, :], 0.0)
            i = v.memset(UPT[1][:, :], 0.0)
            dvA.inc(i)

        @block.sync
        def _(sync):
            sync.wait_ge(dvA.h, dvC + 1)
            for b in range(NBLK):
                sync.wait_ge(mmA.h, max(mmC_C, mmC_C + (b - 1) * NSUB * NCC))
                if b >= 1:
                    sync.wait_ge(ld.h, ldC + b * NCC * 16)
                for cc in range(NCC):
                    base = cc * (HB + 1)
                    if b == 0:
                        sync.dma_start(out=UPT[0][:, base + 3:base + 3 + BLK],
                                       in_=upre_d[cc][:, 0:BLK]).then_inc(ld.h, 16)
                    else:
                        sync.dma_start(out=UPT[b % 2][:, base:base + HB],
                                       in_=upre_d[cc][:, b * BLK - 3:b * BLK + BLK]
                                       ).then_inc(ld.h, 16)
            ld.v += NBLK * NCC * 16

        @block.tensor
        def _(t):
            t.wait_ge(wc.h, wcC + 2)
            for b in range(NBLK):
                t.wait_ge(ld.h, ldC + (b + 1) * NCC * 16)
                for sub in range(NSUB):
                    nbs = b * NSUB + sub
                    for cc in range(NCC):
                        ncv = nbs * NCC + cc
                        if ncv >= 2:
                            t.wait_ge(evA.h, evU_C + ncv - 1)
                        base = cc * (HB + 1)
                        for j in range(4):
                            i = t.matmul(
                                p_a[ncv % 2][:, 0:ST],
                                CVB[:, cc * 512 + j * 128:cc * 512 + (j + 1) * 128],
                                UPT[b % 2][:, base + sub * ST + j:base + sub * ST + j + ST],
                                start=(j == 0), stop=(j == 3))
                        mmA.inc(i)
                        t.wait_ge(evA.h, evU_C + ncv + 1)
                        if cc == 0 and nbs >= 2:
                            t.wait_ge(evB.h, evT_C + nbs - 1)
                        i = t.matmul(p_ts[sub][:, 0:ST],
                                     XPB[:, cc * 160:cc * 160 + 128],
                                     UT[:, (ncv % 4) * ST:(ncv % 4 + 1) * ST],
                                     start=(cc == 0), stop=(cc == NCC - 1))
                        i = t.matmul(p_bc[sub][:, 0:ST],
                                     XPB[:, cc * 160 + 128:cc * 160 + 160],
                                     UT[:, (ncv % 4) * ST:(ncv % 4 + 1) * ST],
                                     start=(cc == 0), stop=(cc == NCC - 1))
                        mmB.inc(i)

        @block.scalar
        def _(sc):
            for b in range(NBLK):
                for sub in range(NSUB):
                    nbs = b * NSUB + sub
                    for cc in range(NCC):
                        ncv = nbs * NCC + cc
                        sc.wait_ge(mmA.h, mmC_C + ncv + 1)
                        if ncv >= 4:
                            sc.wait_ge(stu.h, stuC + (ncv - 3) * 16)
                        i = sc.activation(UT[:, (ncv % 4) * ST:(ncv % 4 + 1) * ST],
                                          p_a[ncv % 2][:, 0:ST], AF.Silu,
                                          bias=convb_sb[:, cc:cc + 1], scale=1.0)
                        evA.inc(i)
                    sc.wait_ge(mmB.h, mmX_C + (nbs + 1) * NCC)
                    if nbs >= 2:
                        sc.wait_ge(stq.h, stqC + (nbs - 1) * 32)
                    i = sc.activation(TSE[:, (nbs % 2) * ST:(nbs % 2 + 1) * ST],
                                      p_ts[sub][:, 0:ST], AF.Copy, scale=1.0)
                    i = sc.activation(BCE[:, (nbs % 2) * ST:(nbs % 2 + 1) * ST],
                                      p_bc[sub][:, 0:ST], AF.Copy, scale=1.0)
                    evB.inc(i)

        @block.gpsimd
        def _(sync):
            for b in range(NBLK):
                for sub in range(NSUB):
                    nbs = b * NSUB + sub
                    for cc in range(NCC):
                        ncv = nbs * NCC + cc
                        sync.wait_ge(evA.h, evU_C + ncv + 1)
                        if ncv >= 1:
                            sync.wait_ge(stu.h, stuC + ncv * 16)
                        sync.dma_start(
                            out=u_d[cc][:, b * BLK + sub * ST:b * BLK + (sub + 1) * ST],
                            in_=UT[:, (ncv % 4) * ST:(ncv % 4 + 1) * ST]
                            ).then_inc(stu.h, 16)
                    t0 = b * BLK + sub * ST
                    sync.wait_ge(evB.h, evT_C + nbs + 1)
                    if nbs >= 1:
                        sync.wait_ge(stq.h, stqC + nbs * 32)
                    slot_dma_out(sync, sh_ts[0][:, t0:t0 + ST],
                                 sh_ts[1][:, t0:t0 + ST],
                                 TSE[:, (nbs % 2) * ST:(nbs % 2 + 1) * ST], stq,
                                 reg=rg_slot)
                    slot_dma_out(sync, sh_bc[0][:, t0:t0 + ST],
                                 sh_bc[1][:, t0:t0 + ST],
                                 BCE[:, (nbs % 2) * ST:(nbs % 2 + 1) * ST], stq,
                                 reg=rg_slot)
            stu.v += NBLK * NSUB * NCC * 16
            sync.wait_ge(stu.h, stu.v)
            sync.wait_ge(stq.h, stq.v)

        @block.sync
        def _(sync):
            sync.wait_ge(stu.h, stu.v)
            sync.wait_ge(stq.h, stq.v)
            flag_write(sync, 0, 1)
        barrier()

    _ns_name, _ns_id = _ns_stack.pop(); nc.leave_named_scope(_ns_name, _ns_id, False)
    _ns_stack.append(("phD", nc.enter_named_scope("phD", False)[0]))
    with contextlib.ExitStack() as ph:
        sba = lambda n, s, d: ph.enter_context(nc.sbuf_tensor(n, s, d))
        WB = sba("WBd", [128, NCC * D], BF16)
        WL = [sba("WLd0", [128, D], F32), sba("WLd1", [128, D], F32)]
        XNT = [sba("XNTd0", [128, NKC * BLK], BF16),
               sba("XNTd1", [128, NKC * BLK], BF16)]
        SGT = sba("SGT", [128, 4 * ST], BF16)
        p_a = [ph.enter_context(nc.psum_tensor("pd_a0", [128, 512], F32)),
               ph.enter_context(nc.psum_tensor("pd_a1", [128, 512], F32))]
        lwD, wcD = lw.v, wc.v

        @block.sync
        def _(sync):
            for pc in range(NCC):
                sync.wait_ge(wc.h, max(wcD, wcD + pc - 1))
                if pc >= 1:
                    sync.wait_ge(lw.h, lwD + pc * 16)
                sync.dma_start(out=WL[pc % 2][:, :], in_=wing[pc]
                               ).then_inc(lw.h, 16)
            lw.v += NCC * 16

        @block.scalar
        def _(sc):
            for pc in range(NCC):
                sc.wait_ge(lw.h, lwD + (pc + 1) * 16)
                i = sc.activation(WB[:, pc * D:(pc + 1) * D], WL[pc % 2][:, :],
                                  AF.Copy, scale=1.0)
                wc.inc(i)

        ldD, mmD, evD, stqD = ld.v, mmA.v, evA.v, stq.v

        @block.sync
        def _(sync):
            for bi in range(TAIL_BLK):
                sync.wait_ge(mmA.h, max(mmD, mmD + (bi - 1) * NCC * NSUB))
                if bi >= 1:
                    sync.wait_ge(ld.h, ldD + bi * NKC * 16)
                for kc in range(NKC):
                    sync.dma_start(out=XNT[bi % 2][:, kc * BLK:(kc + 1) * BLK],
                                   in_=xnt_d[TB0 + bi, kc]).then_inc(ld.h, 16)
            ld.v += TAIL_BLK * NKC * 16

        @block.tensor
        def _(t):
            t.wait_ge(wc.h, wcD + NCC)
            for bi in range(TAIL_BLK):
                t.wait_ge(ld.h, ldD + (bi + 1) * NKC * 16)
                for pc in range(NCC):
                    for sub in range(NSUB):
                        nmm = (bi * NCC + pc) * NSUB + sub
                        if nmm >= 2:
                            t.wait_ge(evA.h, evD + nmm - 1)
                        for kc in range(NKC):
                            i = t.matmul(
                                p_a[nmm % 2][:, 0:ST],
                                WB[:, pc * D + kc * 128:pc * D + (kc + 1) * 128],
                                XNT[bi % 2][:, kc * BLK + sub * ST:kc * BLK + (sub + 1) * ST],
                                start=(kc == 0), stop=(kc == NKC - 1))
                        mmA.inc(i)

        @block.scalar
        def _(sc):
            for nmm in range(TAIL_BLK * NCC * NSUB):
                sc.wait_ge(mmA.h, mmD + nmm + 1)
                if nmm >= 4:
                    sc.wait_ge(stq.h, stqD + (nmm - 3) * 16)
                i = sc.activation(SGT[:, (nmm % 4) * ST:(nmm % 4 + 1) * ST],
                                  p_a[nmm % 2][:, 0:ST], AF.Silu, scale=1.0)
                evA.inc(i)

        @block.gpsimd
        def _(sync):
            for bi in range(TAIL_BLK):
                for pc in range(NCC):
                    for sub in range(NSUB):
                        nmm = (bi * NCC + pc) * NSUB + sub
                        sync.wait_ge(evA.h, evD + nmm + 1)
                        if nmm >= 1:
                            sync.wait_ge(stq.h, stqD + nmm * 16)
                        t0 = bi * BLK + sub * ST
                        sync.dma_start(out=sg_d[pc][:, t0:t0 + ST],
                                       in_=SGT[:, (nmm % 4) * ST:(nmm % 4 + 1) * ST]
                                       ).then_inc(stq.h, 16)
            stq.v += TAIL_BLK * NCC * NSUB * 16
            sync.wait_ge(stq.h, stq.v)
        barrier()

    _ns_name, _ns_id = _ns_stack.pop(); nc.leave_named_scope(_ns_name, _ns_id, False)
    _ns_stack.append(("phE", nc.enter_named_scope("phE", False)[0]))
    with contextlib.ExitStack() as ph:
        sba = lambda n, s, d: ph.enter_context(nc.sbuf_tensor(n, s, d))
        TSA = sba("TSA", [128, 2 * BLK], F32)
        TSS = sba("TSS", [128, BLK], F32)
        TSB = sba("TSBe", [128, BLK], BF16)
        BCA = sba("BCA", [32, 2 * BLK], F32)
        BCS = sba("BCS", [32, BLK], F32)
        BCB = sba("BCBe", [32, BLK], BF16)
        ldE, dvE, evE, stqE = ld.v, dvA.v, evA.v, stq.v

        @block.sync
        def _(sync):
            flag_poll(sync, 0, 1)
            for b in range(NBLK):
                sync.wait_ge(dvA.h, dvE + 2 * b)
                if b >= 1:
                    sync.wait_ge(ld.h, ldE + b * 64)
                t0 = b * BLK
                sync.dma_start(out=TSA[:, 0:BLK],
                               in_=sh_ts[0][:, t0:t0 + BLK]).then_inc(ld.h, 16)
                sync.dma_start(out=TSA[:, BLK:2 * BLK],
                               in_=sh_ts[1][:, t0:t0 + BLK]).then_inc(ld.h, 16)
                sync.dma_start(out=BCA[:, 0:BLK],
                               in_=sh_bc[0][:, t0:t0 + BLK]).then_inc(ld.h, 16)
                sync.dma_start(out=BCA[:, BLK:2 * BLK],
                               in_=sh_bc[1][:, t0:t0 + BLK]).then_inc(ld.h, 16)
            ld.v += NBLK * 64

        @block.vector
        def _(v):
            for b in range(NBLK):
                v.wait_ge(ld.h, ldE + (b + 1) * 64)
                if b >= 1:
                    v.wait_ge(evA.h, evE + 2 * b)
                v.tensor_tensor(TSS[:, :], TSA[:, 0:BLK], TSA[:, BLK:2 * BLK],
                                OP.add)
                i = v.tensor_tensor(BCS[:, :], BCA[:, 0:BLK],
                                    BCA[:, BLK:2 * BLK], OP.add)
                dvA.inc(i, 2)

        @block.scalar
        def _(sc):
            for b in range(NBLK):
                sc.wait_ge(dvA.h, dvE + 2 * (b + 1))
                if b >= 2:
                    sc.wait_ge(stq.h, stqE + (b - 1) * 32)
                i = sc.activation(TSB[:, :], TSS[:, :], AF.Copy, scale=1.0)
                i = sc.activation(BCB[:, :], BCS[:, :], AF.Copy, scale=1.0)
                evA.inc(i, 2)

        @block.gpsimd
        def _(sync):
            for b in range(NBLK):
                sync.wait_ge(evA.h, evE + 2 * (b + 1))
                if b >= 1:
                    sync.wait_ge(stq.h, stqE + b * 32)
                t0 = b * BLK
                sync.dma_start(out=ts_d[:, t0:t0 + BLK], in_=TSB[:, :]
                               ).then_inc(stq.h, 16)
                sync.dma_start(out=bc_d[:, t0:t0 + BLK], in_=BCB[:, :]
                               ).then_inc(stq.h, 16)
            stq.v += NBLK * 32
            sync.wait_ge(stq.h, stq.v)
        barrier()

    _ns_name, _ns_id = _ns_stack.pop(); nc.leave_named_scope(_ns_name, _ns_id, False)
    _ns_stack.append(("phF", nc.enter_named_scope("phF", False)[0]))
    with contextlib.ExitStack() as ph:
        sba = lambda n, s, d: ph.enter_context(nc.sbuf_tensor(n, s, d))
        DTW = sba("DTW", [128, DH], BF16)
        WLf = sba("WLf", [128, DH], F32)
        TSB = [sba("TSBf0", [128, BLK], BF16), sba("TSBf1", [128, BLK], BF16)]
        SPS = [sba("SPS0", [128, 4 * ST], F32), sba("SPS1", [128, 4 * ST], F32)]
        DTT = sba("DTT", [128, 4 * ST], BF16)
        p_a = [ph.enter_context(nc.psum_tensor("pf_a0", [128, 512], F32)),
               ph.enter_context(nc.psum_tensor("pf_a1", [128, 512], F32))]
        lwF, wcF = lw.v, wc.v

        @block.sync
        def _(sync):
            sync.dma_start(out=WLf[:, :], in_=dtwt).then_inc(lw.h, 16)
            lw.v += 16

        @block.scalar
        def _(sc):
            sc.wait_ge(lw.h, lwF + 16)
            i = sc.activation(DTW[:, :], WLf[:, :], AF.Copy, scale=1.0)
            wc.inc(i)

        ldF, mmF, evF, dvF, stqF = ld.v, mmA.v, evA.v, dvA.v, stq.v

        @block.sync
        def _(sync):
            for b in range(NBLK):
                sync.wait_ge(mmA.h, max(mmF, mmF + (b - 1) * NCC * NSUB))
                if b >= 1:
                    sync.wait_ge(ld.h, ldF + b * 16)
                sync.dma_start(out=TSB[b % 2][:, :],
                               in_=ts_d[:, b * BLK:(b + 1) * BLK]
                               ).then_inc(ld.h, 16)
            ld.v += NBLK * 16

        @block.tensor
        def _(t):
            t.wait_ge(wc.h, wcF + 1)
            for b in range(NBLK):
                t.wait_ge(ld.h, ldF + (b + 1) * 16)
                for cc in range(NCC):
                    for sub in range(NSUB):
                        nmm = (b * NCC + cc) * NSUB + sub
                        if nmm >= 2:
                            t.wait_ge(evA.h, evF + 4 * (nmm - 1))
                        i = t.matmul(p_a[nmm % 2][:, 0:ST],
                                     DTW[:, cc * 128:(cc + 1) * 128],
                                     TSB[b % 2][:, sub * ST:(sub + 1) * ST],
                                     start=True, stop=True)
                        mmA.inc(i)

        @block.scalar
        def _(sc):
            for b in range(NBLK):
                for cc in range(NCC):
                    for sub in range(NSUB):
                        nmm = (b * NCC + cc) * NSUB + sub
                        sc.wait_ge(mmA.h, mmF + nmm + 1)
                        if nmm >= 1:
                            sc.wait_ge(dvA.h, dvF + nmm)
                        sl = SPS[nmm % 2]
                        bias = dtb_sb[:, cc:cc + 1]
                        sc.activation(sl[:, 0:ST], p_a[nmm % 2][:, 0:ST],
                                      AF.Abs, bias=bias, scale=1.0)
                        sc.activation(sl[:, ST:2 * ST], sl[:, 0:ST], AF.Exp,
                                      scale=-1.0)
                        sc.activation(sl[:, 2 * ST:3 * ST], sl[:, ST:2 * ST],
                                      AF.Ln, bias=1.0, scale=1.0)
                        i = sc.activation(sl[:, 3 * ST:4 * ST],
                                          p_a[nmm % 2][:, 0:ST],
                                          AF.Relu, bias=bias, scale=1.0)
                        evA.inc(i, 4)

        @block.vector
        def _(v):
            for b in range(NBLK):
                for cc in range(NCC):
                    for sub in range(NSUB):
                        nmm = (b * NCC + cc) * NSUB + sub
                        v.wait_ge(evA.h, evF + 4 * (nmm + 1))
                        if nmm >= 4:
                            v.wait_ge(stq.h, stqF + (nmm - 3) * 16)
                        sl = SPS[nmm % 2]
                        dst = DTT[:, (nmm % 4) * ST:(nmm % 4 + 1) * ST]
                        v.tensor_tensor(dst, sl[:, 2 * ST:3 * ST],
                                        sl[:, 3 * ST:4 * ST], OP.add)
                        i = v.tensor_scalar(dst, dst, mask_sb[:, b:b + 1],
                                            None, OP.mult)
                        dvA.inc(i)

        @block.gpsimd
        def _(sync):
            for b in range(NBLK):
                for cc in range(NCC):
                    for sub in range(NSUB):
                        nmm = (b * NCC + cc) * NSUB + sub
                        sync.wait_ge(dvA.h, dvF + nmm + 1)
                        if nmm >= 1:
                            sync.wait_ge(stq.h, stqF + nmm * 16)
                        t0 = b * BLK + sub * ST
                        sync.dma_start(out=dt_d[cc][:, t0:t0 + ST],
                                       in_=DTT[:, (nmm % 4) * ST:(nmm % 4 + 1) * ST]
                                       ).then_inc(stq.h, 16)
            stq.v += NBLK * NCC * NSUB * 16
            sync.wait_ge(stq.h, stq.v)
        barrier()

    _ns_name, _ns_id = _ns_stack.pop(); nc.leave_named_scope(_ns_name, _ns_id, False)
    _ns_stack.append(("phG1", nc.enter_named_scope("phG1", False)[0]))
    with contextlib.ExitStack() as ph:
        sba = lambda n, s, d: ph.enter_context(nc.sbuf_tensor(n, s, d))
        BBC = sba("BBCg1", [128, NST * BLK], BF16)
        BCT = [sba("BCTg10", [32, BLK], BF16), sba("BCTg11", [32, BLK], BF16)]
        DTU = [sba("DTUg10", [128, BLK], BF16), sba("DTUg11", [128, BLK], BF16)]
        UU = [sba("UUg10", [128, BLK], BF16), sba("UUg11", [128, BLK], BF16)]
        DU = [sba("DUg10", [128, BLK], BF16), sba("DUg11", [128, BLK], BF16)]
        DBU = [sba("DBUg10", [128, BLK], BF16), sba("DBUg11", [128, BLK], BF16)]
        EE = [sba("EEg10", [128, BLK], BF16), sba("EEg11", [128, BLK], BF16)]
        VV = sba("VVg1", [128, BLK], BF16)
        WW = [sba("WWg10", [128, BLK], BF16), sba("WWg11", [128, BLK], BF16)]
        ONES = sba("ONESg1", [128, BLK], BF16)
        CS = [sba("CSg10", [128, BLK], F32), sba("CSg11", [128, BLK], F32)]
        RR = [sba("RRg10", [128, BLK], F32), sba("RRg11", [128, BLK], F32)]
        TOT = sba("TOTg1", [128, 2], F32)
        PFP = [sba("PFPg10", [128, NST], F32), sba("PFPg11", [128, NST], F32)]
        PEX = [sba("PEXg10", [128, NST], F32), sba("PEXg11", [128, NST], F32)]
        QQ = [sba("QQg10", [128, NST], F32), sba("QQg11", [128, NST], F32)]
        SCR = sba("SCRg1", [128, NST], F32)
        p_a = [ph.enter_context(nc.psum_tensor("pg1_a0", [128, 512], F32)),
               ph.enter_context(nc.psum_tensor("pg1_a1", [128, 512], F32))]

        POOL_NS = [n for n in range(NST) if n not in (5, 10, 15)]
        PK = {n: k for k, n in enumerate(POOL_NS)}
        NPN = len(POOL_NS)
        U1 = TB0 * NCC
        ld1, lw1, mm1, ev1, evB1 = ld.v, lw.v, mmA.v, evA.v, evB.v
        dv1, dvB1, hs1, gq1 = dvA.v, dvB.v, hs.v, gq.v
        DVR = lambda j: dv1 + 1 + 2 * j + 2
        DVU = lambda j: dv1 + 1 + 2 * j + 1

        @block.sync
        def _(sync):
            for b in range(TB0):
                sync.wait_ge(mmA.h, mm1 + (32 * (b - 1) if b >= 2 else 0))
                if b >= 1:
                    sync.wait_ge(lw.h, lw1 + b * 16)
                sync.dma_start(out=BCT[b % 2][:, :],
                               in_=bc_d[:, b * BLK:(b + 1) * BLK]
                               ).then_inc(lw.h, 16)
                for cc in range(NCC):
                    j = b * NCC + cc
                    if j >= 2:
                        sync.wait_ge(dvA.h, DVR(j - 2))
                    if j >= 1:
                        sync.wait_ge(ld.h, ld1 + 32 * j)
                    sync.dma_start(out=DTU[j % 2][:, :],
                                   in_=dt_d[cc][:, b * BLK:(b + 1) * BLK]
                                   ).then_inc(ld.h, 16)
                    sync.dma_start(out=UU[j % 2][:, :],
                                   in_=u_d[cc][:, b * BLK:(b + 1) * BLK]
                                   ).then_inc(ld.h, 16)
            ld.v += U1 * 32
            lw.v += TB0 * 16

        @block.tensor
        def _(t):
            nb1 = 0
            for b in range(TB0):
                t.wait_ge(lw.h, lw1 + (b + 1) * 16)
                if b >= 1:
                    t.wait_ge(gq.h, gq1 + NPN * b * NCC)
                    t.wait_ge(hs.h, hs1 + NST * b * NCC)
                for n in range(NST):
                    for sub in range(NSUB):
                        if nb1 >= 2:
                            t.wait_ge(evA.h, ev1 + nb1 - 1)
                        i = t.matmul(p_a[nb1 % 2][:, 0:ST],
                                     sel_sb[:, n * 128:(n + 1) * 128],
                                     BCT[b % 2][:, sub * ST:(sub + 1) * ST],
                                     start=True, stop=True)
                        mmA.inc(i)
                        nb1 += 1

        @block.scalar
        def _(sc):
            ne1 = 0
            for b in range(TB0):
                for n in range(NST):
                    for sub in range(NSUB):
                        sc.wait_ge(mmA.h, mm1 + ne1 + 1)
                        i = sc.activation(
                            BBC[:, n * BLK + sub * ST:n * BLK + (sub + 1) * ST],
                            p_a[ne1 % 2][:, 0:ST], AF.Copy, scale=1.0)
                        evA.inc(i)
                        ne1 += 1
                for cc in range(NCC):
                    j = b * NCC + cc
                    sc.wait_ge(dvA.h, DVR(j))
                    if j >= 2:
                        sc.wait_ge(dvB.h, dvB1 + (j - 1))
                    i = sc.activation(PEX[j % 2][:, :], PFP[j % 2][:, :],
                                      AF.Exp, scale=1.0)
                    evB.inc(i)
                    for n in range(NST):
                        if n >= 2:
                            sc.wait_ge(hs.h, hs1 + NST * j + n - 1)
                        i = sc.activation(
                            EE[n % 2][:, :], RR[j % 2][:, :], AF.Exp,
                            scale=a_sb[:, cc * NST + n:cc * NST + n + 1])
                        evB.inc(i)

        @block.vector
        def _(v):
            i = v.memset(ONES[:, :], 1.0)
            dvA.inc(i)
            v.wait_ge(dvA.h, dv1 + 1)
            for b in range(TB0):
                for cc in range(NCC):
                    j = b * NCC + cc
                    v.wait_ge(ld.h, ld1 + 32 * (j + 1))
                    if j >= 2:
                        v.wait_ge(gq.h, gq1 + NPN * (j - 1))
                        v.wait_ge(evB.h, evB1 + 17 * (j - 1))
                    i = v.tensor_tensor(DU[j % 2][:, :], DTU[j % 2][:, :],
                                        UU[j % 2][:, :], OP.mult)
                    dvA.inc(i)
                    v.reduce_sum(TOT[:, j % 2:j % 2 + 1], DTU[j % 2][:, :],
                                 axis=mybir.AxisListType.X)
                    v.tensor_tensor_scan(CS[j % 2][:, :], ONES[:, :],
                                         DTU[j % 2][:, :], 0.0,
                                         OP.mult, OP.add)
                    v.tensor_scalar(RR[j % 2][:, :], CS[j % 2][:, :], -1.0,
                                    TOT[:, j % 2:j % 2 + 1], OP.mult, OP.add)
                    i = v.tensor_scalar(PFP[j % 2][:, :],
                                        a_sb[:, cc * NST:(cc + 1) * NST],
                                        TOT[:, j % 2:j % 2 + 1], None,
                                        OP.mult)
                    dvA.inc(i)
                    v.wait_ge(evA.h, ev1 + 32 * (b + 1))
                    for n in range(NST):
                        v.wait_ge(evB.h, evB1 + 17 * j + n + 2)
                        if n in PK:
                            k = PK[n]
                            v.wait_ge(gq.h, gq1 + NPN * j + k + 1)
                            i = v.scalar_tensor_tensor(
                                WW[n % 2][:, :], EE[n % 2][:, :], 1.0,
                                DBU[k % 2][:, :], OP.mult, OP.mult,
                                accum_out=QQ[j % 2][:, n:n + 1])
                        else:
                            v.tensor_tensor(VV[:, :], EE[n % 2][:, :],
                                            DU[j % 2][:, :], OP.mult)
                            i = v.scalar_tensor_tensor(
                                WW[n % 2][:, :], VV[:, :], 1.0,
                                BBC[:, n * BLK:(n + 1) * BLK], OP.mult,
                                OP.mult, accum_out=QQ[j % 2][:, n:n + 1])
                        hs.inc(i)
                    v.wait_ge(hs.h, hs.v)
                    if j >= NCC:
                        v.wait_ge(dvB.h, dvB1 + (j - NCC) + 1)
                    c16 = state_sb[:, cc * NST:(cc + 1) * NST]
                    v.tensor_tensor(SCR[:, :], c16, PEX[j % 2][:, :], OP.mult)
                    i = v.tensor_tensor(c16, SCR[:, :], QQ[j % 2][:, :],
                                        OP.add)
                    dvB.inc(i)

        @block.gpsimd
        def _(g):
            for b in range(TB0):
                for cc in range(NCC):
                    j = b * NCC + cc
                    g.wait_ge(dvA.h, DVU(j))
                    g.wait_ge(evA.h, ev1 + 32 * (b + 1))
                    for k, n in enumerate(POOL_NS):
                        if j * NPN + k >= 2:
                            pj, pk = divmod(j * NPN + k - 2, NPN)
                            g.wait_ge(hs.h,
                                      hs1 + NST * pj + POOL_NS[pk] + 1)
                        i = g.tensor_tensor(DBU[k % 2][:, :], DU[j % 2][:, :],
                                            BBC[:, n * BLK:(n + 1) * BLK],
                                            OP.mult)
                        gq.inc(i)
        barrier()

    _ns_name, _ns_id = _ns_stack.pop(); nc.leave_named_scope(_ns_name, _ns_id, False)
    _ns_stack.append(("phG2", nc.enter_named_scope("phG2", False)[0]))
    with contextlib.ExitStack() as ph:
        sba = lambda n, s, d: ph.enter_context(nc.sbuf_tensor(n, s, d))
        BBC = sba("BBC", [128, NST * BLK], BF16)
        CBC = sba("CBC", [128, NST * BLK], BF16)
        BCT = [sba("BCT0", [32, BLK], BF16), sba("BCT1", [32, BLK], BF16)]
        DTU = [sba("DTU0", [128, BLK], BF16), sba("DTU1", [128, BLK], BF16)]
        UU = [sba("UU0", [128, BLK], BF16), sba("UU1", [128, BLK], BF16)]
        SGU = [sba("SGU0", [128, BLK], BF16), sba("SGU1", [128, BLK], BF16)]
        DU = sba("DU", [128, BLK], BF16)
        DBU = [sba("DBU0", [128, BLK], BF16), sba("DBU1", [128, BLK], BF16)]
        SS = [sba("SS0", [128, BLK], BF16), sba("SS1", [128, BLK], BF16)]
        SCT = [sba("SCT0", [128, BLK], BF16), sba("SCT1", [128, BLK], BF16)]
        YG = [sba("YG0", [128, BLK], BF16), sba("YG1", [128, BLK], BF16)]
        DA = [sba("DA0", [128, BLK], F32), sba("DA1", [128, BLK], F32)]
        YY = sba("YY", [128, BLK], F32)
        p_a = [ph.enter_context(nc.psum_tensor("pg_a0", [128, 512], F32)),
               ph.enter_context(nc.psum_tensor("pg_a1", [128, 512], F32))]
        BH = BLK // 2
        p_y = [[ph.enter_context(nc.psum_tensor("pg_y00", [128, BH], F32)),
                ph.enter_context(nc.psum_tensor("pg_y01", [128, BH], F32))],
               [ph.enter_context(nc.psum_tensor("pg_y10", [128, BH], F32)),
                ph.enter_context(nc.psum_tensor("pg_y11", [128, BH], F32))]]

        U2 = TAIL_BLK * NCC
        ld2, lw2, mm2, ev2, ac2 = ld.v, lw.v, mmA.v, evA.v, evB.v
        dv2, dv2Y, hs2, gq2, stq2 = dvA.v, dvB.v, hs.v, gq.v, stq.v
        evC2, mmY2 = evC.v, mmY.v
        W_DU2 = lambda j: dv2 + 1 if j == 0 else dv2 + 17 * j

        @block.sync
        def _(sync):
            for bi in range(TAIL_BLK):
                b = TB0 + bi
                sync.wait_ge(mmA.h, mm2 + (64 * (bi - 1) if bi >= 2 else 0))
                if bi >= 1:
                    sync.wait_ge(lw.h, lw2 + bi * 16)
                sync.dma_start(out=BCT[bi % 2][:, :],
                               in_=bc_d[:, b * BLK:(b + 1) * BLK]
                               ).then_inc(lw.h, 16)
                for cc in range(NCC):
                    j = bi * NCC + cc
                    if j >= 2:
                        sync.wait_ge(dvA.h, W_DU2(j - 2))
                        sync.wait_ge(evB.h, ac2 + (j - 1) * NST)
                        sync.wait_ge(dvB.h, dv2Y + j - 1)
                    if j >= 1:
                        sync.wait_ge(ld.h, ld2 + 48 * j)
                    sync.dma_start(out=DTU[j % 2][:, :],
                                   in_=dt_d[cc][:, b * BLK:(b + 1) * BLK]
                                   ).then_inc(ld.h, 16)
                    sync.dma_start(out=UU[j % 2][:, :],
                                   in_=u_d[cc][:, b * BLK:(b + 1) * BLK]
                                   ).then_inc(ld.h, 16)
                    sync.dma_start(out=SGU[j % 2][:, :],
                                   in_=sg_d[cc][:, bi * BLK:(bi + 1) * BLK]
                                   ).then_inc(ld.h, 16)
            ld.v += U2 * 48
            lw.v += TAIL_BLK * 16

        @block.tensor
        def _(t):
            nbc = 0
            for bi in range(TAIL_BLK):
                t.wait_ge(lw.h, lw2 + (bi + 1) * 16)
                if bi >= 1:
                    t.wait_ge(gq.h, gq2 + 16 * bi * NCC)
                    t.wait_ge(evC.h, evC2 + bi * NCC * NST)
                for rep in range(2):
                    for n in range(NST):
                        for sub in range(NSUB):
                            if nbc >= 2:
                                t.wait_ge(evA.h, ev2 + nbc - 1)
                            selcol = (rep * NST + n) * 128
                            i = t.matmul(
                                p_a[nbc % 2][:, 0:ST],
                                sel_sb[:, selcol:selcol + 128],
                                BCT[bi % 2][:, sub * ST:(sub + 1) * ST],
                                start=True, stop=True)
                            mmA.inc(i)
                            nbc += 1
                for cc in range(NCC):
                    j = bi * NCC + cc
                    if j >= 2:
                        t.wait_ge(dvB.h, dv2Y + j - 1)
                    for n in range(NST):
                        t.wait_ge(evC.h, evC2 + j * NST + n + 1)
                        for h in range(2):
                            i = t.matmul(p_y[j % 2][h][:, :], idnb[:, :],
                                         SCT[n % 2][:, h * BH:(h + 1) * BH],
                                         start=(n == 0), stop=(n == NST - 1))
                            mmY.inc(i)

        @block.scalar
        def _(sc):
            nev = 0
            for bi in range(TAIL_BLK):
                for rep in range(2):
                    dstA = BBC if rep == 0 else CBC
                    for n in range(NST):
                        for sub in range(NSUB):
                            sc.wait_ge(mmA.h, mm2 + nev + 1)
                            i = sc.activation(
                                dstA[:, n * BLK + sub * ST:
                                     n * BLK + (sub + 1) * ST],
                                p_a[nev % 2][:, 0:ST], AF.Copy, scale=1.0)
                            evA.inc(i)
                            nev += 1
                for cc in range(NCC):
                    j = bi * NCC + cc
                    sc.wait_ge(ld.h, ld2 + 48 * (j + 1))
                    for n in range(NST):
                        g = j * NST + n
                        if g >= 2:
                            sc.wait_ge(hs.h, hs2 + g - 1)
                        i = sc.activation(
                            DA[g % 2][:, :], DTU[j % 2][:, :], AF.Exp,
                            scale=a_sb[:, cc * NST + n:cc * NST + n + 1])
                        evB.inc(i)

        @block.vector
        def _(v):
            v.wait_ge(dvB.h, dv2Y)
            v.wait_ge(ld.h, ld2 + 48)
            i = v.tensor_tensor(DU[:, :], DTU[0][:, :], UU[0][:, :], OP.mult)
            dvA.inc(i)
            for bi in range(TAIL_BLK):
                for cc in range(NCC):
                    j = bi * NCC + cc
                    g0 = j * NST
                    v.wait_ge(evA.h, ev2 + 64 * (bi + 1))
                    for n in range(NST):
                        g = g0 + n
                        v.wait_ge(gq.h, gq2 + 16 * j + n + 1)
                        v.wait_ge(evB.h, ac2 + g + 1)
                        i = v.tensor_tensor_scan(
                            SS[n % 2][:, :], DA[g % 2][:, :], DBU[n % 2][:, :],
                            state_sb[:, cc * NST + n:cc * NST + n + 1],
                            OP.mult, OP.add)
                        hs.inc(i)
                        if n == 0:
                            if j >= 1:
                                ccp = (j - 1) % NCC
                                i = v.tensor_copy(
                                    state_sb[:, ccp * NST + NST - 1:
                                             ccp * NST + NST],
                                    SS[1][:, BLK - 1:BLK])
                                dvA.inc(i)
                        else:
                            m = n - 1
                            i = v.tensor_copy(
                                state_sb[:, cc * NST + m:cc * NST + m + 1],
                                SS[m % 2][:, BLK - 1:BLK])
                            dvA.inc(i)
                            if m >= 2:
                                v.wait_ge(mmY.h,
                                          mmY2 + j * 2 * NST + 2 * (m - 1))
                            i = v.tensor_tensor(
                                SCT[m % 2][:, :], SS[m % 2][:, :],
                                CBC[:, m * BLK:(m + 1) * BLK], OP.mult)
                            evC.inc(i)
                    if j < U2 - 1:
                        v.wait_ge(ld.h, ld2 + 48 * (j + 2))
                        i = v.tensor_tensor(DU[:, :], DTU[(j + 1) % 2][:, :],
                                            UU[(j + 1) % 2][:, :], OP.mult)
                        dvA.inc(i)
                    m = NST - 1
                    v.wait_ge(hs.h, hs.v)
                    v.wait_ge(mmY.h, mmY2 + j * 2 * NST + 2 * (m - 1))
                    i = v.tensor_tensor(
                        SCT[m % 2][:, :], SS[m % 2][:, :],
                        CBC[:, m * BLK:(m + 1) * BLK], OP.mult)
                    evC.inc(i)
                    v.wait_ge(mmY.h, mmY2 + (j + 1) * 2 * NST)
                    if j >= 2:
                        v.wait_ge(stq.h, stq2 + (j - 1) * 16)
                    v.scalar_tensor_tensor(YY[:, 0:BH], UU[j % 2][:, 0:BH],
                                           dpar_sb[:, cc:cc + 1],
                                           p_y[j % 2][0][:, :],
                                           OP.mult, OP.add)
                    v.scalar_tensor_tensor(YY[:, BH:BLK],
                                           UU[j % 2][:, BH:BLK],
                                           dpar_sb[:, cc:cc + 1],
                                           p_y[j % 2][1][:, :],
                                           OP.mult, OP.add)
                    v.wait_ge(ld.h, ld2 + 48 * (j + 1))
                    i = v.tensor_tensor(YG[j % 2][:, :], YY[:, :],
                                        SGU[j % 2][:, :], OP.mult)
                    dvB.inc(i)
                    if j == U2 - 1:
                        i = v.tensor_copy(
                            state_sb[:, cc * NST + NST - 1:cc * NST + NST],
                            SS[1][:, BLK - 1:BLK])
                        dvA.inc(i)

        @block.gpsimd
        def _(sync):
            for bi in range(TAIL_BLK):
                for cc in range(NCC):
                    j = bi * NCC + cc
                    g0 = j * NST
                    sync.wait_ge(dvA.h, W_DU2(j))
                    sync.wait_ge(evA.h, ev2 + 64 * (bi + 1))
                    for n in range(NST):
                        g = g0 + n
                        if n == 2 and j >= 1:
                            m = j - 1
                            sync.wait_ge(dvB.h, dv2Y + m + 1)
                            if m >= 1:
                                sync.wait_ge(stq.h, stq2 + m * 16)
                            ccp = m % NCC
                            t0 = (m // NCC) * BLK
                            slot_dma_out(sync, sh_yg[0, ccp][:, t0:t0 + BLK],
                                         sh_yg[1, ccp][:, t0:t0 + BLK],
                                         YG[m % 2][:, :], stq, reg=rg_slot)
                        if g >= 2:
                            sync.wait_ge(hs.h, hs2 + g - 1)
                        i = sync.tensor_tensor(DBU[n % 2][:, :], DU[:, :],
                                               BBC[:, n * BLK:(n + 1) * BLK],
                                               OP.mult)
                        gq.inc(i)
            m = U2 - 1
            sync.wait_ge(dvB.h, dv2Y + m + 1)
            sync.wait_ge(stq.h, stq2 + m * 16)
            ccp = m % NCC
            t0 = (m // NCC) * BLK
            slot_dma_out(sync, sh_yg[0, ccp][:, t0:t0 + BLK],
                         sh_yg[1, ccp][:, t0:t0 + BLK],
                         YG[m % 2][:, :], stq, reg=rg_slot)
            sync.wait_ge(stq.h, stq.v)

        @block.sync
        def _(sync):
            sync.wait_ge(stq.h, stq.v)
            flag_write(sync, 1, 2)
        barrier()

    _ns_name, _ns_id = _ns_stack.pop(); nc.leave_named_scope(_ns_name, _ns_id, False)
    _ns_stack.append(("phH", nc.enter_named_scope("phH", False)[0]))
    with contextlib.ExitStack() as ph:
        sba = lambda n, s, d: ph.enter_context(nc.sbuf_tensor(n, s, d))
        NKG = 2 * NCC
        YGR = sba("YGR", [128, NKG * TLH], BF16)
        WOB = sba("WOB", [128, NKG * 128], BF16)
        WLh = [sba("WLh0", [128, NKG * 128], F32),
               sba("WLh1", [128, NKG * 128], F32)]
        SEV = sba("SEV", [128, 4 * ST], F32)
        p_a = [ph.enter_context(nc.psum_tensor("ph_a0", [128, 512], F32)),
               ph.enter_context(nc.psum_tensor("ph_a1", [128, 512], F32))]
        ldH, lwH, wcH, mmH, evH, stqH = ld.v, lw.v, wc.v, mmA.v, evA.v, stq.v
        NTT = TLH // ST

        @block.sync
        def _(sync):
            flag_poll(sync, 1, 2)
            for ss in range(2):
                for cc in range(NCC):
                    kc = ss * NCC + cc
                    slot_dma_in(sync, YGR[:, kc * TLH:(kc + 1) * TLH],
                                sh_yg[ss, cc][:, 0:TLH],
                                sh_yg[ss, cc][:, TLH:2 * TLH], ld)
            for hp in range(NHC):
                sync.wait_ge(wc.h, max(wcH, wcH + hp - 1))
                if hp >= 1:
                    sync.wait_ge(lw.h, lwH + hp * 16)
                sync.dma_start(out=WLh[hp % 2][:, :], in_=woutt[hp]
                               ).then_inc(lw.h, 16)
            lw.v += NHC * 16

        @block.scalar
        def _(sc):
            for hp in range(NHC):
                sc.wait_ge(lw.h, lwH + (hp + 1) * 16)
                if hp >= 1:
                    sc.wait_ge(mmA.h, mmH + hp * NTT)
                i = sc.activation(WOB[:, :], WLh[hp % 2][:, :], AF.Copy,
                                  scale=1.0)
                wc.inc(i)
                for tt in range(NTT):
                    ntile = hp * NTT + tt
                    sc.wait_ge(mmA.h, mmH + ntile + 1)
                    if ntile >= 4:
                        sc.wait_ge(stq.h, stqH + (ntile - 3) * 16)
                    i = sc.activation(
                        SEV[:, (ntile % 4) * ST:(ntile % 4 + 1) * ST],
                        p_a[ntile % 2][:, 0:ST], AF.Copy, scale=1.0)
                    evA.inc(i)

        @block.tensor
        def _(t):
            t.wait_ge(ld.h, ldH + NKG * 16)
            for hp in range(NHC):
                t.wait_ge(wc.h, wcH + hp + 1)
                for tt in range(NTT):
                    ntile = hp * NTT + tt
                    if ntile >= 2:
                        t.wait_ge(evA.h, evH + ntile - 1)
                    for kc in range(NKG):
                        i = t.matmul(p_a[ntile % 2][:, 0:ST],
                                     WOB[:, kc * 128:(kc + 1) * 128],
                                     YGR[:, kc * TLH + tt * ST:kc * TLH + (tt + 1) * ST],
                                     start=(kc == 0), stop=(kc == NKG - 1))
                    mmA.inc(i)

        @block.gpsimd
        def _(sync):
            for hp in range(NHC):
                for tt in range(NTT):
                    ntile = hp * NTT + tt
                    sync.wait_ge(evA.h, evH + ntile + 1)
                    if ntile >= 1:
                        sync.wait_ge(stq.h, stqH + ntile * 16)
                    sync.dma_start(out=s_d[hp][:, tt * ST:(tt + 1) * ST],
                                   in_=SEV[:, (ntile % 4) * ST:(ntile % 4 + 1) * ST]
                                   ).then_inc(stq.h, 16)
            stq.v += NHC * NTT * 16
            sync.wait_ge(stq.h, stq.v)
        barrier()

    _ns_name, _ns_id = _ns_stack.pop(); nc.leave_named_scope(_ns_name, _ns_id, False)
    _ns_stack.append(("phI", nc.enter_named_scope("phI", False)[0]))
    NSB = max(1, TLH // BLK)
    SBS = TLH // NSB
    NS2 = SBS // ST
    for sb_i in range(NSB):
        with contextlib.ExitStack() as ph:
            sba = (lambda sfx: (lambda n, s, d: ph.enter_context(
                nc.sbuf_tensor(n + sfx, s, d))))(f"_sb{sb_i}")
            XNS = sba("XNS", [128, NKC * SBS], BF16)
            HDN = sba("HDN", [128, NIC * SBS], BF16)
            GWB = sba("GWB", [128, NKC * 128], BF16)
            UWB = sba("UWB", [128, NKC * 128], BF16)
            DWB = sba("DWB", [128, NIC * 128], BF16)
            GUS = sba("GUS", [128, 4 * ST], F32)
            SGF = sba("SGF", [128, 2 * ST], F32)
            SRES = sba("SRES", [128, 2 * ST], F32)
            p_g = ph.enter_context(nc.psum_tensor(f"pi_g{sb_i}", [128, 512], F32))
            p_u = ph.enter_context(nc.psum_tensor(f"pi_u{sb_i}", [128, 512], F32))
            p_d = [ph.enter_context(nc.psum_tensor(f"pi_d0{sb_i}", [128, 512], F32)),
                   ph.enter_context(nc.psum_tensor(f"pi_d1{sb_i}", [128, 512], F32))]

            ldI, lwI, wcI, lgI = ld.v, lw.v, wc.v, lg.v
            mmI, evI, dvI = mmA.v, evA.v, dvA.v
            sA = F - TL + 0 * TLH + sb_i * SBS
            sB = F - TL + 1 * TLH + sb_i * SBS

            @block.sync
            def _(sync):
                for kc in range(NKC):
                    bA, cA = sA // BLK, sA % BLK
                    bB, cB = sB // BLK, sB % BLK
                    slot_dma_in(sync, XNS[:, kc * SBS:(kc + 1) * SBS],
                                xnt_d[bA, kc][:, cA:cA + SBS],
                                xnt_d[bB, kc][:, cB:cB + SBS], ld)

            @block.gpsimd
            def _(g):
                for ic in range(NIC):
                    if ic >= 1:
                        g.wait_ge(mmA.h, mmI + 2 * ic * NS2)
                        g.wait_ge(lg.h, lgI + 32 * ic)
                    g.dma_start(out=GWB[:, :], in_=gatet[ic]).then_inc(lg.h, 16)
                    g.wait_ge(lg.h, lgI + 32 * ic + 16)
                    g.dma_start(out=UWB[:, :], in_=upt[ic]).then_inc(lg.h, 16)
                lg.v += NIC * 32

            @block.scalar
            def _(sc):
                for ic in range(NIC):
                    for s2 in range(NS2):
                        nmm = ic * NS2 + s2
                        sc.wait_ge(mmA.h, mmI + 2 * nmm + 1)
                        if nmm >= 1:
                            sc.wait_ge(dvA.h, dvI + nmm)
                        sc.activation(GUS[:, 0:ST], p_g[:, 0:ST], AF.Copy,
                                      scale=fsc_sb[:, ic:ic + 1])
                        i = sc.activation(GUS[:, ST:2 * ST], GUS[:, 0:ST],
                                          AF.Silu, scale=1.0)
                        evA.inc(i)
                        sc.wait_ge(mmA.h, mmI + 2 * nmm + 2)
                        i = sc.activation(
                            GUS[:, 2 * ST:3 * ST], p_u[:, 0:ST], AF.Copy,
                            scale=fsc_sb[:, NIC + ic:NIC + ic + 1])
                        evA.inc(i)

            @block.tensor
            def _(t):
                t.wait_ge(ld.h, ldI + NKC * 16)
                for ic in range(NIC):
                    for s2 in range(NS2):
                        nmm = ic * NS2 + s2
                        t.wait_ge(lg.h, lgI + 32 * ic + 16)
                        if nmm >= 1:
                            t.wait_ge(evA.h, evI + 2 * nmm - 1)
                        for kc in range(NKC):
                            i = t.matmul(
                                p_g[:, 0:ST], GWB[:, kc * 128:(kc + 1) * 128],
                                XNS[:, kc * SBS + s2 * ST:kc * SBS + (s2 + 1) * ST],
                                start=(kc == 0), stop=(kc == NKC - 1))
                        mmA.inc(i)
                        t.wait_ge(lg.h, lgI + 32 * ic + 32)
                        if nmm >= 1:
                            t.wait_ge(evA.h, evI + 2 * nmm)
                        for kc in range(NKC):
                            i = t.matmul(
                                p_u[:, 0:ST], UWB[:, kc * 128:(kc + 1) * 128],
                                XNS[:, kc * SBS + s2 * ST:kc * SBS + (s2 + 1) * ST],
                                start=(kc == 0), stop=(kc == NKC - 1))
                        mmA.inc(i)


            @block.vector
            def _(v):
                for ic in range(NIC):
                    for s2 in range(NS2):
                        nmm = ic * NS2 + s2
                        v.wait_ge(evA.h, evI + 2 * nmm + 2)
                        i = v.tensor_tensor(
                            HDN[:, ic * SBS + s2 * ST:ic * SBS + (s2 + 1) * ST],
                            GUS[:, ST:2 * ST], GUS[:, 2 * ST:3 * ST], OP.mult)
                        dvA.inc(i)

            ldI2, lwI2, wcI2, lgI2 = ld.v, lw.v, wc.v, lg.v
            mmT, evT, evS, dvS, stqI = mmB.v, evB.v, evC.v, dvB.v, stq.v

            @block.gpsimd
            def _(g):
                def sgf_store(nmm):
                    hc2, s22 = divmod(nmm, NS2)
                    g.wait_ge(dvB.h, dvS + nmm + 1)
                    if nmm >= 1:
                        g.wait_ge(stq.h, stqI + nmm * 16)
                    g.dma_start(
                        out=sgf_d[hc2][:, sb_i * SBS + s22 * ST:sb_i * SBS + (s22 + 1) * ST],
                        in_=SGF[:, (nmm % 2) * ST:(nmm % 2 + 1) * ST]
                        ).then_inc(stq.h, 16)

                for hc in range(NHC):
                    if hc >= 1:
                        g.wait_ge(mmB.h, mmT + hc * NS2)
                        g.wait_ge(lg.h, lgI2 + hc * 16)
                    g.dma_start(out=DWB[:, :], in_=downt[hc]).then_inc(lg.h, 16)
                    if hc >= 1:
                        for s2 in range(NS2):
                            sgf_store((hc - 1) * NS2 + s2)
                for s2 in range(NS2):
                    sgf_store((NHC - 1) * NS2 + s2)
                stq.v += NHC * NS2 * 16
                g.wait_ge(stq.h, stq.v)
                lg.v += NHC * 16

            @block.sync
            def _(sync):
                for hc in range(NHC):
                    for s2 in range(NS2):
                        nmm = hc * NS2 + s2
                        sync.wait_ge(dvB.h, max(dvS, dvS + nmm - 1))
                        if nmm >= 1:
                            sync.wait_ge(ld.h, ldI2 + nmm * 16)
                        sync.dma_start(
                            out=SRES[:, (nmm % 2) * ST:(nmm % 2 + 1) * ST],
                            in_=s_d[hc][:, sb_i * SBS + s2 * ST:sb_i * SBS + (s2 + 1) * ST]
                            ).then_inc(ld.h, 16)
                ld.v += NHC * NS2 * 16

            @block.scalar
            def _(sc):
                for hc in range(NHC):
                    for s2 in range(NS2):
                        nmm = hc * NS2 + s2
                        sc.wait_ge(mmB.h, mmT + nmm + 1)
                        if nmm >= 1:
                            sc.wait_ge(dvB.h, dvS + nmm)
                        sc.activation(
                            GUS[:, 3 * ST:4 * ST], p_d[nmm % 2][:, 0:ST],
                            AF.Copy,
                            scale=fsc_sb[:, 2 * NIC + hc:2 * NIC + hc + 1])
                        i = sc.activation(GUS[:, 3 * ST:4 * ST],
                                          GUS[:, 3 * ST:4 * ST], AF.Silu,
                                          scale=1.0)
                        evB.inc(i)

            @block.tensor
            def _(t):
                t.wait_ge(dvA.h, dvI + NIC * NS2)
                for hc in range(NHC):
                    t.wait_ge(lg.h, lgI2 + (hc + 1) * 16)
                    for s2 in range(NS2):
                        nmm = hc * NS2 + s2
                        if nmm >= 2:
                            t.wait_ge(evB.h, evT + nmm - 1)
                        for ic in range(NIC):
                            i = t.matmul(
                                p_d[nmm % 2][:, 0:ST],
                                DWB[:, ic * 128:(ic + 1) * 128],
                                HDN[:, ic * SBS + s2 * ST:ic * SBS + (s2 + 1) * ST],
                                start=(ic == 0), stop=(ic == NIC - 1))
                        mmB.inc(i)

            @block.vector
            def _(v):
                for hc in range(NHC):
                    for s2 in range(NS2):
                        nmm = hc * NS2 + s2
                        v.wait_ge(evB.h, evT + nmm + 1)
                        v.wait_ge(ld.h, ldI2 + (nmm + 1) * 16)
                        if nmm >= 2:
                            v.wait_ge(stq.h, stqI + (nmm - 1) * 16)
                        i = v.tensor_tensor(
                            SGF[:, (nmm % 2) * ST:(nmm % 2 + 1) * ST],
                            SRES[:, (nmm % 2) * ST:(nmm % 2 + 1) * ST],
                            GUS[:, 3 * ST:4 * ST], OP.mult)
                        dvB.inc(i)

            barrier()

    _ns_name, _ns_id = _ns_stack.pop(); nc.leave_named_scope(_ns_name, _ns_id, False)
    _ns_stack.append(("phJ", nc.enter_named_scope("phJ", False)[0]))
    with contextlib.ExitStack() as ph:
        sba = lambda n, s, d: ph.enter_context(nc.sbuf_tensor(n, s, d))
        SGP = [sba("SGP0", [128, NHC * 128], F32),
               sba("SGP1", [128, NHC * 128], F32)]
        OUTS = [sba("OUTS0", [128, D], F32), sba("OUTS1", [128, D], F32)]
        XRES = [sba("XRES0", [128, D], F32), sba("XRES1", [128, D], F32)]
        p_t = [ph.enter_context(nc.psum_tensor("pj_t0", [128, 512], F32)),
               ph.enter_context(nc.psum_tensor("pj_t1", [128, 512], F32))]
        NTB = TLH // 128
        ldJ, mmJ, evJ, dvJ, stqJ = ld.v, mmA.v, evA.v, dvA.v, stq.v
        TPG = min(4, NHC)
        ngJ = NHC // TPG

        @block.sync
        def _(sync):
            for tb in range(NTB):
                sync.wait_ge(mmA.h, max(mmJ, mmJ + (tb - 1) * ngJ))
                if tb >= 1:
                    sync.wait_ge(ld.h, ldJ + tb * (NHC + 1) * 16)
                for hc in range(NHC):
                    sync.dma_start(out=SGP[tb % 2][:, hc * 128:(hc + 1) * 128],
                                   in_=sgf_d[hc][:, tb * 128:(tb + 1) * 128]
                                   ).then_inc(ld.h, 16)
                sync.wait_ge(dvA.h, max(dvJ, dvJ + tb - 1))
                rA = F - TL + 0 * TLH + tb * 128
                rB = F - TL + 1 * TLH + tb * 128
                slot_dma_in(sync, XRES[tb % 2][:, :],
                            x_f[rA:rA + 128, :], x_f[rB:rB + 128, :], ld)
                ld.v += NHC * 16

        @block.tensor
        def _(t):
            for tb in range(NTB):
                t.wait_ge(ld.h, ldJ + (tb + 1) * (NHC + 1) * 16)
                for g in range(ngJ):
                    ng = tb * ngJ + g
                    if ng >= 2:
                        t.wait_ge(evA.h, evJ + ng - 1)
                    for j in range(TPG):
                        hc = g * TPG + j
                        i = t.transpose(p_t[ng % 2][:, j * 128:(j + 1) * 128],
                                        SGP[tb % 2][:, hc * 128:(hc + 1) * 128],
                                        idnf[:, :])
                    mmA.inc(i)

        @block.scalar
        def _(sc):
            for tb in range(NTB):
                for g in range(ngJ):
                    ng = tb * ngJ + g
                    sc.wait_ge(mmA.h, mmJ + ng + 1)
                    if tb >= 2 and g == 0:
                        sc.wait_ge(dvA.h, dvJ + tb - 1)
                        sc.wait_ge(stq.h, stqJ + (tb - 1) * 16)
                    for j in range(TPG):
                        hc = g * TPG + j
                        i = sc.activation(
                            OUTS[tb % 2][:, hc * 128:(hc + 1) * 128],
                            p_t[ng % 2][:, j * 128:(j + 1) * 128],
                            AF.Copy, scale=1.0)
                    evA.inc(i)

        @block.vector
        def _(v):
            for tb in range(NTB):
                v.wait_ge(evA.h, evJ + (tb + 1) * ngJ)
                v.wait_ge(ld.h, ldJ + (tb + 1) * (NHC + 1) * 16)
                if tb >= 2:
                    v.wait_ge(stq.h, stqJ + (tb - 1) * 16)
                i = v.tensor_tensor(OUTS[tb % 2][:, :], OUTS[tb % 2][:, :],
                                    XRES[tb % 2][:, :], OP.add)
                dvA.inc(i)

        @block.gpsimd
        def _(sync):
            for tb in range(NTB):
                sync.wait_ge(dvA.h, dvJ + tb + 1)
                if tb >= 1:
                    sync.wait_ge(stq.h, stqJ + tb * 16)
                sync.dma_start(out=out_r[tb * 128:(tb + 1) * 128, :],
                               in_=OUTS[tb % 2][:, :]).then_inc(stq.h, 16)
            stq.v += NTB * 16
            sync.wait_ge(stq.h, stq.v)
        barrier()

    _ns_name, _ns_id = _ns_stack.pop(); nc.leave_named_scope(_ns_name, _ns_id, False)
    if debug:
        d_ln = nc.dram_tensor("d_ln", [128, 512], F32, kind="ExternalOutput").ap()
        dbg_specs = [
            ("d_xnt", [NBLK, NKC, 128, BLK], xnt_d),
            ("d_upre", [NCC, 128, F], upre_d),
            ("d_u", [NCC, 128, F], u_d),
            ("d_ts", [128, F], ts_d),
            ("d_bc", [32, F], bc_d),
            ("d_dt", [NCC, 128, F], dt_d),
            ("d_sg", [NCC, 128, TL], sg_d),
            ("d_s", [NHC, 128, TLH], s_d),
            ("d_sgf", [NHC, 128, TLH], sgf_d),
            ("d_ygA", [NCC, 128, TL], sh_yg[0]),
            ("d_ygB", [NCC, 128, TL], sh_yg[1]),
        ]
        dbg_sem = Sem(st_, nc, "dbg")

        @block.sync
        def _(sync):
            for name, shape, src_t in dbg_specs:
                dt_out = F32 if name in ("d_s", "d_sgf") else BF16
                od = nc.dram_tensor(name, shape, dt_out, kind="ExternalOutput").ap()
                ap = src_t.ap() if callable(getattr(src_t, "ap", None)) else src_t
                sync.dma_start(out=od, in_=ap).then_inc(dbg_sem.h, 16)
                dbg_sem.v += 16
            sync.dma_start(out=d_ln, in_=lndbg[:, :]).then_inc(dbg_sem.h, 16)
            dbg_sem.v += 16
            sync.wait_ge(dbg_sem.h, dbg_sem.v)
        barrier()

    st_.close()
    return nc



def stage_inputs(c, inp, nonce):
    F, D, DI, DH = c["F"], c["D"], c["DI"], c["DH"]
    NCC, NKC, NIC, NHC = c["NCC"], c["NKC"], c["NIC"], c["NHC"]
    NST, BLK, NBLK, TL, TLH = c["NST"], c["BLK"], c["NBLK"], c["TL"], c["TLH"]
    bf = ml_dtypes.bfloat16

    x = np.asarray(inp["L0_attn_hidden_states"], np.float32).reshape(-1, D)
    w_in = np.asarray(inp["L0_attn_in_proj_w"], np.float32)
    conv_w = np.asarray(inp["L0_attn_conv_w"], np.float32)
    conv_b = np.asarray(inp["L0_attn_conv_b"], np.float32)
    xpw = np.asarray(inp["L0_attn_x_proj_w"], np.float32)
    dtw = np.asarray(inp["L0_attn_dt_proj_w"], np.float32)
    dtb = np.asarray(inp["L0_attn_dt_proj_b"], np.float32)
    alog = np.asarray(inp["L0_attn_A_log"], np.float32)
    dpar = np.asarray(inp["L0_attn_D_param"], np.float32)
    wout = np.asarray(inp["L0_attn_out_proj_w"], np.float32)
    gw = np.asarray(inp["L0_ffn_gate_w"], np.float32)
    uw = np.asarray(inp["L0_ffn_up_w"], np.float32)
    dw = np.asarray(inp["L0_ffn_down_w"], np.float32)
    gs = np.asarray(inp["L0_ffn_gate_s"], np.float32)
    us = np.asarray(inp["L0_ffn_up_s"], np.float32)
    ds = np.asarray(inp["L0_ffn_down_s"], np.float32)

    idnb = np.eye(128, dtype=bf)
    idnf = np.eye(128, dtype=np.float32)
    sel = np.zeros((32, 2 * NST * 128), dtype=bf)
    for jj in range(2 * NST):
        sel[jj, jj * 128:(jj + 1) * 128] = 1.0

    in_maps = []
    for core in range(N_CORES):
        q, slot = core // 2, core % 2
        pend = (q + 1) * TL
        pad = F - pend
        xf = np.zeros((F, D), np.float32)
        xf[pad:] = x[:pend]
        mask = np.zeros((128, NBLK), np.float32)
        mask[:, pad // BLK:] = 1.0
        ch0 = slot * DH

        wu = np.empty((NCC, 128, D), np.float32)
        wg = np.empty((NCC, 128, D), np.float32)
        for pc in range(NCC):
            blk_u = w_in[ch0 + pc * 128:ch0 + (pc + 1) * 128, :]
            blk_g = w_in[DI + ch0 + pc * 128:DI + ch0 + (pc + 1) * 128, :]
            wu[pc] = (blk_u.T.reshape(NKC, 128, 128)
                      .transpose(1, 0, 2).reshape(128, D))
            wg[pc] = (blk_g.T.reshape(NKC, 128, 128)
                      .transpose(1, 0, 2).reshape(128, D))

        xp = np.empty((128, NCC * 160), np.float32)
        for pc in range(NCC):
            xp[:, pc * 160:(pc + 1) * 160] = \
                xpw[:, ch0 + pc * 128:ch0 + (pc + 1) * 128].T
        dtt = dtw[ch0:ch0 + DH, :].T.copy()

        cvd = np.zeros((128, NCC * 512), np.float32)
        for pc in range(NCC):
            for k in range(128):
                for j in range(4):
                    cvd[k, pc * 512 + j * 128 + k] = \
                        conv_w[ch0 + pc * 128 + k, 0, j]

        vec = np.zeros((128, 3 * NCC), np.float32)
        al = np.zeros((128, NCC * NST), np.float32)
        for pc in range(NCC):
            vec[:, pc] = conv_b[ch0 + pc * 128:ch0 + (pc + 1) * 128]
            vec[:, NCC + pc] = dtb[ch0 + pc * 128:ch0 + (pc + 1) * 128]
            vec[:, 2 * NCC + pc] = dpar[ch0 + pc * 128:ch0 + (pc + 1) * 128]
            al[:, pc * NST:(pc + 1) * NST] = \
                alog[ch0 + pc * 128:ch0 + (pc + 1) * 128, :]

        wo = np.empty((NHC, 128, 2 * NCC * 128), np.float32)
        for hp in range(NHC):
            blk = wout[hp * 128:(hp + 1) * 128, :]
            wo[hp] = (blk.T.reshape(2 * NCC, 128, 128)
                      .transpose(1, 0, 2).reshape(128, 2 * NCC * 128))

        gt = np.empty((NIC, 128, NKC * 128), np.float32)
        ut = np.empty((NIC, 128, NKC * 128), np.float32)
        for ic in range(NIC):
            blkg = gw[ic * 128:(ic + 1) * 128, :]
            blku = uw[ic * 128:(ic + 1) * 128, :]
            gt[ic] = (blkg.T.reshape(NKC, 128, 128)
                      .transpose(1, 0, 2).reshape(128, NKC * 128))
            ut[ic] = (blku.T.reshape(NKC, 128, 128)
                      .transpose(1, 0, 2).reshape(128, NKC * 128))
        dt_ = np.empty((NHC, 128, NIC * 128), np.float32)
        for hc in range(NHC):
            blk = dw[hc * 128:(hc + 1) * 128, :]
            dt_[hc] = (blk.T.reshape(NIC, 128, 128)
                       .transpose(1, 0, 2).reshape(128, NIC * 128))

        fs = np.zeros((128, 2 * NIC + NHC), np.float32)
        for ic in range(NIC):
            fs[:, ic] = gs[ic * 128:(ic + 1) * 128]
            fs[:, NIC + ic] = us[ic * 128:(ic + 1) * 128]
        for hc in range(NHC):
            fs[:, 2 * NIC + hc] = ds[hc * 128:(hc + 1) * 128]

        in_maps.append({
            "x_f": xf, "nonce": np.array([[nonce]], np.int32),
            "winu": wu, "wing": wg, "xpt": xp, "dtwt": dtt, "convd": cvd,
            "vecs": vec, "alog": al, "maskc": mask, "selm": sel,
            "idnb": idnb, "idnf": idnf, "woutt": wo, "gatet": gt,
            "upt": ut, "downt": dt_, "fscale": fs,
        })
    return in_maps


_NC_CACHE = {}


def kernel(L0_attn_hidden_states, L0_attn_in_proj_w, L0_attn_conv_w,
           L0_attn_conv_b, L0_attn_x_proj_w, L0_attn_dt_proj_w,
           L0_attn_dt_proj_b, L0_attn_A_log, L0_attn_D_param,
           L0_attn_out_proj_w, L0_ffn_gate_w, L0_ffn_up_w, L0_ffn_down_w,
           L0_ffn_gate_s, L0_ffn_up_s, L0_ffn_down_s):
    c = make_cfg(mini=False)
    inp = dict(
        L0_attn_hidden_states=L0_attn_hidden_states,
        L0_attn_in_proj_w=L0_attn_in_proj_w, L0_attn_conv_w=L0_attn_conv_w,
        L0_attn_conv_b=L0_attn_conv_b, L0_attn_x_proj_w=L0_attn_x_proj_w,
        L0_attn_dt_proj_w=L0_attn_dt_proj_w,
        L0_attn_dt_proj_b=L0_attn_dt_proj_b, L0_attn_A_log=L0_attn_A_log,
        L0_attn_D_param=L0_attn_D_param,
        L0_attn_out_proj_w=L0_attn_out_proj_w, L0_ffn_gate_w=L0_ffn_gate_w,
        L0_ffn_up_w=L0_ffn_up_w, L0_ffn_down_w=L0_ffn_down_w,
        L0_ffn_gate_s=L0_ffn_gate_s, L0_ffn_up_s=L0_ffn_up_s,
        L0_ffn_down_s=L0_ffn_down_s)
    nonce = int(np.random.randint(1, 1 << 30))
    in_maps = stage_inputs(c, inp, nonce)
    if "full" not in _NC_CACHE:
        _NC_CACHE["full"] = build_program(c)
    nc = _NC_CACHE["full"]
    res = run_bass_kernel_spmd(nc, in_maps, core_ids=list(range(N_CORES)))
    TLH, D = c["TLH"], c["D"]
    out = np.empty((N_CORES * TLH, D), np.float32)
    for core in range(N_CORES):
        out[core * TLH:(core + 1) * TLH] = res.results[core]["out_r"]
    return out.reshape(1, N_CORES * TLH, D)


# version 07 (final)
# speedup vs baseline: 1.1125x; 1.0082x over previous
import contextlib
import numpy as np
import ml_dtypes

import concourse.bass as bass
import concourse.mybir as mybir
from concourse.alu_op_type import AluOpType
from concourse.bass_utils import run_bass_kernel_spmd
from concourse.libnrt import physical_to_logical_core_id as _p2l

F32 = mybir.dt.float32
BF16 = mybir.dt.bfloat16
I32 = mybir.dt.int32
AF = mybir.ActivationFunctionType
OP = AluOpType

N_CORES = 8
NPAIR = 4


def make_cfg(mini=False, **over):
    if mini:
        c = dict(F=2048, D=256, DI=512, NST=16, INTER=512, BLK=512, ST=256)
    else:
        c = dict(F=16384, D=2048, DI=4096, NST=16, INTER=5632, BLK=1024,
                 ST=512)
    c.update(over)
    c["DH"] = c["DI"] // 2
    c["NCC"] = c["DH"] // 128
    c["NKC"] = c["D"] // 128
    c["NIC"] = c["INTER"] // 128
    c["NHC"] = c["D"] // 128
    c["NBLK"] = c["F"] // c["BLK"]
    c["TL"] = c["F"] // NPAIR
    c["TAIL_BLK"] = c["TL"] // c["BLK"]
    c["TLH"] = c["TL"] // 2
    c["NSUB"] = c["BLK"] // c["ST"]
    c["EPS"] = 1e-5
    return c


class Sem:
    def __init__(self, stack, nc, name):
        self.h = stack.enter_context(nc.semaphore(name))
        self.v = 0

    def inc(self, inst, k=1):
        inst.then_inc(self.h, k)
        self.v += k
        return self.v


def build_program(c, debug=False):
    nc = bass.Bass("TRN2", target_bir_lowering=False)
    F, D, DI, DH = c["F"], c["D"], c["DI"], c["DH"]
    NCC, NKC, NIC, NHC = c["NCC"], c["NKC"], c["NIC"], c["NHC"]
    NST = c["NST"]
    BLK, ST, NSUB = c["BLK"], c["ST"], c["NSUB"]
    NBLK, TL, TAIL_BLK, TLH = c["NBLK"], c["TL"], c["TAIL_BLK"], c["TLH"]
    TB0 = NBLK - TAIL_BLK
    NROW = BLK // 128
    NT0 = NBLK * NROW

    din = lambda n, s, d=F32: nc.dram_tensor(n, s, d, kind="ExternalInput").ap()
    x_f = din("x_f", [F, D])
    nonce_i = din("nonce", [1, 1], I32)
    winu = din("winu", [NCC, 128, D])
    wing = din("wing", [NCC, 128, D])
    xpt = din("xpt", [128, NCC * 160])
    dtwt = din("dtwt", [128, DH])
    convd = din("convd", [128, NCC * 512])
    vecs = din("vecs", [128, 3 * NCC])
    alog = din("alog", [128, NCC * NST])
    maskc = din("maskc", [128, NBLK])
    selm = din("selm", [32, 2 * NST * 128], BF16)
    idnb_i = din("idnb", [128, 128], BF16)
    idnf_i = din("idnf", [128, 128])
    woutt = din("woutt", [NHC, 128, 2 * NCC * 128])
    gatet = din("gatet", [NIC, 128, NKC * 128])
    upt = din("upt", [NIC, 128, NKC * 128])
    downt = din("downt", [NHC, 128, NIC * 128])
    fscale = din("fscale", [128, 2 * NIC + NHC])
    out_r = nc.dram_tensor("out_r", [TLH, D], F32, kind="ExternalOutput").ap()

    xnt_d = nc.dram_tensor("xnt_d", [NBLK, NKC, 128, BLK], BF16)
    upre_d = nc.dram_tensor("upre_d", [NCC, 128, F], BF16)
    u_d = nc.dram_tensor("u_d", [NCC, 128, F], BF16)
    dt_d = nc.dram_tensor("dt_d", [NCC, 128, F], BF16)
    ts_d = nc.dram_tensor("ts_d", [128, F], BF16)
    bc_d = nc.dram_tensor("bc_d", [32, F], BF16)
    sg_d = nc.dram_tensor("sg_d", [NCC, 128, TL], BF16)
    s_d = nc.dram_tensor("s_d", [NHC, 128, TLH], F32)
    sgf_d = nc.dram_tensor("sgf_d", [NHC, 128, TLH], F32)

    sh_ts = nc.dram_tensor("sh_ts", [2, 128, F], F32, addr_space="Shared")
    sh_bc = nc.dram_tensor("sh_bc", [2, 32, F], F32, addr_space="Shared")
    sh_yg = nc.dram_tensor("sh_yg", [2, NCC, 128, TL], BF16, addr_space="Shared")
    sh_fl = nc.dram_tensor("sh_fl", [2, 8], I32, addr_space="Shared")

    st_ = contextlib.ExitStack()
    sbuf = lambda n, s, d: st_.enter_context(nc.sbuf_tensor(n, s, d))

    nonce_sb = sbuf("nonce_sb", [1, 1], I32)
    params_sb = sbuf("params_sb", [1, 2], I32)
    a_sb = sbuf("a_sb", [128, NCC * NST], F32)
    convb_sb = sbuf("convb_sb", [128, NCC], F32)
    dtb_sb = sbuf("dtb_sb", [128, NCC], F32)
    dpar_sb = sbuf("dpar_sb", [128, NCC], F32)
    mask_sb = sbuf("mask_sb", [128, NBLK], F32)
    fsc_sb = sbuf("fsc_sb", [128, 2 * NIC + NHC], F32)
    sel_sb = sbuf("sel_sb", [32, 2 * NST * 128], BF16)
    state_sb = sbuf("state_sb", [128, NCC * NST], F32)
    idnb = sbuf("idnb_s", [128, 128], BF16)
    idnf = sbuf("idnf_s", [128, 128], F32)
    lnsc = sbuf("lnsc", [128, 64], F32)
    lndbg = sbuf("lndbg", [128, 512], F32)

    SM = {k: Sem(st_, nc, k) for k in
          ["ld", "lw", "wc", "mmA", "mmB", "evA", "evB", "evC",
           "dvA", "dvB", "stq", "stu", "bar", "hs", "lg", "gq", "mmY"]}
    hs = SM["hs"]
    lg = SM["lg"]
    ld, lw, wc = SM["ld"], SM["lw"], SM["wc"]
    mmA, mmB = SM["mmA"], SM["mmB"]
    evA, evB, evC = SM["evA"], SM["evB"], SM["evC"]
    dvA, dvB = SM["dvA"], SM["dvB"]
    stq, stu, bar = SM["stq"], SM["stu"], SM["bar"]
    gq, mmY = SM["gq"], SM["mmY"]

    _ns_stack = []
    block = st_.enter_context(nc.Block())
    st_.enter_context(nc.allow_low_precision(
        reason="bf16 matmuls / bf16 intermediates sanctioned by tolerance"))

    r_slot = st_.enter_context(nc.sync.register("r_slot"))
    rg_slot = st_.enter_context(nc.gpsimd.register("rg_slot"))
    r_non = st_.enter_context(nc.sync.register("r_non"))
    r_flag = st_.enter_context(nc.sync.register("r_flag"))
    r_cont = st_.enter_context(nc.sync.register("r_cont"))
    r_poll = st_.enter_context(nc.sync.register("r_poll"))

    ALL_ENG = (block.sync, block.tensor, block.scalar, block.vector,
               block.gpsimd)

    def barrier():
        for emit in ALL_ENG:
            @emit
            def _(e):
                e.nop().then_inc(bar.h, 1)
        bar.v += len(ALL_ENG)
        tgt = bar.v
        for emit in ALL_ENG:
            @emit
            def _(e):
                e.wait_ge(bar.h, tgt)

    def flag_write(sync, idx, off):
        sync.reg_alu(r_flag, sync.snap(r_non), off, OP.add)
        s = slot_sv(sync, r_slot)
        with sync.If(s == 0):
            sync.store(sh_fl[0:1, idx:idx + 1], r_flag)
        with sync.Else():
            sync.store(sh_fl[1:2, idx:idx + 1], r_flag)

    def flag_poll(sync, idx, off):
        sync.reg_alu(r_flag, sync.snap(r_non), off, OP.add)
        s = slot_sv(sync, r_slot)
        with sync.If(s == 0):
            sync.reg_mov(r_cont, 1)
            with sync.While(r_cont):
                sync.load(r_poll, sh_fl[1:2, idx:idx + 1])
                sync.reg_alu(r_cont, sync.snap(r_poll), sync.snap(r_flag),
                             OP.not_equal)
        with sync.Else():
            sync.reg_mov(r_cont, 1)
            with sync.While(r_cont):
                sync.load(r_poll, sh_fl[0:1, idx:idx + 1])
                sync.reg_alu(r_cont, sync.snap(r_poll), sync.snap(r_flag),
                             OP.not_equal)

    SLOT_SV = {}

    def slot_sv(eng, reg):
        key = id(eng)
        if key not in SLOT_SV:
            SLOT_SV[key] = eng.snap(reg, min_val=0, max_val=1)
        return SLOT_SV[key]

    def slot_dma_out(eng, dst0, dst1, src, sem, reg=None):
        s = slot_sv(eng, reg if reg is not None else r_slot)
        with eng.If(s == 0):
            eng.dma_start(out=dst0, in_=src).then_inc(sem.h, 16)
        with eng.Else():
            eng.dma_start(out=dst1, in_=src).then_inc(sem.h, 16)
        sem.v += 16

    def slot_dma_in(eng, dst, src0, src1, sem, reg=None):
        s = slot_sv(eng, reg if reg is not None else r_slot)
        with eng.If(s == 0):
            eng.dma_start(out=dst, in_=src0).then_inc(sem.h, 16)
        with eng.Else():
            eng.dma_start(out=dst, in_=src1).then_inc(sem.h, 16)
        sem.v += 16

    def gstore(g, dst, src, sem):
        g.dma_start(out=dst, in_=src).then_inc(sem.h, 16)
        sem.v += 16

    @block.sync
    def _(sync):
        for arm in sync.switch_core_id(n=128):
            sync.reg_mov(r_slot,
                         _p2l(int(arm) % 8, 0, container_num_cores=8) % 2)
        for dst, src in [
                (nonce_sb[:, :], nonce_i), (a_sb[:, :], alog),
                (convb_sb[:, :], vecs[:, 0:NCC]),
                (dtb_sb[:, :], vecs[:, NCC:2 * NCC]),
                (dpar_sb[:, :], vecs[:, 2 * NCC:3 * NCC]),
                (mask_sb[:, :], maskc), (sel_sb[:, :], selm),
                (idnb[:, :], idnb_i), (idnf[:, :], idnf_i),
                (fsc_sb[:, :], fscale)]:
            sync.dma_start(out=dst, in_=src).then_inc(ld.h, 16)
        ld.v += 160
        sync.wait_ge(ld.h, ld.v)
        sync.load(r_non, nonce_sb[0:1, 0:1])
        sync.store(params_sb[0:1, 0:1], r_slot)
        sync.nop().then_inc(hs.h, 1)
    hs.v += 1

    @block.gpsimd
    def _(g):
        g.wait_ge(hs.h, hs.v)
        g.load(rg_slot, params_sb[0:1, 0:1])

    @block.scalar
    def _(sc):
        sc.wait_ge(hs.h, hs.v)
        i = sc.activation(a_sb[:, :], a_sb[:, :], AF.Exp, scale=1.0)
        evA.inc(i)

    @block.vector
    def _(v):
        v.wait_ge(evA.h, evA.v)
        v.tensor_scalar(a_sb[:, :], a_sb[:, :], -1.0, None, OP.mult)
        i = v.memset(state_sb[:, :], 0.0)
        dvA.inc(i)
    barrier()

    _ns_stack.append(("phA", nc.enter_named_scope("phA", False)[0]))
    with contextlib.ExitStack() as ph:
        sba = lambda n, s, d: ph.enter_context(nc.sbuf_tensor(n, s, d))
        XR = sba("XR", [128, 2 * D], F32)
        XNB = sba("XNB", [128, 2 * D], BF16)
        STG = [sba("STG0", [128, NKC * BLK], BF16),
               sba("STG1", [128, NKC * BLK], BF16)]
        pt_bf = [ph.enter_context(nc.psum_tensor("pt_bf0", [128, 512], BF16)),
                 ph.enter_context(nc.psum_tensor("pt_bf1", [128, 512], BF16))]
        xr = [XR[:, 0:D], XR[:, D:2 * D]]
        xnb = [XNB[:, 0:D], XNB[:, D:2 * D]]
        nstat = max(1, D // 512)
        kpg = min(4, NKC)
        ngrp = NKC // kpg

        ldA, dvA_A, evA_A = ld.v, dvA.v, evA.v
        mmA_A, evB_A, stqA = mmA.v, evB.v, stq.v

        @block.sync
        def _(sync):
            for it in range(NT0):
                sync.wait_ge(dvA.h, max(dvA_A, dvA_A + 2 * (it - 1)))
                if it >= 1:
                    sync.wait_ge(ld.h, ldA + it * 16)
                sync.dma_start(out=xr[it % 2],
                               in_=x_f[it * 128:(it + 1) * 128, :]
                               ).then_inc(ld.h, 16)
            ld.v += NT0 * 16

        @block.vector
        def _(v):
            for it in range(NT0):
                sl = lnsc[:, (it % 2) * 8:(it % 2) * 8 + 8]
                stt = lnsc[:, 16 + (it % 2) * 24:16 + (it % 2) * 24 + 6 * nstat]
                v.wait_ge(ld.h, ldA + min(NT0, it + 2) * 16)
                if it >= 2:
                    v.wait_ge(evA.h, evA_A + (it - 1))
                for s in range(nstat):
                    w = min(512, D)
                    i = v.bn_stats(stt[:, s * 6:(s + 1) * 6],
                                   xr[it % 2][:, s * w:(s + 1) * w])
                dvB.inc(i)
                v.wait_ge(dvB.h, dvB.v)
                i = v.bn_aggr(sl[:, 0:2], stt[:, 0:6 * nstat])
                dvB.inc(i)
                v.wait_ge(dvB.h, dvB.v)
                i = v.tensor_scalar(sl[:, 2:3], sl[:, 1:2], 1.0,
                                    float(c["EPS"]), OP.mult, OP.add)
                dvA.inc(i)
                v.wait_ge(evA.h, evA_A + it + 1)
                if it >= 2:
                    v.wait_ge(mmA.h, mmA_A + (it - 1) * ngrp)
                i = v.tensor_scalar(xnb[it % 2], xr[it % 2], sl[:, 0:1],
                                    sl[:, 3:4], OP.subtract, OP.mult)
                dvA.inc(i)

        def emit_evacs(sc, it):
            b, r = it // NROW, it % NROW
            if r == 0 and b >= 2:
                sc.wait_ge(stq.h, stqA + (b - 1) * NKC * 16)
            for g in range(ngrp):
                e = it * ngrp + g
                sc.wait_ge(mmA.h, mmA_A + e + 1)
                for j in range(kpg):
                    kc = g * kpg + j
                    i = sc.activation(
                        STG[b % 2][:, kc * BLK + r * 128:kc * BLK + (r + 1) * 128],
                        pt_bf[e % 2][:, j * 128:(j + 1) * 128], AF.Copy,
                        scale=1.0)
                evB.inc(i)

        @block.scalar
        def _(sc):
            for it in range(NT0):
                sl = lnsc[:, (it % 2) * 8:(it % 2) * 8 + 8]
                sc.wait_ge(dvA.h, dvA_A + 2 * it + 1)
                sc.activation(sl[:, 5:6], sl[:, 2:3], AF.Ln, scale=1.0)
                i = sc.activation(sl[:, 3:4], sl[:, 5:6], AF.Exp, scale=-0.5)
                evA.inc(i)
                if it >= 1:
                    emit_evacs(sc, it - 1)
            emit_evacs(sc, NT0 - 1)

        @block.tensor
        def _(t):
            for it in range(NT0):
                t.wait_ge(dvA.h, dvA_A + 2 * it + 2)
                for g in range(ngrp):
                    e = it * ngrp + g
                    if e >= 2:
                        t.wait_ge(evB.h, evB_A + e - 1)
                    for j in range(kpg):
                        kc = g * kpg + j
                        i = t.transpose(pt_bf[e % 2][:, j * 128:(j + 1) * 128],
                                        xnb[it % 2][:, kc * 128:(kc + 1) * 128],
                                        idnb[:, :])
                    mmA.inc(i)

        @block.gpsimd
        def _(sync):
            for b in range(NBLK):
                sync.wait_ge(evB.h, evB_A + (b + 1) * NROW * ngrp)
                if b >= 1:
                    sync.wait_ge(stq.h, stqA + b * NKC * 16)
                for kc in range(NKC):
                    sync.dma_start(out=xnt_d[b, kc],
                                   in_=STG[b % 2][:, kc * BLK:(kc + 1) * BLK]
                                   ).then_inc(stq.h, 16)
            stq.v += NBLK * NKC * 16
            sync.wait_ge(stq.h, stq.v)
        barrier()

    _ns_name, _ns_id = _ns_stack.pop(); nc.leave_named_scope(_ns_name, _ns_id, False)
    _ns_stack.append(("phB", nc.enter_named_scope("phB", False)[0]))
    with contextlib.ExitStack() as ph:
        sba = lambda n, s, d: ph.enter_context(nc.sbuf_tensor(n, s, d))
        WB = sba("WB", [128, NCC * D], BF16)
        WL = [sba("WL0", [128, D], F32), sba("WL1", [128, D], F32)]
        XNT = [sba("XNT0", [128, NKC * BLK], BF16),
               sba("XNT1", [128, NKC * BLK], BF16)]
        UPS = sba("UPS", [128, 4 * ST], BF16)
        p_a = [ph.enter_context(nc.psum_tensor("pb_a0", [128, 512], F32)),
               ph.enter_context(nc.psum_tensor("pb_a1", [128, 512], F32))]
        lwB, wcB = lw.v, wc.v

        @block.sync
        def _(sync):
            for pc in range(NCC):
                sync.wait_ge(wc.h, max(wcB, wcB + pc - 1))
                if pc >= 1:
                    sync.wait_ge(lw.h, lwB + pc * 16)
                sync.dma_start(out=WL[pc % 2][:, :], in_=winu[pc]
                               ).then_inc(lw.h, 16)
            lw.v += NCC * 16

        @block.scalar
        def _(sc):
            for pc in range(NCC):
                sc.wait_ge(lw.h, lwB + (pc + 1) * 16)
                i = sc.activation(WB[:, pc * D:(pc + 1) * D], WL[pc % 2][:, :],
                                  AF.Copy, scale=1.0)
                wc.inc(i)

        ldB, mmB_B, evC_B, stqB = ld.v, mmA.v, evC.v, stq.v

        @block.sync
        def _(sync):
            for b in range(NBLK):
                sync.wait_ge(mmA.h, max(mmB_B, mmB_B + (b - 1) * NCC * NSUB))
                if b >= 1:
                    sync.wait_ge(ld.h, ldB + b * NKC * 16)
                for kc in range(NKC):
                    sync.dma_start(out=XNT[b % 2][:, kc * BLK:(kc + 1) * BLK],
                                   in_=xnt_d[b, kc]).then_inc(ld.h, 16)
            ld.v += NBLK * NKC * 16

        @block.tensor
        def _(t):
            t.wait_ge(wc.h, wcB + NCC)
            for b in range(NBLK):
                t.wait_ge(ld.h, ldB + (b + 1) * NKC * 16)
                for pc in range(NCC):
                    for sub in range(NSUB):
                        nmm = (b * NCC + pc) * NSUB + sub
                        if nmm >= 2:
                            t.wait_ge(evC.h, evC_B + nmm - 1)
                        for kc in range(NKC):
                            i = t.matmul(
                                p_a[nmm % 2][:, 0:ST],
                                WB[:, pc * D + kc * 128:pc * D + (kc + 1) * 128],
                                XNT[b % 2][:, kc * BLK + sub * ST:kc * BLK + (sub + 1) * ST],
                                start=(kc == 0), stop=(kc == NKC - 1))
                        mmA.inc(i)

        @block.scalar
        def _(sc):
            for nmm in range(NBLK * NCC * NSUB):
                sc.wait_ge(mmA.h, mmB_B + nmm + 1)
                if nmm >= 4:
                    sc.wait_ge(stq.h, stqB + (nmm - 3) * 16)
                i = sc.activation(UPS[:, (nmm % 4) * ST:(nmm % 4 + 1) * ST],
                                  p_a[nmm % 2][:, 0:ST], AF.Copy, scale=1.0)
                evC.inc(i)

        @block.gpsimd
        def _(sync):
            for b in range(NBLK):
                for pc in range(NCC):
                    for sub in range(NSUB):
                        nmm = (b * NCC + pc) * NSUB + sub
                        sync.wait_ge(evC.h, evC_B + nmm + 1)
                        if nmm >= 1:
                            sync.wait_ge(stq.h, stqB + nmm * 16)
                        sync.dma_start(
                            out=upre_d[pc][:, b * BLK + sub * ST:b * BLK + (sub + 1) * ST],
                            in_=UPS[:, (nmm % 4) * ST:(nmm % 4 + 1) * ST]
                            ).then_inc(stq.h, 16)
            stq.v += NBLK * NCC * NSUB * 16
            sync.wait_ge(stq.h, stq.v)
        barrier()

    _ns_name, _ns_id = _ns_stack.pop(); nc.leave_named_scope(_ns_name, _ns_id, False)
    _ns_stack.append(("phC", nc.enter_named_scope("phC", False)[0]))
    with contextlib.ExitStack() as ph:
        sba = lambda n, s, d: ph.enter_context(nc.sbuf_tensor(n, s, d))
        HB = BLK + 3
        XPB = sba("XPB", [128, NCC * 160], BF16)
        CVB = sba("CVB", [128, NCC * 512], BF16)
        WLx = sba("WLx", [128, NCC * 512], F32)
        UPT = [sba("UPT0", [128, NCC * (HB + 1)], BF16),
               sba("UPT1", [128, NCC * (HB + 1)], BF16)]
        UT = sba("UT", [128, 4 * ST], BF16)
        TSE = sba("TSE", [128, 2 * ST], F32)
        BCE = sba("BCE", [32, 2 * ST], F32)
        p_a = [ph.enter_context(nc.psum_tensor("pc_a0", [128, 512], F32)),
               ph.enter_context(nc.psum_tensor("pc_a1", [128, 512], F32))]
        p_ts = [ph.enter_context(nc.psum_tensor("pc_t0", [128, 512], F32)),
                ph.enter_context(nc.psum_tensor("pc_t1", [128, 512], F32))]
        p_bc = [ph.enter_context(nc.psum_tensor("pc_b0", [32, 512], F32)),
                ph.enter_context(nc.psum_tensor("pc_b1", [32, 512], F32))]
        lwC, wcC = lw.v, wc.v

        @block.sync
        def _(sync):
            sync.dma_start(out=WLx[:, 0:NCC * 160], in_=xpt).then_inc(lw.h, 16)
            sync.wait_ge(wc.h, wcC + 1)
            sync.wait_ge(lw.h, lwC + 16)
            sync.dma_start(out=WLx[:, 0:NCC * 512], in_=convd).then_inc(lw.h, 16)
            lw.v += 32

        @block.scalar
        def _(sc):
            sc.wait_ge(lw.h, lwC + 16)
            i = sc.activation(XPB[:, :], WLx[:, 0:NCC * 160], AF.Copy, scale=1.0)
            wc.inc(i)
            sc.wait_ge(lw.h, lwC + 32)
            i = sc.activation(CVB[:, :], WLx[:, 0:NCC * 512], AF.Copy, scale=1.0)
            wc.inc(i)

        ldC, dvC = ld.v, dvA.v
        mmC_C, mmX_C, evU_C, evT_C = mmA.v, mmB.v, evA.v, evB.v
        stqC, stuC = stq.v, stu.v

        @block.vector
        def _(v):
            v.memset(UPT[0][:, :], 0.0)
            i = v.memset(UPT[1][:, :], 0.0)
            dvA.inc(i)

        @block.sync
        def _(sync):
            sync.wait_ge(dvA.h, dvC + 1)
            for b in range(NBLK):
                sync.wait_ge(mmA.h, max(mmC_C, mmC_C + (b - 1) * NSUB * NCC))
                if b >= 1:
                    sync.wait_ge(ld.h, ldC + b * NCC * 16)
                for cc in range(NCC):
                    base = cc * (HB + 1)
                    if b == 0:
                        sync.dma_start(out=UPT[0][:, base + 3:base + 3 + BLK],
                                       in_=upre_d[cc][:, 0:BLK]).then_inc(ld.h, 16)
                    else:
                        sync.dma_start(out=UPT[b % 2][:, base:base + HB],
                                       in_=upre_d[cc][:, b * BLK - 3:b * BLK + BLK]
                                       ).then_inc(ld.h, 16)
            ld.v += NBLK * NCC * 16

        @block.tensor
        def _(t):
            t.wait_ge(wc.h, wcC + 2)
            for b in range(NBLK):
                t.wait_ge(ld.h, ldC + (b + 1) * NCC * 16)
                for sub in range(NSUB):
                    nbs = b * NSUB + sub
                    for cc in range(NCC):
                        ncv = nbs * NCC + cc
                        if ncv >= 2:
                            t.wait_ge(evA.h, evU_C + ncv - 1)
                        base = cc * (HB + 1)
                        for j in range(4):
                            i = t.matmul(
                                p_a[ncv % 2][:, 0:ST],
                                CVB[:, cc * 512 + j * 128:cc * 512 + (j + 1) * 128],
                                UPT[b % 2][:, base + sub * ST + j:base + sub * ST + j + ST],
                                start=(j == 0), stop=(j == 3))
                        mmA.inc(i)
                        t.wait_ge(evA.h, evU_C + ncv + 1)
                        if cc == 0 and nbs >= 2:
                            t.wait_ge(evB.h, evT_C + nbs - 1)
                        i = t.matmul(p_ts[sub][:, 0:ST],
                                     XPB[:, cc * 160:cc * 160 + 128],
                                     UT[:, (ncv % 4) * ST:(ncv % 4 + 1) * ST],
                                     start=(cc == 0), stop=(cc == NCC - 1))
                        i = t.matmul(p_bc[sub][:, 0:ST],
                                     XPB[:, cc * 160 + 128:cc * 160 + 160],
                                     UT[:, (ncv % 4) * ST:(ncv % 4 + 1) * ST],
                                     start=(cc == 0), stop=(cc == NCC - 1))
                        mmB.inc(i)

        @block.scalar
        def _(sc):
            for b in range(NBLK):
                for sub in range(NSUB):
                    nbs = b * NSUB + sub
                    for cc in range(NCC):
                        ncv = nbs * NCC + cc
                        sc.wait_ge(mmA.h, mmC_C + ncv + 1)
                        if ncv >= 4:
                            sc.wait_ge(stu.h, stuC + (ncv - 3) * 16)
                        i = sc.activation(UT[:, (ncv % 4) * ST:(ncv % 4 + 1) * ST],
                                          p_a[ncv % 2][:, 0:ST], AF.Silu,
                                          bias=convb_sb[:, cc:cc + 1], scale=1.0)
                        evA.inc(i)
                    sc.wait_ge(mmB.h, mmX_C + (nbs + 1) * NCC)
                    if nbs >= 2:
                        sc.wait_ge(stq.h, stqC + (nbs - 1) * 32)
                    i = sc.activation(TSE[:, (nbs % 2) * ST:(nbs % 2 + 1) * ST],
                                      p_ts[sub][:, 0:ST], AF.Copy, scale=1.0)
                    i = sc.activation(BCE[:, (nbs % 2) * ST:(nbs % 2 + 1) * ST],
                                      p_bc[sub][:, 0:ST], AF.Copy, scale=1.0)
                    evB.inc(i)

        @block.gpsimd
        def _(sync):
            for b in range(NBLK):
                for sub in range(NSUB):
                    nbs = b * NSUB + sub
                    for cc in range(NCC):
                        ncv = nbs * NCC + cc
                        sync.wait_ge(evA.h, evU_C + ncv + 1)
                        if ncv >= 1:
                            sync.wait_ge(stu.h, stuC + ncv * 16)
                        sync.dma_start(
                            out=u_d[cc][:, b * BLK + sub * ST:b * BLK + (sub + 1) * ST],
                            in_=UT[:, (ncv % 4) * ST:(ncv % 4 + 1) * ST]
                            ).then_inc(stu.h, 16)
                    t0 = b * BLK + sub * ST
                    sync.wait_ge(evB.h, evT_C + nbs + 1)
                    if nbs >= 1:
                        sync.wait_ge(stq.h, stqC + nbs * 32)
                    slot_dma_out(sync, sh_ts[0][:, t0:t0 + ST],
                                 sh_ts[1][:, t0:t0 + ST],
                                 TSE[:, (nbs % 2) * ST:(nbs % 2 + 1) * ST], stq,
                                 reg=rg_slot)
                    slot_dma_out(sync, sh_bc[0][:, t0:t0 + ST],
                                 sh_bc[1][:, t0:t0 + ST],
                                 BCE[:, (nbs % 2) * ST:(nbs % 2 + 1) * ST], stq,
                                 reg=rg_slot)
            stu.v += NBLK * NSUB * NCC * 16
            sync.wait_ge(stu.h, stu.v)
            sync.wait_ge(stq.h, stq.v)

        @block.sync
        def _(sync):
            sync.wait_ge(stu.h, stu.v)
            sync.wait_ge(stq.h, stq.v)
            flag_write(sync, 0, 1)
        barrier()

    _ns_name, _ns_id = _ns_stack.pop(); nc.leave_named_scope(_ns_name, _ns_id, False)
    _ns_stack.append(("phD", nc.enter_named_scope("phD", False)[0]))
    with contextlib.ExitStack() as ph:
        sba = lambda n, s, d: ph.enter_context(nc.sbuf_tensor(n, s, d))
        WB = sba("WBd", [128, NCC * D], BF16)
        WL = [sba("WLd0", [128, D], F32), sba("WLd1", [128, D], F32)]
        XNT = [sba("XNTd0", [128, NKC * BLK], BF16),
               sba("XNTd1", [128, NKC * BLK], BF16)]
        SGT = sba("SGT", [128, 4 * ST], BF16)
        p_a = [ph.enter_context(nc.psum_tensor("pd_a0", [128, 512], F32)),
               ph.enter_context(nc.psum_tensor("pd_a1", [128, 512], F32))]
        lwD, wcD = lw.v, wc.v

        @block.sync
        def _(sync):
            for pc in range(NCC):
                sync.wait_ge(wc.h, max(wcD, wcD + pc - 1))
                if pc >= 1:
                    sync.wait_ge(lw.h, lwD + pc * 16)
                sync.dma_start(out=WL[pc % 2][:, :], in_=wing[pc]
                               ).then_inc(lw.h, 16)
            lw.v += NCC * 16

        @block.scalar
        def _(sc):
            for pc in range(NCC):
                sc.wait_ge(lw.h, lwD + (pc + 1) * 16)
                i = sc.activation(WB[:, pc * D:(pc + 1) * D], WL[pc % 2][:, :],
                                  AF.Copy, scale=1.0)
                wc.inc(i)

        ldD, mmD, evD, stqD = ld.v, mmA.v, evA.v, stq.v

        @block.sync
        def _(sync):
            for bi in range(TAIL_BLK):
                sync.wait_ge(mmA.h, max(mmD, mmD + (bi - 1) * NCC * NSUB))
                if bi >= 1:
                    sync.wait_ge(ld.h, ldD + bi * NKC * 16)
                for kc in range(NKC):
                    sync.dma_start(out=XNT[bi % 2][:, kc * BLK:(kc + 1) * BLK],
                                   in_=xnt_d[TB0 + bi, kc]).then_inc(ld.h, 16)
            ld.v += TAIL_BLK * NKC * 16

        @block.tensor
        def _(t):
            t.wait_ge(wc.h, wcD + NCC)
            for bi in range(TAIL_BLK):
                t.wait_ge(ld.h, ldD + (bi + 1) * NKC * 16)
                for pc in range(NCC):
                    for sub in range(NSUB):
                        nmm = (bi * NCC + pc) * NSUB + sub
                        if nmm >= 2:
                            t.wait_ge(evA.h, evD + nmm - 1)
                        for kc in range(NKC):
                            i = t.matmul(
                                p_a[nmm % 2][:, 0:ST],
                                WB[:, pc * D + kc * 128:pc * D + (kc + 1) * 128],
                                XNT[bi % 2][:, kc * BLK + sub * ST:kc * BLK + (sub + 1) * ST],
                                start=(kc == 0), stop=(kc == NKC - 1))
                        mmA.inc(i)

        @block.scalar
        def _(sc):
            for nmm in range(TAIL_BLK * NCC * NSUB):
                sc.wait_ge(mmA.h, mmD + nmm + 1)
                if nmm >= 4:
                    sc.wait_ge(stq.h, stqD + (nmm - 3) * 16)
                i = sc.activation(SGT[:, (nmm % 4) * ST:(nmm % 4 + 1) * ST],
                                  p_a[nmm % 2][:, 0:ST], AF.Silu, scale=1.0)
                evA.inc(i)

        @block.gpsimd
        def _(sync):
            for bi in range(TAIL_BLK):
                for pc in range(NCC):
                    for sub in range(NSUB):
                        nmm = (bi * NCC + pc) * NSUB + sub
                        sync.wait_ge(evA.h, evD + nmm + 1)
                        if nmm >= 1:
                            sync.wait_ge(stq.h, stqD + nmm * 16)
                        t0 = bi * BLK + sub * ST
                        sync.dma_start(out=sg_d[pc][:, t0:t0 + ST],
                                       in_=SGT[:, (nmm % 4) * ST:(nmm % 4 + 1) * ST]
                                       ).then_inc(stq.h, 16)
            stq.v += TAIL_BLK * NCC * NSUB * 16
            sync.wait_ge(stq.h, stq.v)
        barrier()

    _ns_name, _ns_id = _ns_stack.pop(); nc.leave_named_scope(_ns_name, _ns_id, False)
    _ns_stack.append(("phE", nc.enter_named_scope("phE", False)[0]))
    with contextlib.ExitStack() as ph:
        sba = lambda n, s, d: ph.enter_context(nc.sbuf_tensor(n, s, d))
        TSA = sba("TSA", [128, 2 * BLK], F32)
        TSS = sba("TSS", [128, BLK], F32)
        TSB = sba("TSBe", [128, BLK], BF16)
        BCA = sba("BCA", [32, 2 * BLK], F32)
        BCS = sba("BCS", [32, BLK], F32)
        BCB = sba("BCBe", [32, BLK], BF16)
        ldE, dvE, evE, stqE = ld.v, dvA.v, evA.v, stq.v

        @block.sync
        def _(sync):
            flag_poll(sync, 0, 1)
            for b in range(NBLK):
                sync.wait_ge(dvA.h, dvE + 2 * b)
                if b >= 1:
                    sync.wait_ge(ld.h, ldE + b * 64)
                t0 = b * BLK
                sync.dma_start(out=TSA[:, 0:BLK],
                               in_=sh_ts[0][:, t0:t0 + BLK]).then_inc(ld.h, 16)
                sync.dma_start(out=TSA[:, BLK:2 * BLK],
                               in_=sh_ts[1][:, t0:t0 + BLK]).then_inc(ld.h, 16)
                sync.dma_start(out=BCA[:, 0:BLK],
                               in_=sh_bc[0][:, t0:t0 + BLK]).then_inc(ld.h, 16)
                sync.dma_start(out=BCA[:, BLK:2 * BLK],
                               in_=sh_bc[1][:, t0:t0 + BLK]).then_inc(ld.h, 16)
            ld.v += NBLK * 64

        @block.vector
        def _(v):
            for b in range(NBLK):
                v.wait_ge(ld.h, ldE + (b + 1) * 64)
                if b >= 1:
                    v.wait_ge(evA.h, evE + 2 * b)
                v.tensor_tensor(TSS[:, :], TSA[:, 0:BLK], TSA[:, BLK:2 * BLK],
                                OP.add)
                i = v.tensor_tensor(BCS[:, :], BCA[:, 0:BLK],
                                    BCA[:, BLK:2 * BLK], OP.add)
                dvA.inc(i, 2)

        @block.scalar
        def _(sc):
            for b in range(NBLK):
                sc.wait_ge(dvA.h, dvE + 2 * (b + 1))
                if b >= 2:
                    sc.wait_ge(stq.h, stqE + (b - 1) * 32)
                i = sc.activation(TSB[:, :], TSS[:, :], AF.Copy, scale=1.0)
                i = sc.activation(BCB[:, :], BCS[:, :], AF.Copy, scale=1.0)
                evA.inc(i, 2)

        @block.gpsimd
        def _(sync):
            for b in range(NBLK):
                sync.wait_ge(evA.h, evE + 2 * (b + 1))
                if b >= 1:
                    sync.wait_ge(stq.h, stqE + b * 32)
                t0 = b * BLK
                sync.dma_start(out=ts_d[:, t0:t0 + BLK], in_=TSB[:, :]
                               ).then_inc(stq.h, 16)
                sync.dma_start(out=bc_d[:, t0:t0 + BLK], in_=BCB[:, :]
                               ).then_inc(stq.h, 16)
            stq.v += NBLK * 32
            sync.wait_ge(stq.h, stq.v)
        barrier()

    _ns_name, _ns_id = _ns_stack.pop(); nc.leave_named_scope(_ns_name, _ns_id, False)
    _ns_stack.append(("phF", nc.enter_named_scope("phF", False)[0]))
    with contextlib.ExitStack() as ph:
        sba = lambda n, s, d: ph.enter_context(nc.sbuf_tensor(n, s, d))
        DTW = sba("DTW", [128, DH], BF16)
        WLf = sba("WLf", [128, DH], F32)
        TSB = [sba("TSBf0", [128, BLK], BF16), sba("TSBf1", [128, BLK], BF16)]
        SPS = [sba("SPS0", [128, 4 * ST], F32), sba("SPS1", [128, 4 * ST], F32)]
        DTT = sba("DTT", [128, 4 * ST], BF16)
        p_a = [ph.enter_context(nc.psum_tensor("pf_a0", [128, 512], F32)),
               ph.enter_context(nc.psum_tensor("pf_a1", [128, 512], F32))]
        lwF, wcF = lw.v, wc.v

        @block.sync
        def _(sync):
            sync.dma_start(out=WLf[:, :], in_=dtwt).then_inc(lw.h, 16)
            lw.v += 16

        @block.scalar
        def _(sc):
            sc.wait_ge(lw.h, lwF + 16)
            i = sc.activation(DTW[:, :], WLf[:, :], AF.Copy, scale=1.0)
            wc.inc(i)

        ldF, mmF, evF, dvF, stqF = ld.v, mmA.v, evA.v, dvA.v, stq.v

        @block.sync
        def _(sync):
            for b in range(NBLK):
                sync.wait_ge(mmA.h, max(mmF, mmF + (b - 1) * NCC * NSUB))
                if b >= 1:
                    sync.wait_ge(ld.h, ldF + b * 16)
                sync.dma_start(out=TSB[b % 2][:, :],
                               in_=ts_d[:, b * BLK:(b + 1) * BLK]
                               ).then_inc(ld.h, 16)
            ld.v += NBLK * 16

        @block.tensor
        def _(t):
            t.wait_ge(wc.h, wcF + 1)
            for b in range(NBLK):
                t.wait_ge(ld.h, ldF + (b + 1) * 16)
                for cc in range(NCC):
                    for sub in range(NSUB):
                        nmm = (b * NCC + cc) * NSUB + sub
                        if nmm >= 2:
                            t.wait_ge(evA.h, evF + 4 * (nmm - 1))
                        i = t.matmul(p_a[nmm % 2][:, 0:ST],
                                     DTW[:, cc * 128:(cc + 1) * 128],
                                     TSB[b % 2][:, sub * ST:(sub + 1) * ST],
                                     start=True, stop=True)
                        mmA.inc(i)

        @block.scalar
        def _(sc):
            for b in range(NBLK):
                for cc in range(NCC):
                    for sub in range(NSUB):
                        nmm = (b * NCC + cc) * NSUB + sub
                        sc.wait_ge(mmA.h, mmF + nmm + 1)
                        if nmm >= 1:
                            sc.wait_ge(dvA.h, dvF + nmm)
                        sl = SPS[nmm % 2]
                        bias = dtb_sb[:, cc:cc + 1]
                        sc.activation(sl[:, 0:ST], p_a[nmm % 2][:, 0:ST],
                                      AF.Abs, bias=bias, scale=1.0)
                        sc.activation(sl[:, ST:2 * ST], sl[:, 0:ST], AF.Exp,
                                      scale=-1.0)
                        sc.activation(sl[:, 2 * ST:3 * ST], sl[:, ST:2 * ST],
                                      AF.Ln, bias=1.0, scale=1.0)
                        i = sc.activation(sl[:, 3 * ST:4 * ST],
                                          p_a[nmm % 2][:, 0:ST],
                                          AF.Relu, bias=bias, scale=1.0)
                        evA.inc(i, 4)

        @block.vector
        def _(v):
            for b in range(NBLK):
                for cc in range(NCC):
                    for sub in range(NSUB):
                        nmm = (b * NCC + cc) * NSUB + sub
                        v.wait_ge(evA.h, evF + 4 * (nmm + 1))
                        if nmm >= 4:
                            v.wait_ge(stq.h, stqF + (nmm - 3) * 16)
                        sl = SPS[nmm % 2]
                        dst = DTT[:, (nmm % 4) * ST:(nmm % 4 + 1) * ST]
                        v.tensor_tensor(dst, sl[:, 2 * ST:3 * ST],
                                        sl[:, 3 * ST:4 * ST], OP.add)
                        i = v.tensor_scalar(dst, dst, mask_sb[:, b:b + 1],
                                            None, OP.mult)
                        dvA.inc(i)

        @block.gpsimd
        def _(sync):
            for b in range(NBLK):
                for cc in range(NCC):
                    for sub in range(NSUB):
                        nmm = (b * NCC + cc) * NSUB + sub
                        sync.wait_ge(dvA.h, dvF + nmm + 1)
                        if nmm >= 1:
                            sync.wait_ge(stq.h, stqF + nmm * 16)
                        t0 = b * BLK + sub * ST
                        sync.dma_start(out=dt_d[cc][:, t0:t0 + ST],
                                       in_=DTT[:, (nmm % 4) * ST:(nmm % 4 + 1) * ST]
                                       ).then_inc(stq.h, 16)
            stq.v += NBLK * NCC * NSUB * 16
            sync.wait_ge(stq.h, stq.v)
        barrier()

    _ns_name, _ns_id = _ns_stack.pop(); nc.leave_named_scope(_ns_name, _ns_id, False)
    _ns_stack.append(("phG1", nc.enter_named_scope("phG1", False)[0]))
    with contextlib.ExitStack() as ph:
        sba = lambda n, s, d: ph.enter_context(nc.sbuf_tensor(n, s, d))
        BBC = sba("BBCg1", [128, NST * BLK], BF16)
        BCT = [sba("BCTg10", [32, BLK], BF16), sba("BCTg11", [32, BLK], BF16)]
        DTU = [sba("DTUg10", [128, BLK], BF16), sba("DTUg11", [128, BLK], BF16)]
        UU = [sba("UUg10", [128, BLK], BF16), sba("UUg11", [128, BLK], BF16)]
        DU = [sba("DUg10", [128, BLK], BF16), sba("DUg11", [128, BLK], BF16)]
        DBU = [sba(f"DBUg1{i}", [128, BLK], BF16) for i in range(4)]
        EE = [sba(f"EEg1{i}", [128, BLK], BF16) for i in range(4)]
        VV = sba("VVg1", [128, BLK], BF16)
        WW = [sba("WWg10", [128, BLK], BF16), sba("WWg11", [128, BLK], BF16)]
        ONES = sba("ONESg1", [128, BLK], BF16)
        CS = [sba("CSg10", [128, BLK], F32), sba("CSg11", [128, BLK], F32)]
        RR = [sba("RRg10", [128, BLK], F32), sba("RRg11", [128, BLK], F32)]
        TOT = sba("TOTg1", [128, 2], F32)
        PFP = [sba("PFPg10", [128, NST], F32), sba("PFPg11", [128, NST], F32)]
        PEX = [sba("PEXg10", [128, NST], F32), sba("PEXg11", [128, NST], F32)]
        QQ = [sba("QQg10", [128, NST], F32), sba("QQg11", [128, NST], F32)]
        SCR = sba("SCRg1", [128, NST], F32)
        p_a = [ph.enter_context(nc.psum_tensor("pg1_a0", [128, 512], F32)),
               ph.enter_context(nc.psum_tensor("pg1_a1", [128, 512], F32))]

        POOL_NS = [n for n in range(NST) if n not in (0, 3, 6, 9, 12)]
        PK = {n: k for k, n in enumerate(POOL_NS)}
        NPN = len(POOL_NS)
        U1 = TB0 * NCC
        ld1, lw1, mm1, ev1, evB1 = ld.v, lw.v, mmA.v, evA.v, evB.v
        dv1, dvB1, hs1, gq1 = dvA.v, dvB.v, hs.v, gq.v
        DVR = lambda j: dv1 + 1 + 2 * j + 2
        DVU = lambda j: dv1 + 1 + 2 * j + 1

        @block.sync
        def _(sync):
            for b in range(TB0):
                sync.wait_ge(mmA.h, mm1 + (32 * (b - 1) if b >= 2 else 0))
                if b >= 1:
                    sync.wait_ge(lw.h, lw1 + b * 16)
                sync.dma_start(out=BCT[b % 2][:, :],
                               in_=bc_d[:, b * BLK:(b + 1) * BLK]
                               ).then_inc(lw.h, 16)
                for cc in range(NCC):
                    j = b * NCC + cc
                    if j >= 2:
                        sync.wait_ge(dvA.h, DVR(j - 2))
                    if j >= 1:
                        sync.wait_ge(ld.h, ld1 + 32 * j)
                    sync.dma_start(out=DTU[j % 2][:, :],
                                   in_=dt_d[cc][:, b * BLK:(b + 1) * BLK]
                                   ).then_inc(ld.h, 16)
                    sync.dma_start(out=UU[j % 2][:, :],
                                   in_=u_d[cc][:, b * BLK:(b + 1) * BLK]
                                   ).then_inc(ld.h, 16)
            ld.v += U1 * 32
            lw.v += TB0 * 16

        @block.tensor
        def _(t):
            nb1 = 0
            for b in range(TB0):
                t.wait_ge(lw.h, lw1 + (b + 1) * 16)
                if b >= 1:
                    t.wait_ge(gq.h, gq1 + NPN * b * NCC)
                    t.wait_ge(hs.h, hs1 + NST * b * NCC)
                for n in range(NST):
                    for sub in range(NSUB):
                        if nb1 >= 2:
                            t.wait_ge(evA.h, ev1 + nb1 - 1)
                        i = t.matmul(p_a[nb1 % 2][:, 0:ST],
                                     sel_sb[:, n * 128:(n + 1) * 128],
                                     BCT[b % 2][:, sub * ST:(sub + 1) * ST],
                                     start=True, stop=True)
                        mmA.inc(i)
                        nb1 += 1

        @block.scalar
        def _(sc):
            ne1 = 0
            for b in range(TB0):
                for n in range(NST):
                    for sub in range(NSUB):
                        sc.wait_ge(mmA.h, mm1 + ne1 + 1)
                        i = sc.activation(
                            BBC[:, n * BLK + sub * ST:n * BLK + (sub + 1) * ST],
                            p_a[ne1 % 2][:, 0:ST], AF.Copy, scale=1.0)
                        evA.inc(i)
                        ne1 += 1
                for cc in range(NCC):
                    j = b * NCC + cc
                    sc.wait_ge(dvA.h, DVR(j))
                    if j >= 2:
                        sc.wait_ge(dvB.h, dvB1 + (j - 1))
                    i = sc.activation(PEX[j % 2][:, :], PFP[j % 2][:, :],
                                      AF.Exp, scale=1.0)
                    evB.inc(i)
                    for n in range(NST):
                        if n >= 4:
                            sc.wait_ge(hs.h, hs1 + NST * j + n - 3)
                        i = sc.activation(
                            EE[n % 4][:, :], RR[j % 2][:, :], AF.Exp,
                            scale=a_sb[:, cc * NST + n:cc * NST + n + 1])
                        evB.inc(i)

        @block.vector
        def _(v):
            def state_update(u):
                v.wait_ge(hs.h, hs1 + NST * (u + 1))
                if u >= NCC:
                    v.wait_ge(dvB.h, dvB1 + (u - NCC) + 1)
                ccu = u % NCC
                c16 = state_sb[:, ccu * NST:(ccu + 1) * NST]
                v.tensor_tensor(SCR[:, :], c16, PEX[u % 2][:, :], OP.mult)
                i = v.tensor_tensor(c16, SCR[:, :], QQ[u % 2][:, :], OP.add)
                dvB.inc(i)

            i = v.memset(ONES[:, :], 1.0)
            dvA.inc(i)
            v.wait_ge(dvA.h, dv1 + 1)
            for b in range(TB0):
                for cc in range(NCC):
                    j = b * NCC + cc
                    v.wait_ge(ld.h, ld1 + 32 * (j + 1))
                    if j >= 2:
                        v.wait_ge(gq.h, gq1 + NPN * (j - 1))
                        v.wait_ge(evB.h, evB1 + 17 * (j - 1))
                    i = v.tensor_tensor(DU[j % 2][:, :], DTU[j % 2][:, :],
                                        UU[j % 2][:, :], OP.mult)
                    dvA.inc(i)
                    v.reduce_sum(TOT[:, j % 2:j % 2 + 1], DTU[j % 2][:, :],
                                 axis=mybir.AxisListType.X)
                    v.tensor_tensor_scan(CS[j % 2][:, :], ONES[:, :],
                                         DTU[j % 2][:, :], 0.0,
                                         OP.mult, OP.add)
                    v.tensor_scalar(RR[j % 2][:, :], CS[j % 2][:, :], -1.0,
                                    TOT[:, j % 2:j % 2 + 1], OP.mult, OP.add)
                    i = v.tensor_scalar(PFP[j % 2][:, :],
                                        a_sb[:, cc * NST:(cc + 1) * NST],
                                        TOT[:, j % 2:j % 2 + 1], None,
                                        OP.mult)
                    dvA.inc(i)
                    if j >= 1:
                        state_update(j - 1)
                    v.wait_ge(evA.h, ev1 + 32 * (b + 1))
                    for n in range(NST):
                        v.wait_ge(evB.h, evB1 + 17 * j + n + 2)
                        if n in PK:
                            k = PK[n]
                            gidx = j * NPN + k
                            v.wait_ge(gq.h, gq1 + gidx + 1)
                            i = v.scalar_tensor_tensor(
                                WW[n % 2][:, :], EE[n % 4][:, :], 1.0,
                                DBU[gidx % 4][:, :], OP.mult, OP.mult,
                                accum_out=QQ[j % 2][:, n:n + 1])
                        else:
                            v.tensor_tensor(VV[:, :], EE[n % 4][:, :],
                                            DU[j % 2][:, :], OP.mult)
                            i = v.scalar_tensor_tensor(
                                WW[n % 2][:, :], VV[:, :], 1.0,
                                BBC[:, n * BLK:(n + 1) * BLK], OP.mult,
                                OP.mult, accum_out=QQ[j % 2][:, n:n + 1])
                        hs.inc(i)
            state_update(U1 - 1)

        @block.gpsimd
        def _(g):
            for b in range(TB0):
                for cc in range(NCC):
                    j = b * NCC + cc
                    g.wait_ge(dvA.h, DVU(j))
                    g.wait_ge(evA.h, ev1 + 32 * (b + 1))
                    for k, n in enumerate(POOL_NS):
                        gidx = j * NPN + k
                        if gidx >= 4:
                            pj, pk = divmod(gidx - 4, NPN)
                            g.wait_ge(hs.h,
                                      hs1 + NST * pj + POOL_NS[pk] + 1)
                        i = g.tensor_tensor(DBU[gidx % 4][:, :],
                                            DU[j % 2][:, :],
                                            BBC[:, n * BLK:(n + 1) * BLK],
                                            OP.mult)
                        gq.inc(i)
        barrier()

    _ns_name, _ns_id = _ns_stack.pop(); nc.leave_named_scope(_ns_name, _ns_id, False)
    _ns_stack.append(("phG2", nc.enter_named_scope("phG2", False)[0]))
    with contextlib.ExitStack() as ph:
        sba = lambda n, s, d: ph.enter_context(nc.sbuf_tensor(n, s, d))
        BBC = sba("BBC", [128, NST * BLK], BF16)
        CBC = sba("CBC", [128, NST * BLK], BF16)
        BCT = [sba("BCT0", [32, BLK], BF16), sba("BCT1", [32, BLK], BF16)]
        DTU = [sba("DTU0", [128, BLK], BF16), sba("DTU1", [128, BLK], BF16)]
        UU = [sba("UU0", [128, BLK], BF16), sba("UU1", [128, BLK], BF16)]
        SGU = [sba("SGU0", [128, BLK], BF16), sba("SGU1", [128, BLK], BF16)]
        DU = sba("DU", [128, BLK], BF16)
        DBU = [sba("DBU0", [128, BLK], BF16), sba("DBU1", [128, BLK], BF16)]
        SS = [sba("SS0", [128, BLK], BF16), sba("SS1", [128, BLK], BF16)]
        SCT = [sba("SCT0", [128, BLK], BF16), sba("SCT1", [128, BLK], BF16)]
        YG = [sba("YG0", [128, BLK], BF16), sba("YG1", [128, BLK], BF16)]
        DA = [sba("DA0", [128, BLK], F32), sba("DA1", [128, BLK], F32)]
        YY = sba("YY", [128, BLK], F32)
        p_a = [ph.enter_context(nc.psum_tensor("pg_a0", [128, 512], F32)),
               ph.enter_context(nc.psum_tensor("pg_a1", [128, 512], F32))]
        BH = BLK // 2
        p_y = [[ph.enter_context(nc.psum_tensor("pg_y00", [128, BH], F32)),
                ph.enter_context(nc.psum_tensor("pg_y01", [128, BH], F32))],
               [ph.enter_context(nc.psum_tensor("pg_y10", [128, BH], F32)),
                ph.enter_context(nc.psum_tensor("pg_y11", [128, BH], F32))]]

        U2 = TAIL_BLK * NCC
        ld2, lw2, mm2, ev2, ac2 = ld.v, lw.v, mmA.v, evA.v, evB.v
        dv2, dv2Y, hs2, gq2, stq2 = dvA.v, dvB.v, hs.v, gq.v, stq.v
        evC2, mmY2 = evC.v, mmY.v
        W_DU2 = lambda j: dv2 + 1 if j == 0 else dv2 + 17 * j

        @block.sync
        def _(sync):
            for bi in range(TAIL_BLK):
                b = TB0 + bi
                sync.wait_ge(mmA.h, mm2 + (64 * (bi - 1) if bi >= 2 else 0))
                if bi >= 1:
                    sync.wait_ge(lw.h, lw2 + bi * 16)
                sync.dma_start(out=BCT[bi % 2][:, :],
                               in_=bc_d[:, b * BLK:(b + 1) * BLK]
                               ).then_inc(lw.h, 16)
                for cc in range(NCC):
                    j = bi * NCC + cc
                    if j >= 2:
                        sync.wait_ge(dvA.h, W_DU2(j - 2))
                        sync.wait_ge(evB.h, ac2 + (j - 1) * NST)
                        sync.wait_ge(dvB.h, dv2Y + j - 1)
                    if j >= 1:
                        sync.wait_ge(ld.h, ld2 + 48 * j)
                    sync.dma_start(out=DTU[j % 2][:, :],
                                   in_=dt_d[cc][:, b * BLK:(b + 1) * BLK]
                                   ).then_inc(ld.h, 16)
                    sync.dma_start(out=UU[j % 2][:, :],
                                   in_=u_d[cc][:, b * BLK:(b + 1) * BLK]
                                   ).then_inc(ld.h, 16)
                    sync.dma_start(out=SGU[j % 2][:, :],
                                   in_=sg_d[cc][:, bi * BLK:(bi + 1) * BLK]
                                   ).then_inc(ld.h, 16)
            ld.v += U2 * 48
            lw.v += TAIL_BLK * 16

        @block.tensor
        def _(t):
            nbc = 0
            for bi in range(TAIL_BLK):
                t.wait_ge(lw.h, lw2 + (bi + 1) * 16)
                if bi >= 1:
                    t.wait_ge(gq.h, gq2 + 16 * bi * NCC)
                    t.wait_ge(evC.h, evC2 + bi * NCC * NST)
                for rep in range(2):
                    for n in range(NST):
                        for sub in range(NSUB):
                            if nbc >= 2:
                                t.wait_ge(evA.h, ev2 + nbc - 1)
                            selcol = (rep * NST + n) * 128
                            i = t.matmul(
                                p_a[nbc % 2][:, 0:ST],
                                sel_sb[:, selcol:selcol + 128],
                                BCT[bi % 2][:, sub * ST:(sub + 1) * ST],
                                start=True, stop=True)
                            mmA.inc(i)
                            nbc += 1
                for cc in range(NCC):
                    j = bi * NCC + cc
                    if j >= 2:
                        t.wait_ge(dvB.h, dv2Y + j - 1)
                    for n in range(NST):
                        t.wait_ge(evC.h, evC2 + j * NST + n + 1)
                        for h in range(2):
                            i = t.matmul(p_y[j % 2][h][:, :], idnb[:, :],
                                         SCT[n % 2][:, h * BH:(h + 1) * BH],
                                         start=(n == 0), stop=(n == NST - 1))
                            mmY.inc(i)

        @block.scalar
        def _(sc):
            nev = 0
            for bi in range(TAIL_BLK):
                for rep in range(2):
                    dstA = BBC if rep == 0 else CBC
                    for n in range(NST):
                        for sub in range(NSUB):
                            sc.wait_ge(mmA.h, mm2 + nev + 1)
                            i = sc.activation(
                                dstA[:, n * BLK + sub * ST:
                                     n * BLK + (sub + 1) * ST],
                                p_a[nev % 2][:, 0:ST], AF.Copy, scale=1.0)
                            evA.inc(i)
                            nev += 1
                for cc in range(NCC):
                    j = bi * NCC + cc
                    sc.wait_ge(ld.h, ld2 + 48 * (j + 1))
                    for n in range(NST):
                        g = j * NST + n
                        if g >= 2:
                            sc.wait_ge(hs.h, hs2 + g - 1)
                        i = sc.activation(
                            DA[g % 2][:, :], DTU[j % 2][:, :], AF.Exp,
                            scale=a_sb[:, cc * NST + n:cc * NST + n + 1])
                        evB.inc(i)

        @block.vector
        def _(v):
            v.wait_ge(dvB.h, dv2Y)
            v.wait_ge(ld.h, ld2 + 48)
            i = v.tensor_tensor(DU[:, :], DTU[0][:, :], UU[0][:, :], OP.mult)
            dvA.inc(i)
            for bi in range(TAIL_BLK):
                for cc in range(NCC):
                    j = bi * NCC + cc
                    g0 = j * NST
                    v.wait_ge(evA.h, ev2 + 64 * (bi + 1))
                    for n in range(NST):
                        g = g0 + n
                        v.wait_ge(gq.h, gq2 + 16 * j + n + 1)
                        v.wait_ge(evB.h, ac2 + g + 1)
                        i = v.tensor_tensor_scan(
                            SS[n % 2][:, :], DA[g % 2][:, :], DBU[n % 2][:, :],
                            state_sb[:, cc * NST + n:cc * NST + n + 1],
                            OP.mult, OP.add)
                        hs.inc(i)
                        if n == 0:
                            if j >= 1:
                                ccp = (j - 1) % NCC
                                i = v.tensor_copy(
                                    state_sb[:, ccp * NST + NST - 1:
                                             ccp * NST + NST],
                                    SS[1][:, BLK - 1:BLK])
                                dvA.inc(i)
                        else:
                            m = n - 1
                            i = v.tensor_copy(
                                state_sb[:, cc * NST + m:cc * NST + m + 1],
                                SS[m % 2][:, BLK - 1:BLK])
                            dvA.inc(i)
                            if m >= 2:
                                v.wait_ge(mmY.h,
                                          mmY2 + j * 2 * NST + 2 * (m - 1))
                            i = v.tensor_tensor(
                                SCT[m % 2][:, :], SS[m % 2][:, :],
                                CBC[:, m * BLK:(m + 1) * BLK], OP.mult)
                            evC.inc(i)
                    if j < U2 - 1:
                        v.wait_ge(ld.h, ld2 + 48 * (j + 2))
                        i = v.tensor_tensor(DU[:, :], DTU[(j + 1) % 2][:, :],
                                            UU[(j + 1) % 2][:, :], OP.mult)
                        dvA.inc(i)
                    m = NST - 1
                    v.wait_ge(hs.h, hs.v)
                    v.wait_ge(mmY.h, mmY2 + j * 2 * NST + 2 * (m - 1))
                    i = v.tensor_tensor(
                        SCT[m % 2][:, :], SS[m % 2][:, :],
                        CBC[:, m * BLK:(m + 1) * BLK], OP.mult)
                    evC.inc(i)
                    v.wait_ge(mmY.h, mmY2 + (j + 1) * 2 * NST)
                    if j >= 2:
                        v.wait_ge(stq.h, stq2 + (j - 1) * 16)
                    v.scalar_tensor_tensor(YY[:, 0:BH], UU[j % 2][:, 0:BH],
                                           dpar_sb[:, cc:cc + 1],
                                           p_y[j % 2][0][:, :],
                                           OP.mult, OP.add)
                    v.scalar_tensor_tensor(YY[:, BH:BLK],
                                           UU[j % 2][:, BH:BLK],
                                           dpar_sb[:, cc:cc + 1],
                                           p_y[j % 2][1][:, :],
                                           OP.mult, OP.add)
                    v.wait_ge(ld.h, ld2 + 48 * (j + 1))
                    i = v.tensor_tensor(YG[j % 2][:, :], YY[:, :],
                                        SGU[j % 2][:, :], OP.mult)
                    dvB.inc(i)
                    if j == U2 - 1:
                        i = v.tensor_copy(
                            state_sb[:, cc * NST + NST - 1:cc * NST + NST],
                            SS[1][:, BLK - 1:BLK])
                        dvA.inc(i)

        @block.gpsimd
        def _(sync):
            for bi in range(TAIL_BLK):
                for cc in range(NCC):
                    j = bi * NCC + cc
                    g0 = j * NST
                    sync.wait_ge(dvA.h, W_DU2(j))
                    sync.wait_ge(evA.h, ev2 + 64 * (bi + 1))
                    for n in range(NST):
                        g = g0 + n
                        if n == 2 and j >= 1:
                            m = j - 1
                            sync.wait_ge(dvB.h, dv2Y + m + 1)
                            if m >= 1:
                                sync.wait_ge(stq.h, stq2 + m * 16)
                            ccp = m % NCC
                            t0 = (m // NCC) * BLK
                            slot_dma_out(sync, sh_yg[0, ccp][:, t0:t0 + BLK],
                                         sh_yg[1, ccp][:, t0:t0 + BLK],
                                         YG[m % 2][:, :], stq, reg=rg_slot)
                        if g >= 2:
                            sync.wait_ge(hs.h, hs2 + g - 1)
                        i = sync.tensor_tensor(DBU[n % 2][:, :], DU[:, :],
                                               BBC[:, n * BLK:(n + 1) * BLK],
                                               OP.mult)
                        gq.inc(i)
            m = U2 - 1
            sync.wait_ge(dvB.h, dv2Y + m + 1)
            sync.wait_ge(stq.h, stq2 + m * 16)
            ccp = m % NCC
            t0 = (m // NCC) * BLK
            slot_dma_out(sync, sh_yg[0, ccp][:, t0:t0 + BLK],
                         sh_yg[1, ccp][:, t0:t0 + BLK],
                         YG[m % 2][:, :], stq, reg=rg_slot)
            sync.wait_ge(stq.h, stq.v)

        @block.sync
        def _(sync):
            sync.wait_ge(stq.h, stq.v)
            flag_write(sync, 1, 2)
        barrier()

    _ns_name, _ns_id = _ns_stack.pop(); nc.leave_named_scope(_ns_name, _ns_id, False)
    _ns_stack.append(("phH", nc.enter_named_scope("phH", False)[0]))
    with contextlib.ExitStack() as ph:
        sba = lambda n, s, d: ph.enter_context(nc.sbuf_tensor(n, s, d))
        NKG = 2 * NCC
        YGR = sba("YGR", [128, NKG * TLH], BF16)
        WOB = sba("WOB", [128, NKG * 128], BF16)
        WLh = [sba("WLh0", [128, NKG * 128], F32),
               sba("WLh1", [128, NKG * 128], F32)]
        SEV = sba("SEV", [128, 4 * ST], F32)
        p_a = [ph.enter_context(nc.psum_tensor("ph_a0", [128, 512], F32)),
               ph.enter_context(nc.psum_tensor("ph_a1", [128, 512], F32))]
        ldH, lwH, wcH, mmH, evH, stqH = ld.v, lw.v, wc.v, mmA.v, evA.v, stq.v
        NTT = TLH // ST

        @block.sync
        def _(sync):
            flag_poll(sync, 1, 2)
            for ss in range(2):
                for cc in range(NCC):
                    kc = ss * NCC + cc
                    slot_dma_in(sync, YGR[:, kc * TLH:(kc + 1) * TLH],
                                sh_yg[ss, cc][:, 0:TLH],
                                sh_yg[ss, cc][:, TLH:2 * TLH], ld)
            for hp in range(NHC):
                sync.wait_ge(wc.h, max(wcH, wcH + hp - 1))
                if hp >= 1:
                    sync.wait_ge(lw.h, lwH + hp * 16)
                sync.dma_start(out=WLh[hp % 2][:, :], in_=woutt[hp]
                               ).then_inc(lw.h, 16)
            lw.v += NHC * 16

        @block.scalar
        def _(sc):
            for hp in range(NHC):
                sc.wait_ge(lw.h, lwH + (hp + 1) * 16)
                if hp >= 1:
                    sc.wait_ge(mmA.h, mmH + hp * NTT)
                i = sc.activation(WOB[:, :], WLh[hp % 2][:, :], AF.Copy,
                                  scale=1.0)
                wc.inc(i)
                for tt in range(NTT):
                    ntile = hp * NTT + tt
                    sc.wait_ge(mmA.h, mmH + ntile + 1)
                    if ntile >= 4:
                        sc.wait_ge(stq.h, stqH + (ntile - 3) * 16)
                    i = sc.activation(
                        SEV[:, (ntile % 4) * ST:(ntile % 4 + 1) * ST],
                        p_a[ntile % 2][:, 0:ST], AF.Copy, scale=1.0)
                    evA.inc(i)

        @block.tensor
        def _(t):
            t.wait_ge(ld.h, ldH + NKG * 16)
            for hp in range(NHC):
                t.wait_ge(wc.h, wcH + hp + 1)
                for tt in range(NTT):
                    ntile = hp * NTT + tt
                    if ntile >= 2:
                        t.wait_ge(evA.h, evH + ntile - 1)
                    for kc in range(NKG):
                        i = t.matmul(p_a[ntile % 2][:, 0:ST],
                                     WOB[:, kc * 128:(kc + 1) * 128],
                                     YGR[:, kc * TLH + tt * ST:kc * TLH + (tt + 1) * ST],
                                     start=(kc == 0), stop=(kc == NKG - 1))
                    mmA.inc(i)

        @block.gpsimd
        def _(sync):
            for hp in range(NHC):
                for tt in range(NTT):
                    ntile = hp * NTT + tt
                    sync.wait_ge(evA.h, evH + ntile + 1)
                    if ntile >= 1:
                        sync.wait_ge(stq.h, stqH + ntile * 16)
                    sync.dma_start(out=s_d[hp][:, tt * ST:(tt + 1) * ST],
                                   in_=SEV[:, (ntile % 4) * ST:(ntile % 4 + 1) * ST]
                                   ).then_inc(stq.h, 16)
            stq.v += NHC * NTT * 16
            sync.wait_ge(stq.h, stq.v)
        barrier()

    _ns_name, _ns_id = _ns_stack.pop(); nc.leave_named_scope(_ns_name, _ns_id, False)
    _ns_stack.append(("phI", nc.enter_named_scope("phI", False)[0]))
    NSB = max(1, TLH // BLK)
    SBS = TLH // NSB
    NS2 = SBS // ST
    for sb_i in range(NSB):
        with contextlib.ExitStack() as ph:
            sba = (lambda sfx: (lambda n, s, d: ph.enter_context(
                nc.sbuf_tensor(n + sfx, s, d))))(f"_sb{sb_i}")
            XNS = sba("XNS", [128, NKC * SBS], BF16)
            HDN = sba("HDN", [128, NIC * SBS], BF16)
            GWB = sba("GWB", [128, NKC * 128], BF16)
            UWB = sba("UWB", [128, NKC * 128], BF16)
            DWB = sba("DWB", [128, NIC * 128], BF16)
            GUS = sba("GUS", [128, 4 * ST], F32)
            SGF = sba("SGF", [128, 2 * ST], F32)
            SRES = sba("SRES", [128, 2 * ST], F32)
            p_g = ph.enter_context(nc.psum_tensor(f"pi_g{sb_i}", [128, 512], F32))
            p_u = ph.enter_context(nc.psum_tensor(f"pi_u{sb_i}", [128, 512], F32))
            p_d = [ph.enter_context(nc.psum_tensor(f"pi_d0{sb_i}", [128, 512], F32)),
                   ph.enter_context(nc.psum_tensor(f"pi_d1{sb_i}", [128, 512], F32))]

            ldI, lwI, wcI, lgI = ld.v, lw.v, wc.v, lg.v
            mmI, evI, dvI = mmA.v, evA.v, dvA.v
            sA = F - TL + 0 * TLH + sb_i * SBS
            sB = F - TL + 1 * TLH + sb_i * SBS

            @block.sync
            def _(sync):
                for kc in range(NKC):
                    bA, cA = sA // BLK, sA % BLK
                    bB, cB = sB // BLK, sB % BLK
                    slot_dma_in(sync, XNS[:, kc * SBS:(kc + 1) * SBS],
                                xnt_d[bA, kc][:, cA:cA + SBS],
                                xnt_d[bB, kc][:, cB:cB + SBS], ld)

            @block.gpsimd
            def _(g):
                for ic in range(NIC):
                    if ic >= 1:
                        g.wait_ge(mmA.h, mmI + 2 * ic * NS2)
                        g.wait_ge(lg.h, lgI + 32 * ic)
                    g.dma_start(out=GWB[:, :], in_=gatet[ic]).then_inc(lg.h, 16)
                    g.wait_ge(lg.h, lgI + 32 * ic + 16)
                    g.dma_start(out=UWB[:, :], in_=upt[ic]).then_inc(lg.h, 16)
                lg.v += NIC * 32

            @block.scalar
            def _(sc):
                for ic in range(NIC):
                    for s2 in range(NS2):
                        nmm = ic * NS2 + s2
                        sc.wait_ge(mmA.h, mmI + 2 * nmm + 1)
                        if nmm >= 1:
                            sc.wait_ge(dvA.h, dvI + nmm)
                        sc.activation(GUS[:, 0:ST], p_g[:, 0:ST], AF.Copy,
                                      scale=fsc_sb[:, ic:ic + 1])
                        i = sc.activation(GUS[:, ST:2 * ST], GUS[:, 0:ST],
                                          AF.Silu, scale=1.0)
                        evA.inc(i)
                        sc.wait_ge(mmA.h, mmI + 2 * nmm + 2)
                        i = sc.activation(
                            GUS[:, 2 * ST:3 * ST], p_u[:, 0:ST], AF.Copy,
                            scale=fsc_sb[:, NIC + ic:NIC + ic + 1])
                        evA.inc(i)

            @block.tensor
            def _(t):
                t.wait_ge(ld.h, ldI + NKC * 16)
                for ic in range(NIC):
                    for s2 in range(NS2):
                        nmm = ic * NS2 + s2
                        t.wait_ge(lg.h, lgI + 32 * ic + 16)
                        if nmm >= 1:
                            t.wait_ge(evA.h, evI + 2 * nmm - 1)
                        for kc in range(NKC):
                            i = t.matmul(
                                p_g[:, 0:ST], GWB[:, kc * 128:(kc + 1) * 128],
                                XNS[:, kc * SBS + s2 * ST:kc * SBS + (s2 + 1) * ST],
                                start=(kc == 0), stop=(kc == NKC - 1))
                        mmA.inc(i)
                        t.wait_ge(lg.h, lgI + 32 * ic + 32)
                        if nmm >= 1:
                            t.wait_ge(evA.h, evI + 2 * nmm)
                        for kc in range(NKC):
                            i = t.matmul(
                                p_u[:, 0:ST], UWB[:, kc * 128:(kc + 1) * 128],
                                XNS[:, kc * SBS + s2 * ST:kc * SBS + (s2 + 1) * ST],
                                start=(kc == 0), stop=(kc == NKC - 1))
                        mmA.inc(i)


            @block.vector
            def _(v):
                for ic in range(NIC):
                    for s2 in range(NS2):
                        nmm = ic * NS2 + s2
                        v.wait_ge(evA.h, evI + 2 * nmm + 2)
                        i = v.tensor_tensor(
                            HDN[:, ic * SBS + s2 * ST:ic * SBS + (s2 + 1) * ST],
                            GUS[:, ST:2 * ST], GUS[:, 2 * ST:3 * ST], OP.mult)
                        dvA.inc(i)

            ldI2, lwI2, wcI2, lgI2 = ld.v, lw.v, wc.v, lg.v
            mmT, evT, evS, dvS, stqI = mmB.v, evB.v, evC.v, dvB.v, stq.v

            @block.gpsimd
            def _(g):
                def sgf_store(nmm):
                    hc2, s22 = divmod(nmm, NS2)
                    g.wait_ge(dvB.h, dvS + nmm + 1)
                    if nmm >= 1:
                        g.wait_ge(stq.h, stqI + nmm * 16)
                    g.dma_start(
                        out=sgf_d[hc2][:, sb_i * SBS + s22 * ST:sb_i * SBS + (s22 + 1) * ST],
                        in_=SGF[:, (nmm % 2) * ST:(nmm % 2 + 1) * ST]
                        ).then_inc(stq.h, 16)

                for hc in range(NHC):
                    if hc >= 1:
                        g.wait_ge(mmB.h, mmT + hc * NS2)
                        g.wait_ge(lg.h, lgI2 + hc * 16)
                    g.dma_start(out=DWB[:, :], in_=downt[hc]).then_inc(lg.h, 16)
                    if hc >= 1:
                        for s2 in range(NS2):
                            sgf_store((hc - 1) * NS2 + s2)
                for s2 in range(NS2):
                    sgf_store((NHC - 1) * NS2 + s2)
                stq.v += NHC * NS2 * 16
                g.wait_ge(stq.h, stq.v)
                lg.v += NHC * 16

            @block.sync
            def _(sync):
                for hc in range(NHC):
                    for s2 in range(NS2):
                        nmm = hc * NS2 + s2
                        sync.wait_ge(dvB.h, max(dvS, dvS + nmm - 1))
                        if nmm >= 1:
                            sync.wait_ge(ld.h, ldI2 + nmm * 16)
                        sync.dma_start(
                            out=SRES[:, (nmm % 2) * ST:(nmm % 2 + 1) * ST],
                            in_=s_d[hc][:, sb_i * SBS + s2 * ST:sb_i * SBS + (s2 + 1) * ST]
                            ).then_inc(ld.h, 16)
                ld.v += NHC * NS2 * 16

            @block.scalar
            def _(sc):
                for hc in range(NHC):
                    for s2 in range(NS2):
                        nmm = hc * NS2 + s2
                        sc.wait_ge(mmB.h, mmT + nmm + 1)
                        if nmm >= 1:
                            sc.wait_ge(dvB.h, dvS + nmm)
                        sc.activation(
                            GUS[:, 3 * ST:4 * ST], p_d[nmm % 2][:, 0:ST],
                            AF.Copy,
                            scale=fsc_sb[:, 2 * NIC + hc:2 * NIC + hc + 1])
                        i = sc.activation(GUS[:, 3 * ST:4 * ST],
                                          GUS[:, 3 * ST:4 * ST], AF.Silu,
                                          scale=1.0)
                        evB.inc(i)

            @block.tensor
            def _(t):
                t.wait_ge(dvA.h, dvI + NIC * NS2)
                for hc in range(NHC):
                    t.wait_ge(lg.h, lgI2 + (hc + 1) * 16)
                    for s2 in range(NS2):
                        nmm = hc * NS2 + s2
                        if nmm >= 2:
                            t.wait_ge(evB.h, evT + nmm - 1)
                        for ic in range(NIC):
                            i = t.matmul(
                                p_d[nmm % 2][:, 0:ST],
                                DWB[:, ic * 128:(ic + 1) * 128],
                                HDN[:, ic * SBS + s2 * ST:ic * SBS + (s2 + 1) * ST],
                                start=(ic == 0), stop=(ic == NIC - 1))
                        mmB.inc(i)

            @block.vector
            def _(v):
                for hc in range(NHC):
                    for s2 in range(NS2):
                        nmm = hc * NS2 + s2
                        v.wait_ge(evB.h, evT + nmm + 1)
                        v.wait_ge(ld.h, ldI2 + (nmm + 1) * 16)
                        if nmm >= 2:
                            v.wait_ge(stq.h, stqI + (nmm - 1) * 16)
                        i = v.tensor_tensor(
                            SGF[:, (nmm % 2) * ST:(nmm % 2 + 1) * ST],
                            SRES[:, (nmm % 2) * ST:(nmm % 2 + 1) * ST],
                            GUS[:, 3 * ST:4 * ST], OP.mult)
                        dvB.inc(i)

            barrier()

    _ns_name, _ns_id = _ns_stack.pop(); nc.leave_named_scope(_ns_name, _ns_id, False)
    _ns_stack.append(("phJ", nc.enter_named_scope("phJ", False)[0]))
    with contextlib.ExitStack() as ph:
        sba = lambda n, s, d: ph.enter_context(nc.sbuf_tensor(n, s, d))
        SGP = [sba("SGP0", [128, NHC * 128], F32),
               sba("SGP1", [128, NHC * 128], F32)]
        OUTS = [sba("OUTS0", [128, D], F32), sba("OUTS1", [128, D], F32)]
        XRES = [sba("XRES0", [128, D], F32), sba("XRES1", [128, D], F32)]
        p_t = [ph.enter_context(nc.psum_tensor("pj_t0", [128, 512], F32)),
               ph.enter_context(nc.psum_tensor("pj_t1", [128, 512], F32))]
        NTB = TLH // 128
        ldJ, mmJ, evJ, dvJ, stqJ = ld.v, mmA.v, evA.v, dvA.v, stq.v
        TPG = min(4, NHC)
        ngJ = NHC // TPG

        @block.sync
        def _(sync):
            for tb in range(NTB):
                sync.wait_ge(mmA.h, max(mmJ, mmJ + (tb - 1) * ngJ))
                if tb >= 1:
                    sync.wait_ge(ld.h, ldJ + tb * (NHC + 1) * 16)
                for hc in range(NHC):
                    sync.dma_start(out=SGP[tb % 2][:, hc * 128:(hc + 1) * 128],
                                   in_=sgf_d[hc][:, tb * 128:(tb + 1) * 128]
                                   ).then_inc(ld.h, 16)
                sync.wait_ge(dvA.h, max(dvJ, dvJ + tb - 1))
                rA = F - TL + 0 * TLH + tb * 128
                rB = F - TL + 1 * TLH + tb * 128
                slot_dma_in(sync, XRES[tb % 2][:, :],
                            x_f[rA:rA + 128, :], x_f[rB:rB + 128, :], ld)
                ld.v += NHC * 16

        @block.tensor
        def _(t):
            for tb in range(NTB):
                t.wait_ge(ld.h, ldJ + (tb + 1) * (NHC + 1) * 16)
                for g in range(ngJ):
                    ng = tb * ngJ + g
                    if ng >= 2:
                        t.wait_ge(evA.h, evJ + ng - 1)
                    for j in range(TPG):
                        hc = g * TPG + j
                        i = t.transpose(p_t[ng % 2][:, j * 128:(j + 1) * 128],
                                        SGP[tb % 2][:, hc * 128:(hc + 1) * 128],
                                        idnf[:, :])
                    mmA.inc(i)

        @block.scalar
        def _(sc):
            for tb in range(NTB):
                for g in range(ngJ):
                    ng = tb * ngJ + g
                    sc.wait_ge(mmA.h, mmJ + ng + 1)
                    if tb >= 2 and g == 0:
                        sc.wait_ge(dvA.h, dvJ + tb - 1)
                        sc.wait_ge(stq.h, stqJ + (tb - 1) * 16)
                    for j in range(TPG):
                        hc = g * TPG + j
                        i = sc.activation(
                            OUTS[tb % 2][:, hc * 128:(hc + 1) * 128],
                            p_t[ng % 2][:, j * 128:(j + 1) * 128],
                            AF.Copy, scale=1.0)
                    evA.inc(i)

        @block.vector
        def _(v):
            for tb in range(NTB):
                v.wait_ge(evA.h, evJ + (tb + 1) * ngJ)
                v.wait_ge(ld.h, ldJ + (tb + 1) * (NHC + 1) * 16)
                if tb >= 2:
                    v.wait_ge(stq.h, stqJ + (tb - 1) * 16)
                i = v.tensor_tensor(OUTS[tb % 2][:, :], OUTS[tb % 2][:, :],
                                    XRES[tb % 2][:, :], OP.add)
                dvA.inc(i)

        @block.gpsimd
        def _(sync):
            for tb in range(NTB):
                sync.wait_ge(dvA.h, dvJ + tb + 1)
                if tb >= 1:
                    sync.wait_ge(stq.h, stqJ + tb * 16)
                sync.dma_start(out=out_r[tb * 128:(tb + 1) * 128, :],
                               in_=OUTS[tb % 2][:, :]).then_inc(stq.h, 16)
            stq.v += NTB * 16
            sync.wait_ge(stq.h, stq.v)
        barrier()

    _ns_name, _ns_id = _ns_stack.pop(); nc.leave_named_scope(_ns_name, _ns_id, False)
    if debug:
        d_ln = nc.dram_tensor("d_ln", [128, 512], F32, kind="ExternalOutput").ap()
        dbg_specs = [
            ("d_xnt", [NBLK, NKC, 128, BLK], xnt_d),
            ("d_upre", [NCC, 128, F], upre_d),
            ("d_u", [NCC, 128, F], u_d),
            ("d_ts", [128, F], ts_d),
            ("d_bc", [32, F], bc_d),
            ("d_dt", [NCC, 128, F], dt_d),
            ("d_sg", [NCC, 128, TL], sg_d),
            ("d_s", [NHC, 128, TLH], s_d),
            ("d_sgf", [NHC, 128, TLH], sgf_d),
            ("d_ygA", [NCC, 128, TL], sh_yg[0]),
            ("d_ygB", [NCC, 128, TL], sh_yg[1]),
        ]
        dbg_sem = Sem(st_, nc, "dbg")

        @block.sync
        def _(sync):
            for name, shape, src_t in dbg_specs:
                dt_out = F32 if name in ("d_s", "d_sgf") else BF16
                od = nc.dram_tensor(name, shape, dt_out, kind="ExternalOutput").ap()
                ap = src_t.ap() if callable(getattr(src_t, "ap", None)) else src_t
                sync.dma_start(out=od, in_=ap).then_inc(dbg_sem.h, 16)
                dbg_sem.v += 16
            sync.dma_start(out=d_ln, in_=lndbg[:, :]).then_inc(dbg_sem.h, 16)
            dbg_sem.v += 16
            sync.wait_ge(dbg_sem.h, dbg_sem.v)
        barrier()

    st_.close()
    return nc



def stage_inputs(c, inp, nonce):
    F, D, DI, DH = c["F"], c["D"], c["DI"], c["DH"]
    NCC, NKC, NIC, NHC = c["NCC"], c["NKC"], c["NIC"], c["NHC"]
    NST, BLK, NBLK, TL, TLH = c["NST"], c["BLK"], c["NBLK"], c["TL"], c["TLH"]
    bf = ml_dtypes.bfloat16

    x = np.asarray(inp["L0_attn_hidden_states"], np.float32).reshape(-1, D)
    w_in = np.asarray(inp["L0_attn_in_proj_w"], np.float32)
    conv_w = np.asarray(inp["L0_attn_conv_w"], np.float32)
    conv_b = np.asarray(inp["L0_attn_conv_b"], np.float32)
    xpw = np.asarray(inp["L0_attn_x_proj_w"], np.float32)
    dtw = np.asarray(inp["L0_attn_dt_proj_w"], np.float32)
    dtb = np.asarray(inp["L0_attn_dt_proj_b"], np.float32)
    alog = np.asarray(inp["L0_attn_A_log"], np.float32)
    dpar = np.asarray(inp["L0_attn_D_param"], np.float32)
    wout = np.asarray(inp["L0_attn_out_proj_w"], np.float32)
    gw = np.asarray(inp["L0_ffn_gate_w"], np.float32)
    uw = np.asarray(inp["L0_ffn_up_w"], np.float32)
    dw = np.asarray(inp["L0_ffn_down_w"], np.float32)
    gs = np.asarray(inp["L0_ffn_gate_s"], np.float32)
    us = np.asarray(inp["L0_ffn_up_s"], np.float32)
    ds = np.asarray(inp["L0_ffn_down_s"], np.float32)

    idnb = np.eye(128, dtype=bf)
    idnf = np.eye(128, dtype=np.float32)
    sel = np.zeros((32, 2 * NST * 128), dtype=bf)
    for jj in range(2 * NST):
        sel[jj, jj * 128:(jj + 1) * 128] = 1.0

    in_maps = []
    for core in range(N_CORES):
        q, slot = core // 2, core % 2
        pend = (q + 1) * TL
        pad = F - pend
        xf = np.zeros((F, D), np.float32)
        xf[pad:] = x[:pend]
        mask = np.zeros((128, NBLK), np.float32)
        mask[:, pad // BLK:] = 1.0
        ch0 = slot * DH

        wu = np.empty((NCC, 128, D), np.float32)
        wg = np.empty((NCC, 128, D), np.float32)
        for pc in range(NCC):
            blk_u = w_in[ch0 + pc * 128:ch0 + (pc + 1) * 128, :]
            blk_g = w_in[DI + ch0 + pc * 128:DI + ch0 + (pc + 1) * 128, :]
            wu[pc] = (blk_u.T.reshape(NKC, 128, 128)
                      .transpose(1, 0, 2).reshape(128, D))
            wg[pc] = (blk_g.T.reshape(NKC, 128, 128)
                      .transpose(1, 0, 2).reshape(128, D))

        xp = np.empty((128, NCC * 160), np.float32)
        for pc in range(NCC):
            xp[:, pc * 160:(pc + 1) * 160] = \
                xpw[:, ch0 + pc * 128:ch0 + (pc + 1) * 128].T
        dtt = dtw[ch0:ch0 + DH, :].T.copy()

        cvd = np.zeros((128, NCC * 512), np.float32)
        for pc in range(NCC):
            for k in range(128):
                for j in range(4):
                    cvd[k, pc * 512 + j * 128 + k] = \
                        conv_w[ch0 + pc * 128 + k, 0, j]

        vec = np.zeros((128, 3 * NCC), np.float32)
        al = np.zeros((128, NCC * NST), np.float32)
        for pc in range(NCC):
            vec[:, pc] = conv_b[ch0 + pc * 128:ch0 + (pc + 1) * 128]
            vec[:, NCC + pc] = dtb[ch0 + pc * 128:ch0 + (pc + 1) * 128]
            vec[:, 2 * NCC + pc] = dpar[ch0 + pc * 128:ch0 + (pc + 1) * 128]
            al[:, pc * NST:(pc + 1) * NST] = \
                alog[ch0 + pc * 128:ch0 + (pc + 1) * 128, :]

        wo = np.empty((NHC, 128, 2 * NCC * 128), np.float32)
        for hp in range(NHC):
            blk = wout[hp * 128:(hp + 1) * 128, :]
            wo[hp] = (blk.T.reshape(2 * NCC, 128, 128)
                      .transpose(1, 0, 2).reshape(128, 2 * NCC * 128))

        gt = np.empty((NIC, 128, NKC * 128), np.float32)
        ut = np.empty((NIC, 128, NKC * 128), np.float32)
        for ic in range(NIC):
            blkg = gw[ic * 128:(ic + 1) * 128, :]
            blku = uw[ic * 128:(ic + 1) * 128, :]
            gt[ic] = (blkg.T.reshape(NKC, 128, 128)
                      .transpose(1, 0, 2).reshape(128, NKC * 128))
            ut[ic] = (blku.T.reshape(NKC, 128, 128)
                      .transpose(1, 0, 2).reshape(128, NKC * 128))
        dt_ = np.empty((NHC, 128, NIC * 128), np.float32)
        for hc in range(NHC):
            blk = dw[hc * 128:(hc + 1) * 128, :]
            dt_[hc] = (blk.T.reshape(NIC, 128, 128)
                       .transpose(1, 0, 2).reshape(128, NIC * 128))

        fs = np.zeros((128, 2 * NIC + NHC), np.float32)
        for ic in range(NIC):
            fs[:, ic] = gs[ic * 128:(ic + 1) * 128]
            fs[:, NIC + ic] = us[ic * 128:(ic + 1) * 128]
        for hc in range(NHC):
            fs[:, 2 * NIC + hc] = ds[hc * 128:(hc + 1) * 128]

        in_maps.append({
            "x_f": xf, "nonce": np.array([[nonce]], np.int32),
            "winu": wu, "wing": wg, "xpt": xp, "dtwt": dtt, "convd": cvd,
            "vecs": vec, "alog": al, "maskc": mask, "selm": sel,
            "idnb": idnb, "idnf": idnf, "woutt": wo, "gatet": gt,
            "upt": ut, "downt": dt_, "fscale": fs,
        })
    return in_maps


_NC_CACHE = {}


def kernel(L0_attn_hidden_states, L0_attn_in_proj_w, L0_attn_conv_w,
           L0_attn_conv_b, L0_attn_x_proj_w, L0_attn_dt_proj_w,
           L0_attn_dt_proj_b, L0_attn_A_log, L0_attn_D_param,
           L0_attn_out_proj_w, L0_ffn_gate_w, L0_ffn_up_w, L0_ffn_down_w,
           L0_ffn_gate_s, L0_ffn_up_s, L0_ffn_down_s):
    c = make_cfg(mini=False)
    inp = dict(
        L0_attn_hidden_states=L0_attn_hidden_states,
        L0_attn_in_proj_w=L0_attn_in_proj_w, L0_attn_conv_w=L0_attn_conv_w,
        L0_attn_conv_b=L0_attn_conv_b, L0_attn_x_proj_w=L0_attn_x_proj_w,
        L0_attn_dt_proj_w=L0_attn_dt_proj_w,
        L0_attn_dt_proj_b=L0_attn_dt_proj_b, L0_attn_A_log=L0_attn_A_log,
        L0_attn_D_param=L0_attn_D_param,
        L0_attn_out_proj_w=L0_attn_out_proj_w, L0_ffn_gate_w=L0_ffn_gate_w,
        L0_ffn_up_w=L0_ffn_up_w, L0_ffn_down_w=L0_ffn_down_w,
        L0_ffn_gate_s=L0_ffn_gate_s, L0_ffn_up_s=L0_ffn_up_s,
        L0_ffn_down_s=L0_ffn_down_s)
    nonce = int(np.random.randint(1, 1 << 30))
    in_maps = stage_inputs(c, inp, nonce)
    if "full" not in _NC_CACHE:
        _NC_CACHE["full"] = build_program(c)
    nc = _NC_CACHE["full"]
    res = run_bass_kernel_spmd(nc, in_maps, core_ids=list(range(N_CORES)))
    TLH, D = c["TLH"], c["D"]
    out = np.empty((N_CORES * TLH, D), np.float32)
    for core in range(N_CORES):
        out[core * TLH:(core + 1) * TLH] = res.results[core]["out_r"]
    return out.reshape(1, N_CORES * TLH, D)
